# version 07
# baseline (speedup 1.0000x reference)
import numpy as np
import ml_dtypes
from contextlib import ExitStack
import concourse.bass as bass
import concourse.mybir as mybir
from concourse.bass_utils import run_bass_kernel_spmd

F32 = mybir.dt.float32
BF16 = mybir.dt.bfloat16
AF = mybir.ActivationFunctionType
ALU = mybir.AluOpType
AX = mybir.AxisListType
bfdt = ml_dtypes.bfloat16

T = 4096
D = 1024
NH = 16
DHEAD = 64
FF = 4096
NE = 8
NKC = D // 128
NT = T // 128


ENGINES = ("pe", "act", "dve", "pool", "sp")


class Op:
    __slots__ = ("eng", "fn", "waits", "inc", "done", "is_dma", "tag",
                 "need_inc", "emitted", "dma_key", "dma_count")

    def __init__(self, eng, fn, waits, is_dma=False, tag=""):
        self.eng = eng
        self.fn = fn
        self.waits = waits
        self.inc = None
        self.done = None
        self.is_dma = is_dma
        self.tag = tag
        self.need_inc = False
        self.emitted = False


class Prog:
    def __init__(self, nc):
        self.nc = nc
        self.q = {e: [] for e in ENGINES}
        self.sems = {}
        self.counts = {e: 0 for e in ENGINES}
        self.counts["dma_sp"] = 0
        self.counts["dma_gp"] = 0
        self._stack = ExitStack()
        for key in list(self.counts):
            self.sems[key] = self._stack.enter_context(nc.semaphore(f"s_{key}"))
        self.watermark = {e: {} for e in ENGINES}
        self.fence = {}
        self.consts = {}
        self.consts_res = {}
        self.host_in = {}

    def op(self, eng, fn, reads=(), writes=(), tag=""):
        waits = []
        for r in reads:
            if r is not None:
                waits += r._read_edges(eng)
        for w in writes:
            if w is not None:
                waits += w._write_edges(eng)
        o = Op(eng, fn, waits, tag=tag)
        self.q[eng].append(o)
        for r in reads:
            if r is not None:
                r._note_read(o)
        for w in writes:
            if w is not None:
                w._note_write(o)
        return o

    def dma(self, queue, out_ap, in_ap, reads=(), writes=(), tag="",
            accum=False, slow=False):
        eng = queue
        key = "dma_sp" if queue == "sp" else "dma_gp"
        waits = []
        for r in reads:
            if r is not None:
                waits += r._read_edges(eng)
        for w in writes:
            if w is not None:
                waits += w._write_edges(eng)

        def fn(e, _o=out_ap, _i=in_ap, _a=accum, _s=slow):
            kw = {}
            if _a:
                kw["accum_op"] = ALU.add
            if _s:
                kw["allow_slow_non_contiguous"] = True
            return e.dma_start(out=_o, in_=_i, **kw)

        o = Op(eng, fn, waits, is_dma=True, tag=tag)
        self.counts[key] += 16
        o.inc = (key, 16)
        o.done = None
        o.dma_key = key
        o.dma_count = self.counts[key]
        self.q[eng].append(o)
        for r in reads:
            if r is not None:
                r._note_read(o)
        for w in writes:
            if w is not None:
                w._note_write(o)
        return o

    def milestone(self, o):
        if o.is_dma:
            key = o.dma_key
            last = self.fence.get(key)
            if last is not None and last[0] >= o.dma_count:
                return last[1]
            total = self.counts[key]
            def _fence_fn(e, _k=key, _v=total):
                e.wait_ge(self.sems[_k], _v)
                return e.nop()
            f = Op(o.eng, _fence_fn, [], tag=f"fence_{key}")
            f.need_inc = True
            self.q[o.eng].append(f)
            self.fence[key] = (total, f)
            return f
        o.need_inc = True
        return o

    def wait_all_dma(self):
        for queue, key in (("sp", "dma_sp"), ("pool", "dma_gp")):
            v = self.counts[key]
            if v:
                self.op(queue, lambda e, _k=key, _v=v: e.wait_ge(self.sems[_k], _v),
                        tag="drain")

    def flush(self):
        nc = self.nc
        if not any(self.q.values()):
            return
        for eng_name in ENGINES:
            for o in self.q[eng_name]:
                if o.need_inc and o.done is None:
                    self.counts[eng_name] += 1
                    o.inc = (eng_name, 1)
                    o.done = (eng_name, self.counts[eng_name])
        with nc.Block() as blk:
            for eng_name in ENGINES:
                ops = self.q[eng_name]
                if not ops:
                    continue
                section = {"pe": blk.tensor, "act": blk.scalar, "dve": blk.vector,
                           "pool": blk.gpsimd, "sp": blk.sync}[eng_name]
                wm = self.watermark[eng_name]

                def body(e, _ops=ops, _wm=wm):
                    for o in _ops:
                        for w in o.waits:
                            if isinstance(w, Op):
                                if w.done is None:
                                    assert w.emitted, \
                                        f"unresolved same-flush edge to {w.tag}"
                                    continue
                                k, v = w.done
                            else:
                                k, v = w
                            if _wm.get(k, 0) < v:
                                e.wait_ge(self.sems[k], v)
                                _wm[k] = v
                        ins = o.fn(e)
                        if o.inc is not None:
                            assert ins is not None, f"op {o.tag} milestone but no ins"
                            ins.then_inc(self.sems[o.inc[0]], o.inc[1])
                        o.emitted = True
                section(body)
        self.q = {e: [] for e in ENGINES}
        self.fence = {}


class Res:
    __slots__ = ("prog", "name", "writers", "readers")

    def __init__(self, prog, name=""):
        self.prog = prog
        self.name = name
        self.writers = {}
        self.readers = {}

    def _read_edges(self, eng):
        ws = []
        for w in self.writers.values():
            if w.eng == "pe" and eng == "pe":
                continue
            ws.append(self.prog.milestone(w))
        return ws

    def _write_edges(self, eng):
        ws = []
        for w in self.writers.values():
            if w.eng != eng and not (w.eng == "pe" and eng == "pe"):
                ws.append(self.prog.milestone(w))
        for r in self.readers.values():
            if r.eng != eng and not (r.eng == "pe" and eng == "pe"):
                ws.append(self.prog.milestone(r))
        return ws

    def _note_read(self, op):
        self.readers[op.eng] = op

    def _note_write(self, op):
        self.writers[op.eng] = op
        self.readers = {}

    def set_writer(self, op):
        self.writers = {op.eng: op}
        self.readers = {}


def cdiv(a, b):
    return (a + b - 1) // b


_ALLOC_N = [0]


def alloc(ctx, nc, name, shape, dt=F32, psum=False):
    _ALLOC_N[0] += 1
    name = f"{name}_u{_ALLOC_N[0]}"
    if psum:
        return ctx.enter_context(nc.psum_tensor(name, shape, dt))
    return ctx.enter_context(nc.sbuf_tensor(name, shape, dt))


def rtile(p, ctx, nc, name, shape, dt=F32, psum=False, n=1):
    if n == 1:
        return alloc(ctx, nc, name, shape, dt, psum), Res(p, name)
    bufs = [alloc(ctx, nc, f"{name}{i}", shape, dt, psum) for i in range(n)]
    ress = [Res(p, f"{name}{i}") for i in range(n)]
    return bufs, ress


def mm_group(p, psum_ap, pres, mms, tag=""):
    n = len(mms)
    last = None
    for i, (lt, rh, reads) in enumerate(mms):
        wr = [pres] if i == 0 else []
        last = p.op("pe",
                    lambda e, _l=lt, _r=rh, _s=(i == 0), _e=(i == n - 1), _o=psum_ap:
                    e.matmul(_o, _l, _r, start=_s, stop=_e),
                    reads=reads, writes=wr, tag=tag)
    pres.set_writer(last)
    return last



def emit_ln_xnT(p, ctx, nc, x_hbm_ap, xnT, xnT_res, uid, tc0=0, ntc=NT, G=4):
    xt, xtr = rtile(p, ctx, nc, f"{uid}lx", [128, D], F32, n=G)
    sq = p.dump
    st, str_ = rtile(p, ctx, nc, f"{uid}lst", [128, 8], F32, n=G)
    xn, xnr = rtile(p, ctx, nc, f"{uid}lxn", [128, D], BF16, n=G)
    ptr, ptres = rtile(p, ctx, nc, f"{uid}lpt", [128, 128], BF16, psum=True, n=2)
    ident = p.consts["identbf"]
    identr = p.consts_res["identbf"]

    for g0 in range(0, ntc, G):
        js = [j for j in range(G) if g0 + j < ntc]
        for j in js:
            tc = tc0 + g0 + j
            p.dma("sp", xt[j][:, :], x_hbm_ap[tc * 128:(tc + 1) * 128, :],
                  writes=[xtr[j]], tag="lnload")
        for j in js:
            p.op("dve", lambda e, _j=j: e.reduce_sum(st[_j][:, 0:1], xt[_j][:, :],
                                                     axis=AX.X),
                 reads=[xtr[j]], writes=[str_[j]], tag="lnsum")
        for j in js:
            p.op("act", lambda e, _j=j: e.activation(sq[:, :], xt[_j][:, :], AF.Square,
                                                     accum_out=st[_j][:, 1:2]),
                 reads=[xtr[j]], writes=[str_[j]], tag="lnsq")
        for j in js:
            stt = st[j]
            p.op("dve", lambda e, _s=stt: e.tensor_scalar(_s[:, 2:3], _s[:, 0:1],
                                                          1.0 / D, None, ALU.mult),
                 reads=[str_[j]], writes=[str_[j]], tag="lnmean")
            p.op("dve", lambda e, _s=stt: e.tensor_tensor(_s[:, 3:4], _s[:, 2:3],
                                                          _s[:, 2:3], ALU.mult),
                 reads=[str_[j]], writes=[str_[j]], tag="lnmsq")
            p.op("dve", lambda e, _s=stt: e.tensor_scalar(_s[:, 4:5], _s[:, 1:2],
                                                          1.0 / D, _s[:, 3:4],
                                                          ALU.mult, ALU.subtract),
                 reads=[str_[j]], writes=[str_[j]], tag="lnvar")
        for j in js:
            stt = st[j]
            p.op("act", lambda e, _s=stt: e.activation(_s[:, 5:6], _s[:, 4:5], AF.Ln,
                                                       bias=p.consts["eps"][:, 0:1]),
                 reads=[str_[j], p.consts_res["eps"]], writes=[str_[j]], tag="lnln")
            p.op("act", lambda e, _s=stt: e.activation(_s[:, 6:7], _s[:, 5:6], AF.Exp,
                                                       scale=p.consts["nhalf"][:, 0:1]),
                 reads=[str_[j], p.consts_res["nhalf"]], writes=[str_[j]], tag="lnexp")
        for j in js:
            stt = st[j]
            p.op("dve", lambda e, _s=stt: e.scalar_tensor_tensor(_s[:, 7:8], _s[:, 2:3],
                                                                 -1.0, _s[:, 6:7],
                                                                 ALU.mult, ALU.mult),
                 reads=[str_[j]], writes=[str_[j]], tag="lnnmb")
        for j in js:
            stt = st[j]
            p.op("act", lambda e, _j=j, _s=stt: e.activation(xn[_j][:, :], xt[_j][:, :],
                                                             AF.Identity,
                                                             bias=_s[:, 7:8],
                                                             scale=_s[:, 6:7]),
                 reads=[xtr[j], str_[j]], writes=[xnr[j]], tag="lnapply")
        for j in js:
            tl = g0 + j
            for k in range(NKC):
                pb = (j * NKC + k) % 2
                p.op("pe", lambda e, _j=j, _k=k, _p=pb:
                     e.transpose(ptr[_p][:, :], xn[_j][:, _k * 128:(_k + 1) * 128],
                                 ident[:, :]),
                     reads=[xnr[j], identr], writes=[ptres[pb]], tag="lntr")
                p.op("dve", lambda e, _k=k, _tl=tl, _p=pb:
                     e.tensor_copy(xnT[:, _k, _tl * 128:(_tl + 1) * 128], ptr[_p][:, :]),
                     reads=[ptres[pb]], writes=[xnT_res], tag="lnev")



TH = T // 2
NFB_G = FF // 128
NTH = TH // 128


def emit_ffn_l1(p, ctx, nc, aps, x_in, x_out, uid="f1", n_experts=NE, n_tg=None):
    w1, w1_r = rtile(p, ctx, nc, f"{uid}_w1", [128, NKC, FF], BF16)
    w2, w2_r = rtile(p, ctx, nc, f"{uid}_w2", [128, NFB_G, D], BF16)
    b1c, b1c_r = rtile(p, ctx, nc, f"{uid}_b1c", [128, NFB_G], F32)
    b1row, b1row_r = rtile(p, ctx, nc, f"{uid}_b1r", [32, 128], F32)

    def load_expert(e_idx):
        p.dma("sp", w1[:, :, :],
              aps["w1m"][e_idx].rearrange("(k p) f -> p k f", p=128),
              writes=[w1_r], tag=f"w1e{e_idx}")
        p.dma("sp", w2[:, :, :],
              aps["w2m"][e_idx].rearrange("(k p) f -> p k f", p=128),
              writes=[w2_r], tag=f"w2e{e_idx}")
        p.dma("pool", b1row[:, :],
              aps["b1"][e_idx].rearrange("(a b) -> a b", b=128),
              writes=[b1row_r], tag=f"b1e{e_idx}")
        p.op("pe", lambda e: e.transpose(p.pb1f1[:, 0:32], b1row[:, :],
                                         p.consts["identf"][0:32, 0:32]),
             reads=[b1row_r, p.consts_res["identf"]], writes=[p.pb1f1_r], tag="b1tr")
        p.op("dve", lambda e: e.tensor_copy(b1c[:, :], p.pb1f1[:, 0:32]),
             reads=[p.pb1f1_r], writes=[b1c_r], tag="b1ev")

    xnT = alloc(ctx, nc, f"{uid}_xnT", [128, NKC, TH], BF16)
    xnT_res = Res(p, "xnT")
    gates = alloc(ctx, nc, f"{uid}_gates", [128, NT, NE], F32)
    gates_r = Res(p, "gates")
    ident = p.consts["identbf"]
    identr = p.consts_res["identbf"]

    for half in range(2):
        hb0 = half * TH
        htc0 = half * NTH
        with ExitStack() as c1:
            p.pb1f1, p.pb1f1_r = rtile(p, c1, nc, f"{uid}_pb1", [128, 32], F32,
                                       psum=True)
            emit_ln_xnT(p, c1, nc, x_in, xnT, xnT_res, uid=uid, tc0=htc0, ntc=NTH,
                        G=3)
            load_expert(0)
            wgr, wgr_r = rtile(p, c1, nc, f"{uid}_wgr", [16, D], BF16)
            p.dma("pool", wgr[:, :], aps["wg_red"], writes=[wgr_r])
            wg, wg_r = rtile(p, c1, nc, f"{uid}_wg", [8, 16], F32)
            p.dma("pool", wg[:, :], aps["wg"], writes=[wg_r])
            gsc, gsc_r = rtile(p, c1, nc, f"{uid}_gsc", [8, 20], F32)
            wgu, wgu_r = rtile(p, c1, nc, f"{uid}_wgu", [8, 16], BF16)
            p.op("dve", lambda e: e.tensor_tensor(gsc[:, 0:16], wg[:, :], wg[:, :], ALU.mult),
                 reads=[wg_r], writes=[gsc_r])
            p.op("dve", lambda e: e.reduce_sum(gsc[:, 16:17], gsc[:, 0:16], axis=AX.X),
                 reads=[gsc_r], writes=[gsc_r])
            p.op("act", lambda e: e.activation(gsc[:, 17:18], gsc[:, 16:17], AF.Ln),
                 reads=[gsc_r], writes=[gsc_r])
            p.op("act", lambda e: e.activation(gsc[:, 17:18], gsc[:, 17:18], AF.Exp,
                                               scale=p.consts["phalf"][0:8, 0:1]),
                 reads=[gsc_r, p.consts_res["phalf"]], writes=[gsc_r])
            p.op("dve", lambda e: e.tensor_scalar_max(gsc[:, 18:19], gsc[:, 17:18], 1e-4),
                 reads=[gsc_r], writes=[gsc_r])
            p.op("dve", lambda e: e.reciprocal(gsc[:, 19:20], gsc[:, 18:19]),
                 reads=[gsc_r], writes=[gsc_r])
            p.op("dve", lambda e: e.tensor_scalar(wgu[:, :], wg[:, :], gsc[:, 19:20], None, ALU.mult),
                 reads=[wg_r, gsc_r], writes=[wgu_r])
            pgt, pgt_r = rtile(p, c1, nc, f"{uid}_pgt", [128, 128], BF16, psum=True)
            wguT, wguT_r = rtile(p, c1, nc, f"{uid}_wguT", [16, 8], BF16)
            p.op("pe", lambda e: e.transpose(pgt[0:16, 0:8], wgu[:, :], ident[0:8, 0:8]),
                 reads=[wgu_r, identr], writes=[pgt_r])
            p.op("dve", lambda e: e.tensor_copy(wguT[:, :], pgt[0:16, 0:8]),
                 reads=[pgt_r], writes=[wguT_r])
            pgc, pgc_r = rtile(p, c1, nc, f"{uid}_pgc", [8, 512], F32, psum=True)
            wgcT, wgcT_r = rtile(p, c1, nc, f"{uid}_wgcT", [8, D], BF16)
            for h2 in range(2):
                mm_group(p, pgc[:, :], pgc_r,
                         [(wguT[:, :], wgr[:, h2 * 512:(h2 + 1) * 512], [wguT_r, wgr_r])])
                p.op("dve", lambda e, _h=h2: e.tensor_copy(wgcT[:, _h * 512:(_h + 1) * 512],
                                                           pgc[:, :]),
                     reads=[pgc_r], writes=[wgcT_r])
            wgc, wgc_r = rtile(p, c1, nc, f"{uid}_wgc", [128, NKC, 8], BF16)
            for k in range(NKC):
                p.op("pe", lambda e, _k=k: e.transpose(pgt[0:128, 0:8],
                                                       wgcT[:, _k * 128:(_k + 1) * 128],
                                                       ident[0:8, 0:8]),
                     reads=[wgcT_r, identr], writes=[pgt_r])
                p.op("dve", lambda e, _k=k: e.tensor_copy(wgc[:, _k, :], pgt[0:128, 0:8]),
                     reads=[pgt_r], writes=[wgc_r])

            gatesT, gatesT_r = rtile(p, c1, nc, f"{uid}_gT", [8, TH], BF16)
            b2sb, b2sb_r = rtile(p, c1, nc, f"{uid}_b2sb", [8, D], BF16)
            p.dma("pool", b2sb[:, :], aps["b2"], writes=[b2sb_r])
            plog, plog_r = rtile(p, c1, nc, f"{uid}_plog", [128, 8], F32, psum=True, n=2)
            gl, gl_r = rtile(p, c1, nc, f"{uid}_gl", [128, 8], F32)
            gsc2, gsc2_r = rtile(p, c1, nc, f"{uid}_gsc2", [128, 4], F32, n=2)
            gexp = alloc(c1, nc, f"{uid}_gexp", [128, 8], F32)
            gbf, gbf_r = rtile(p, c1, nc, f"{uid}_gbf", [128, 8], BF16, n=2)
            pacc0, pacc0_r = rtile(p, c1, nc, f"{uid}_pacc", [128, 512], F32, psum=True)
            pacc, pacc_r = [pacc0, pacc0], [pacc0_r, pacc0_r]
            xacc0, xacc0_r = rtile(p, c1, nc, f"{uid}_xacc", [128, D], F32)
            xacc, xacc_r = [xacc0, xacc0], [xacc0_r, xacc0_r]
            xtl0, xtl0_r = rtile(p, c1, nc, f"{uid}_xtl", [128, D], F32)
            xtl, xtl_r = [xtl0, xtl0], [xtl0_r, xtl0_r]

            for tl in range(NTH):
                tc = htc0 + tl
                pb = tl % 2
                gts = slice(tc * 128, (tc + 1) * 128)
                lts = slice(tl * 128, (tl + 1) * 128)
                g2c, g2r = gsc2[pb], gsc2_r[pb]
                mm_group(p, plog[pb][:, :], plog_r[pb],
                         [(xnT[:, k, lts], wgc[:, k, :], [xnT_res, wgc_r])
                          for k in range(NKC)], tag="glog")
                p.op("dve", lambda e, _p=pb, _g=g2c: e.reduce_max(_g[:, 0:1], plog[_p][:, :],
                                                                  axis=AX.X, negate=True),
                     reads=[plog_r[pb]], writes=[g2r], tag="gmax")
                p.op("act", lambda e, _p=pb, _g=g2c: e.activation(gexp[:, :], plog[_p][:, :],
                                                                  AF.Exp, bias=_g[:, 0:1],
                                                                  accum_out=_g[:, 1:2]),
                     reads=[plog_r[pb], g2r], writes=[g2r], tag="gexp")
                p.op("dve", lambda e, _g=g2c: e.reciprocal(_g[:, 2:3], _g[:, 1:2]),
                     reads=[g2r], writes=[g2r], tag="gscore")
                p.op("dve", lambda e, _g=g2c: e.tensor_scalar(_g[:, 3:4], _g[:, 0:1], -1.0,
                                                              None, ALU.mult),
                     reads=[g2r], writes=[g2r], tag="gneg")
                p.op("dve", lambda e, _p=pb, _g=g2c: e.tensor_scalar(gl[:, :], plog[_p][:, :],
                                                                     _g[:, 3:4], None, ALU.is_ge),
                     reads=[plog_r[pb], g2r], writes=[gl_r], tag="gmask")
                p.op("dve", lambda e, _tc=tc, _g=g2c: e.tensor_scalar(gates[:, _tc, :], gl[:, :],
                                                                      _g[:, 2:3], None, ALU.mult),
                     reads=[gl_r, g2r], writes=[gates_r], tag="gfin")
                p.op("dve", lambda e, _tc=tc, _p=pb: e.tensor_copy(gbf[_p][:, :], gates[:, _tc, :]),
                     reads=[gates_r], writes=[gbf_r[pb]], tag="gcast")
                p.op("pe", lambda e, _p=pb: e.transpose(pgt[0:8, 0:128], gbf[_p][:, :], ident[:, :]),
                     reads=[gbf_r[pb], identr], writes=[pgt_r], tag="gtr")
                p.op("dve", lambda e, _tl=tl: e.tensor_copy(gatesT[:, _tl * 128:(_tl + 1) * 128],
                                                            pgt[0:8, 0:128]),
                     reads=[pgt_r], writes=[gatesT_r], tag="gtev")
                p.dma("sp", xtl[pb][:, :], x_in[gts, :], writes=[xtl_r[pb]], tag="gxload")
                for dh in range(2):
                    mm_group(p, pacc[pb][:, :], pacc_r[pb],
                             [(gatesT[:, lts], b2sb[:, dh * 512:(dh + 1) * 512],
                               [gatesT_r, b2sb_r])], tag="gaccmm")
                    p.op("dve", lambda e, _p=pb, _d=dh: e.tensor_add(
                             xacc[_p][:, _d * 512:(_d + 1) * 512],
                             xtl[_p][:, _d * 512:(_d + 1) * 512], pacc[_p][:, :]),
                         reads=[pacc_r[pb], xtl_r[pb]], writes=[xacc_r[pb]], tag="gaccadd")
                p.dma("pool", x_out[gts, :], xacc[pb][:, :], reads=[xacc_r[pb]], tag="gaccst")
            p.wait_all_dma()
            p.flush()

        TG = 256
        NTG = TH // TG
        NFB = FF // 128
        with ExitStack() as c2:
            pb1, pb1_r = rtile(p, c2, nc, f"{uid}_pb1b", [128, 32], F32, psum=True)
            p.pb1f1, p.pb1f1_r = pb1, pb1_r
            ph, ph_r = rtile(p, c2, nc, f"{uid}_ph", [128, TG], F32, psum=True, n=2)
            rt, rt_r = rtile(p, c2, nc, f"{uid}_rt", [128, TG], BF16, n=2)
            pY, pY_r = rtile(p, c2, nc, f"{uid}_pY", [128, 512], F32, psum=True, n=4)
            ytl, ytl_r = rtile(p, c2, nc, f"{uid}_ytl", [128, 512], F32, n=4)

            def emit_s2_l1(fb):
                hbi = fb % 2
                for tc2 in range(2):
                    for dh in range(2):
                        iY = tc2 * 2 + dh
                        last = p.op("pe",
                                    lambda e, _h=hbi, _f=fb, _t=tc2, _d=dh, _i=iY:
                                    e.matmul(pY[_i][:, :],
                                             rt[_h][:, _t * 128:(_t + 1) * 128],
                                             w2[:, _f, _d * 512:(_d + 1) * 512],
                                             start=(_f == 0), stop=(_f == NFB - 1)),
                                    reads=[rt_r[hbi], w2_r],
                                    writes=[pY_r[iY]] if fb == 0 else [], tag="ffs2")
                        if fb == NFB - 1:
                            pY_r[iY].set_writer(last)

            for e_idx in range(n_experts):
                if e_idx > 0:
                    load_expert(e_idx)
                for tg in range(n_tg if n_tg is not None else NTG):
                    t0 = tg * TG
                    pend = None
                    for fb in range(NFB):
                        hbi = fb % 2
                        mm_group(p, ph[hbi][:, :], ph_r[hbi],
                                 [(w1[:, k, fb * 128:(fb + 1) * 128],
                                   xnT[:, k, t0:t0 + TG], [w1_r, xnT_res])
                                  for k in range(NKC)], tag="ffs1")
                        p.op("act", lambda e, _h=hbi, _f=fb: e.activation(
                                 rt[_h][:, :], ph[_h][:, :], AF.Gelu,
                                 bias=b1c[:, _f:_f + 1]),
                             reads=[ph_r[hbi], b1c_r], writes=[rt_r[hbi]], tag="ffgelu")
                        if pend is not None:
                            emit_s2_l1(pend)
                        pend = fb
                    emit_s2_l1(pend)
                    for tc2 in range(2):
                        tci = htc0 + tg * 2 + tc2
                        for dh in range(2):
                            iY = tc2 * 2 + dh
                            p.op("dve", lambda e, _i=iY, _tc=tci, _e=e_idx: e.tensor_scalar(
                                     ytl[_i][:, :], pY[_i][:, :], gates[:, _tc, _e:_e + 1],
                                     None, ALU.mult),
                                 reads=[pY_r[iY], gates_r], writes=[ytl_r[iY]], tag="ffev")
                            p.dma("pool",
                                  x_out[tci * 128:(tci + 1) * 128, dh * 512:(dh + 1) * 512],
                                  ytl[iY][:, :], reads=[ytl_r[iY]], tag="ffacc", accum=True)
            p.wait_all_dma()
            p.flush()



def emit_ffn_l0(p, ctx, nc, aps, x_in, x_out, uid="f0", n_experts=NE, n_tg=None):
    xnT = alloc(ctx, nc, f"{uid}_xnT", [128, NKC, TH], BF16)
    xnT_res = Res(p, "xnT")
    gates = alloc(ctx, nc, f"{uid}_gates", [128, NT, NE], F32)
    gates_r = Res(p, "gates")
    ident = p.consts["identbf"]
    identr = p.consts_res["identbf"]
    identf = p.consts["identf"]
    identfr = p.consts_res["identf"]
    epsap = p.consts["eps"]
    epsr = p.consts_res["eps"]

    w1, w1_r = rtile(p, ctx, nc, f"{uid}_w1", [128, NKC, FF], BF16)
    w2, w2_r = rtile(p, ctx, nc, f"{uid}_w2", [128, NFB_G, D], BF16)
    vrow, vrow_r = rtile(p, ctx, nc, f"{uid}_vr", [32, 3 * 128], F32)
    g2row, g2row_r = rtile(p, ctx, nc, f"{uid}_g2row", [1, D], BF16)
    crow_bf, crow_bf_r = rtile(p, ctx, nc, f"{uid}_crowb", [1, D], BF16)

    def load_weights(e_idx):
        p.dma("sp", w1[:, :, :],
              aps["w1m"][e_idx].rearrange("(k p) f -> p k f", p=128),
              writes=[w1_r], tag=f"w1e{e_idx}")
        p.dma("sp", w2[:, :, :],
              aps["w2m"][e_idx].rearrange("(k p) f -> p k f", p=128),
              writes=[w2_r], tag=f"w2e{e_idx}")
        for j, nm in enumerate(("b1", "g1", "be1")):
            p.dma("pool", vrow[:, j * 128:(j + 1) * 128],
                  aps[nm][e_idx].rearrange("(a b) -> a b", b=128),
                  writes=[vrow_r], tag=f"{nm}e{e_idx}")
        p.dma("pool", g2row[:, :], aps["g2"][e_idx].rearrange("(o d) -> o d", o=1),
              writes=[g2row_r], tag=f"g2e{e_idx}")
        p.dma("pool", crow_bf[:, :], aps["b2"][e_idx].rearrange("(o d) -> o d", o=1),
              writes=[crow_bf_r], tag=f"b2e{e_idx}")

    for half in range(2):
        hb0 = half * TH
        htc0 = half * NTH
        with ExitStack() as c1:
            emit_ln_xnT(p, c1, nc, x_in, xnT, xnT_res, uid=uid, tc0=htc0, ntc=NTH,
                        G=3)
            load_weights(0)
            wr, wr_r = rtile(p, c1, nc, f"{uid}_wr", [128, NKC, 8], BF16)
            p.dma("pool", wr[:, :, :], aps["Wr"].rearrange("(k p) e -> p k e", p=128),
                  writes=[wr_r])
            brrow, brrow_r = rtile(p, c1, nc, f"{uid}_br", [1, 8], BF16)
            p.dma("pool", brrow[:, :], aps["br"].rearrange("(o e) -> o e", o=1),
                  writes=[brrow_r])
            onesrow, onesrow_r = rtile(p, c1, nc, f"{uid}_o1r", [1, 128], BF16)
            p.op("dve", lambda e: e.memset(onesrow[:, :], 1.0), writes=[onesrow_r])
            be2sb, be2sb_r = rtile(p, c1, nc, f"{uid}_be2", [8, D], BF16)
            p.dma("pool", be2sb[:, :], aps["be2"], writes=[be2sb_r])
            gatesT, gatesT_r = rtile(p, c1, nc, f"{uid}_gT", [8, TH], BF16)
            pgt, pgt_r = rtile(p, c1, nc, f"{uid}_pgt", [128, 128], BF16, psum=True)
            plog, plog_r = rtile(p, c1, nc, f"{uid}_plog", [128, 8], F32, psum=True, n=2)
            gp_, gp_r = rtile(p, c1, nc, f"{uid}_gp", [128, 8], F32, n=2)
            gq, gq_r = rtile(p, c1, nc, f"{uid}_gq", [128, 8], F32, n=2)
            gsc2, gsc2_r = rtile(p, c1, nc, f"{uid}_gsc2", [128, 8], F32, n=2)
            gexp = alloc(c1, nc, f"{uid}_gexp", [128, 8], F32)
            gbf, gbf_r = rtile(p, c1, nc, f"{uid}_gbf", [128, 8], BF16, n=2)
            pacc, pacc_r = rtile(p, c1, nc, f"{uid}_pacc", [128, 512], F32, psum=True, n=2)
            xacc0, xacc0_r = rtile(p, c1, nc, f"{uid}_xacc", [128, D], F32)
            xacc, xacc_r = [xacc0, xacc0], [xacc0_r, xacc0_r]
            xtl0, xtl0_r = rtile(p, c1, nc, f"{uid}_xtl", [128, D], F32)
            xtl, xtl_r = [xtl0, xtl0], [xtl0_r, xtl0_r]

            for tl in range(NTH):
                tc = htc0 + tl
                pb = tl % 2
                gts = slice(tc * 128, (tc + 1) * 128)
                lts = slice(tl * 128, (tl + 1) * 128)
                g2c, g2r = gsc2[pb], gsc2_r[pb]
                mms = [(xnT[:, k, lts], wr[:, k, :], [xnT_res, wr_r]) for k in range(NKC)]
                mms.append((onesrow[:, :], brrow[:, :], [onesrow_r, brrow_r]))
                mm_group(p, plog[pb][:, :], plog_r[pb], mms, tag="glog")
                p.op("dve", lambda e, _p=pb, _g=g2c: e.reduce_max(_g[:, 0:1], plog[_p][:, :],
                                                                  axis=AX.X, negate=True),
                     reads=[plog_r[pb]], writes=[g2r], tag="gmax")
                p.op("act", lambda e, _p=pb, _g=g2c: e.activation(gexp[:, :], plog[_p][:, :],
                                                                  AF.Exp, bias=_g[:, 0:1],
                                                                  accum_out=_g[:, 1:2]),
                     reads=[plog_r[pb], g2r], writes=[g2r], tag="gexp")
                p.op("dve", lambda e, _g=g2c: e.reciprocal(_g[:, 2:3], _g[:, 1:2]),
                     reads=[g2r], writes=[g2r], tag="grec")
                p.op("dve", lambda e, _p=pb, _g=g2c: e.tensor_scalar(gp_[_p][:, :], gexp[:, :],
                                                                     _g[:, 2:3], None, ALU.mult),
                     reads=[g2r], writes=[gp_r[pb]], tag="gprob")
                p.op("dve", lambda e, _p=pb, _g=g2c: e.reduce_max(_g[:, 3:4], gp_[_p][:, :],
                                                                  axis=AX.X),
                     reads=[gp_r[pb]], writes=[g2r], tag="gm1")
                p.op("dve", lambda e, _p=pb, _g=g2c: e.tensor_scalar(gq[_p][:, :], gp_[_p][:, :],
                                                                     _g[:, 3:4], None, ALU.is_ge),
                     reads=[gp_r[pb], g2r], writes=[gq_r[pb]], tag="geq")
                p.op("dve", lambda e, _p=pb: e.scalar_tensor_tensor(gq[_p][:, :], gq[_p][:, :],
                                                                    -1e9, gp_[_p][:, :],
                                                                    ALU.mult, ALU.add),
                     reads=[gq_r[pb], gp_r[pb]], writes=[gq_r[pb]], tag="gp2")
                p.op("dve", lambda e, _p=pb, _g=g2c: e.reduce_max(_g[:, 4:5], gq[_p][:, :],
                                                                  axis=AX.X),
                     reads=[gq_r[pb]], writes=[g2r], tag="gm2")
                p.op("dve", lambda e, _p=pb, _g=g2c: e.tensor_scalar(gq[_p][:, :], gp_[_p][:, :],
                                                                     _g[:, 4:5], None, ALU.is_ge),
                     reads=[gp_r[pb], g2r], writes=[gq_r[pb]], tag="gkeep")
                p.op("dve", lambda e, _g=g2c: e.tensor_tensor(_g[:, 5:6], _g[:, 3:4],
                                                              _g[:, 4:5], ALU.add),
                     reads=[g2r], writes=[g2r], tag="gs12")
                p.op("dve", lambda e, _g=g2c: e.reciprocal(_g[:, 6:7], _g[:, 5:6]),
                     reads=[g2r], writes=[g2r], tag="gr12")
                p.op("dve", lambda e, _p=pb: e.tensor_tensor(gq[_p][:, :], gq[_p][:, :],
                                                             gp_[_p][:, :], ALU.mult),
                     reads=[gq_r[pb], gp_r[pb]], writes=[gq_r[pb]], tag="gmulp")
                p.op("dve", lambda e, _p=pb, _tc=tc, _g=g2c: e.tensor_scalar(
                         gates[:, _tc, :], gq[_p][:, :], _g[:, 6:7], None, ALU.mult),
                     reads=[gq_r[pb], g2r], writes=[gates_r], tag="gfin")
                p.op("dve", lambda e, _tc=tc, _p=pb: e.tensor_copy(gbf[_p][:, :], gates[:, _tc, :]),
                     reads=[gates_r], writes=[gbf_r[pb]], tag="gcast")
                p.op("pe", lambda e, _p=pb: e.transpose(pgt[0:8, 0:128], gbf[_p][:, :], ident[:, :]),
                     reads=[gbf_r[pb], identr], writes=[pgt_r], tag="gtr")
                p.op("dve", lambda e, _tl=tl: e.tensor_copy(gatesT[:, _tl * 128:(_tl + 1) * 128],
                                                            pgt[0:8, 0:128]),
                     reads=[pgt_r], writes=[gatesT_r], tag="gtev")
                p.dma("sp", xtl[pb][:, :], x_in[gts, :], writes=[xtl_r[pb]], tag="gxload")
                for dh in range(2):
                    mm_group(p, pacc[pb][:, :], pacc_r[pb],
                             [(gatesT[:, lts], be2sb[:, dh * 512:(dh + 1) * 512],
                               [gatesT_r, be2sb_r])], tag="gaccmm")
                    p.op("dve", lambda e, _p=pb, _d=dh: e.tensor_add(
                             xacc[_p][:, _d * 512:(_d + 1) * 512],
                             xtl[_p][:, _d * 512:(_d + 1) * 512], pacc[_p][:, :]),
                         reads=[pacc_r[pb], xtl_r[pb]], writes=[xacc_r[pb]], tag="gaccadd")
                p.dma("pool", x_out[gts, :], xacc[pb][:, :], reads=[xacc_r[pb]], tag="gaccst")
            p.wait_all_dma()
            p.flush()

        TG = 256
        NTG = TH // TG
        NFB = FF // 128
        with ExitStack() as c2:
            pvc, pvc_r = rtile(p, c2, nc, f"{uid}_pvc", [128, 32], F32, psum=True)
            pstt, pstt_r = pvc, pvc_r
            b1c, b1c_r = rtile(p, c2, nc, f"{uid}_b1c", [128, NFB], F32)
            g1c, g1c_r = rtile(p, c2, nc, f"{uid}_g1c", [128, NFB], F32)
            be1c, be1c_r = rtile(p, c2, nc, f"{uid}_be1c", [128, NFB], BF16)
            negc1, negc1_r = rtile(p, c2, nc, f"{uid}_negc1", [1, D], BF16)
            c2b2b, c2b2b_r = rtile(p, c2, nc, f"{uid}_c2b2b", [128, D], BF16)
            g2b, g2b_r = rtile(p, c2, nc, f"{uid}_g2b", [128, D], BF16)
            ones128, ones128_r = rtile(p, c2, nc, f"{uid}_o128", [128, 1], BF16)
            p.op("dve", lambda e: e.memset(ones128[:, :], 1.0), writes=[ones128_r])
            onescol, onescol_r = rtile(p, c2, nc, f"{uid}_oc1", [1, 128], BF16)
            p.op("dve", lambda e: e.memset(onescol[:, :], 1.0), writes=[onescol_r])
            prow, prow_r = rtile(p, c2, nc, f"{uid}_prow", [1, 512], F32, psum=True)
            ph, ph_r = rtile(p, c2, nc, f"{uid}_ph", [128, TG], F32, psum=True, n=2)
            rt, rt_r = rtile(p, c2, nc, f"{uid}_rt", [128, 2 * TG], BF16, n=2)
            pstat, pstat_r = prow, prow_r
            srow_f, srow_f_r = rtile(p, c2, nc, f"{uid}_srf", [1, 512], F32, n=2)
            stf, stf_r = rtile(p, c2, nc, f"{uid}_stf", [128, 16], F32, n=2)
            ybraw, ybraw_r = rtile(p, c2, nc, f"{uid}_ybr", [128, D], F32, n=4)
            nc1b, nc1b_r = rtile(p, c2, nc, f"{uid}_nc1b", [128, D], BF16)
            pY, pY_r = rtile(p, c2, nc, f"{uid}_pY", [128, 512], F32, psum=True, n=4)
            ybuf, ybuf_r = rtile(p, c2, nc, f"{uid}_yb", [128, D], F32)
            yst, yst_r = rtile(p, c2, nc, f"{uid}_yst", [128, 10], F32, n=2)
            ysq = p.dump

            def prep_expert(e_idx):
                for j, (dst, dstr) in enumerate(((b1c, b1c_r), (g1c, g1c_r), (be1c, be1c_r))):
                    p.op("pe", lambda e, _j=j: e.transpose(pvc[:, 0:32],
                                                           vrow[:, _j * 128:(_j + 1) * 128],
                                                           identf[0:32, 0:32]),
                         reads=[vrow_r, identfr], writes=[pvc_r], tag="vtr")
                    p.op("dve", lambda e, _d=dst: e.tensor_copy(_d[:, :], pvc[:, 0:32]),
                         reads=[pvc_r], writes=[dstr], tag="vev")
                for dh in range(2):
                    mm_group(p, prow[:, :], prow_r,
                             [(be1c[:, c:c + 1], w2[:, c, dh * 512:(dh + 1) * 512],
                               [be1c_r, w2_r]) for c in range(NFB)], tag="c2mm")
                    p.op("dve", lambda e, _d=dh: e.tensor_add(
                             crow_bf[:, _d * 512:(_d + 1) * 512],
                             crow_bf[:, _d * 512:(_d + 1) * 512], prow[:, :]),
                         reads=[prow_r, crow_bf_r], writes=[crow_bf_r], tag="c2add")
                for c in range(NFB):
                    p.op("dve", lambda e, _c=c: e.tensor_scalar(w2[:, _c, :], w2[:, _c, :],
                                                                g1c[:, _c:_c + 1], None, ALU.mult),
                         reads=[w2_r, g1c_r], writes=[w2_r], tag="w2scale")
                for dh in range(2):
                    mm_group(p, prow[:, :], prow_r,
                             [(ones128[:, :], w2[:, c, dh * 512:(dh + 1) * 512],
                               [ones128_r, w2_r]) for c in range(NFB)], tag="c1mm")
                    p.op("dve", lambda e, _d=dh: e.tensor_scalar(
                             negc1[:, _d * 512:(_d + 1) * 512], prow[:, :], -1.0 / FF, None,
                             ALU.mult),
                         reads=[prow_r], writes=[negc1_r], tag="c1ev")
                for q in range(4):
                    hb2 = q % 2
                    mm_group(p, ph[hb2][:, :], ph_r[hb2],
                             [(onescol[:, :], crow_bf[:, q * 256:(q + 1) * 256],
                               [onescol_r, crow_bf_r])], tag="bc1")
                    p.op("dve", lambda e, _q=q, _h=hb2: e.tensor_copy(
                             c2b2b[:, _q * 256:(_q + 1) * 256], ph[_h][:, :]),
                         reads=[ph_r[hb2]], writes=[c2b2b_r], tag="bc1ev")
                    mm_group(p, ph[hb2][:, :], ph_r[hb2],
                             [(onescol[:, :], g2row[:, q * 256:(q + 1) * 256],
                               [onescol_r, g2row_r])], tag="bc2")
                    p.op("dve", lambda e, _q=q, _h=hb2: e.tensor_copy(
                             g2b[:, _q * 256:(_q + 1) * 256], ph[_h][:, :]),
                         reads=[ph_r[hb2]], writes=[g2b_r], tag="bc2ev")
                    mm_group(p, ph[hb2][:, :], ph_r[hb2],
                             [(onescol[:, :], negc1[:, q * 256:(q + 1) * 256],
                               [onescol_r, negc1_r])], tag="bc3")
                    p.op("dve", lambda e, _q=q, _h=hb2: e.tensor_copy(
                             nc1b[:, _q * 256:(_q + 1) * 256], ph[_h][:, :]),
                         reads=[ph_r[hb2]], writes=[nc1b_r], tag="bc3ev")

            def emit_s2_l0(fb):
                hbi = fb % 2
                o = p.op("pe", lambda e, _h=hbi, _f=fb: e.matmul(
                             pstat[:, :], ones128[:, :], rt[_h][:, :],
                             start=(_f == 0), stop=(_f == NFB - 1)),
                         reads=[ones128_r, rt_r[hbi]],
                         writes=[pstat_r] if fb == 0 else [], tag="ffstat")
                if fb == NFB - 1:
                    pstat_r.set_writer(o)
                for tc2 in range(2):
                    for dh in range(2):
                        iY = tc2 * 2 + dh
                        o = p.op("pe",
                                 lambda e, _h=hbi, _f=fb, _t=tc2, _d=dh, _i=iY:
                                 e.matmul(pY[_i][:, :],
                                          rt[_h][:, _t * 128:(_t + 1) * 128],
                                          w2[:, _f, _d * 512:(_d + 1) * 512],
                                          start=(_f == 0), stop=(_f == NFB - 1)),
                                 reads=[rt_r[hbi], w2_r],
                                 writes=[pY_r[iY]] if fb == 0 else [], tag="ffs2")
                        if fb == NFB - 1:
                            pY_r[iY].set_writer(o)

            def make_tail(tg, e_idx, par):
                sr, srr = srow_f[par], srow_f_r[par]
                st_, str2 = stf[par], stf_r[par]
                pieces = []

                def pc_sttr():
                    _lasttr = None
                    for tc2 in range(2):
                        for j, off in ((0, tc2 * 128), (1, 256 + tc2 * 128)):
                            col = tc2 * 2 + j
                            _lasttr = p.op("pe", lambda e, _c=col, _o=off: e.transpose(
                                     pstt[:, _c:_c + 1], sr[0:1, _o:_o + 128],
                                     identf[0:1, 0:1]),
                                 reads=[srr, identfr],
                                 writes=[pstt_r] if col == 0 else [], tag="sttr")
                    pstt_r.set_writer(_lasttr)
                    p.op("dve", lambda e: e.tensor_copy(st_[:, 0:4], pstt[:, 0:4]),
                         reads=[pstt_r], writes=[str2], tag="stev")
                pieces.append(pc_sttr)

                def pc_stats1():
                    p.op("dve", lambda e: e.tensor_scalar(st_[:, 4:6], st_[:, 0:4:2],
                                                          1.0 / FF, None, ALU.mult),
                         reads=[str2], writes=[str2], tag="stmu")
                    p.op("dve", lambda e: e.tensor_tensor(st_[:, 6:8], st_[:, 4:6],
                                                          st_[:, 4:6], ALU.mult),
                         reads=[str2], writes=[str2], tag="stmusq")
                    p.op("dve", lambda e: e.tensor_scalar(st_[:, 8:10], st_[:, 1:4:2],
                                                          1.0 / FF, None, ALU.mult),
                         reads=[str2], writes=[str2], tag="stvar1")
                    p.op("dve", lambda e: e.tensor_tensor(st_[:, 8:10], st_[:, 8:10],
                                                          st_[:, 6:8], ALU.subtract),
                         reads=[str2], writes=[str2], tag="stvar")
                pieces.append(pc_stats1)

                def pc_stats2():
                    p.op("act", lambda e: e.activation(st_[:, 10:12], st_[:, 8:10], AF.Ln,
                                                       bias=epsap[:, 0:1]),
                         reads=[str2, epsr], writes=[str2], tag="stln")
                    p.op("act", lambda e: e.activation(st_[:, 12:14], st_[:, 10:12], AF.Exp,
                                                       scale=p.consts["nhalf"][:, 0:1]),
                         reads=[str2, p.consts_res["nhalf"]], writes=[str2], tag="stexp")
                pieces.append(pc_stats2)

                def pc_stats3():
                    p.op("dve", lambda e: e.tensor_tensor(st_[:, 14:16], st_[:, 0:4:2],
                                                          st_[:, 12:14], ALU.mult),
                         reads=[str2], writes=[str2], tag="stcoef")
                pieces.append(pc_stats3)

                for tc2 in range(2):
                    tci = htc0 + tg * 2 + tc2
                    yb_i = par * 2 + tc2
                    ys, ysr = yst[tc2], yst_r[tc2]

                    def pc_y1(tc2=tc2, yb_i=yb_i):
                        p.op("dve", lambda e, _t=tc2, _y=yb_i: e.tensor_scalar(
                                 ybuf[:, :], ybraw[_y][:, :], st_[:, 12 + _t:13 + _t],
                                 None, ALU.mult),
                             reads=[ybraw_r[yb_i], str2], writes=[ybuf_r], tag="ffy1")
                        p.op("dve", lambda e, _t=tc2: e.scalar_tensor_tensor(
                                 ybuf[:, :], nc1b[:, :], st_[:, 14 + _t:15 + _t],
                                 ybuf[:, :], ALU.mult, ALU.add),
                             reads=[nc1b_r, str2, ybuf_r], writes=[ybuf_r], tag="ffy2")
                        p.op("dve", lambda e: e.tensor_add(ybuf[:, :], ybuf[:, :],
                                                           c2b2b[:, :]),
                             reads=[ybuf_r, c2b2b_r], writes=[ybuf_r], tag="ffy3")
                    pieces.append(pc_y1)

                    def pc_y2(ys=ys, ysr=ysr):
                        p.op("act", lambda e: e.activation(ybuf[:, :], ybuf[:, :], AF.Relu),
                             reads=[ybuf_r], writes=[ybuf_r], tag="ffyrelu")
                        p.op("dve", lambda e, _s=ys: e.reduce_sum(_s[:, 0:1], ybuf[:, :],
                                                                  axis=AX.X),
                             reads=[ybuf_r], writes=[ysr], tag="ln2sum")
                        p.op("act", lambda e, _s=ys: e.activation(ysq[:, :], ybuf[:, :],
                                                                  AF.Square,
                                                                  accum_out=_s[:, 1:2]),
                             reads=[ybuf_r], writes=[ysr], tag="ln2sq")
                    pieces.append(pc_y2)

                    def pc_y3(ys=ys, ysr=ysr):
                        p.op("dve", lambda e, _s=ys: e.tensor_scalar(_s[:, 2:3], _s[:, 0:1],
                                                                     1.0 / D, None, ALU.mult),
                             reads=[ysr], writes=[ysr], tag="ln2mu")
                        p.op("dve", lambda e, _s=ys: e.tensor_tensor(_s[:, 3:4], _s[:, 2:3],
                                                                     _s[:, 2:3], ALU.mult),
                             reads=[ysr], writes=[ysr], tag="ln2musq")
                        p.op("dve", lambda e, _s=ys: e.tensor_scalar(_s[:, 4:5], _s[:, 1:2],
                                                                     1.0 / D, _s[:, 3:4],
                                                                     ALU.mult, ALU.subtract),
                             reads=[ysr], writes=[ysr], tag="ln2var")
                    pieces.append(pc_y3)

                    def pc_y4(ys=ys, ysr=ysr, tci=tci, e_idx=e_idx):
                        p.op("act", lambda e, _s=ys: e.activation(_s[:, 5:6], _s[:, 4:5],
                                                                  AF.Ln, bias=epsap[:, 0:1]),
                             reads=[ysr, epsr], writes=[ysr], tag="ln2ln")
                        p.op("act", lambda e, _s=ys: e.activation(_s[:, 6:7], _s[:, 5:6],
                                                                  AF.Exp,
                                                                  scale=p.consts["nhalf"][:, 0:1]),
                             reads=[ysr, p.consts_res["nhalf"]], writes=[ysr], tag="ln2exp")
                        p.op("dve", lambda e, _s=ys, _tc=tci, _e=e_idx: e.tensor_tensor(
                                 _s[:, 7:8], _s[:, 6:7], gates[:, _tc, _e:_e + 1], ALU.mult),
                             reads=[ysr, gates_r], writes=[ysr], tag="ln2A")
                        p.op("dve", lambda e, _s=ys: e.scalar_tensor_tensor(
                                 _s[:, 8:9], _s[:, 2:3], -1.0, _s[:, 7:8], ALU.mult, ALU.mult),
                             reads=[ysr], writes=[ysr], tag="ln2B")
                    pieces.append(pc_y4)

                    for dh in range(2):
                        def pc_y5(ys=ys, ysr=ysr, tci=tci, dh=dh):
                            p.op("dve", lambda e, _d=dh, _s=ys: e.tensor_scalar(
                                     ybuf[:, _d * 512:(_d + 1) * 512],
                                     ybuf[:, _d * 512:(_d + 1) * 512],
                                     _s[:, 7:8], _s[:, 8:9], ALU.mult, ALU.add),
                                 reads=[ybuf_r, ysr], writes=[ybuf_r], tag="ffg1")
                            p.op("pool", lambda e, _d=dh: e.tensor_tensor(
                                     ybuf[:, _d * 512:(_d + 1) * 512],
                                     ybuf[:, _d * 512:(_d + 1) * 512],
                                     g2b[:, _d * 512:(_d + 1) * 512], ALU.mult),
                                 reads=[ybuf_r, g2b_r], writes=[ybuf_r], tag="ffg2")
                            p.dma("pool",
                                  x_out[tci * 128:(tci + 1) * 128, dh * 512:(dh + 1) * 512],
                                  ybuf[:, dh * 512:(dh + 1) * 512],
                                  reads=[ybuf_r], tag="ffacc", accum=True)
                        pieces.append(pc_y5)
                return pieces

            pending = []
            for e_idx in range(n_experts):
                if e_idx > 0:
                    load_weights(e_idx)
                    for f in pending:
                        f()
                    pending = []
                prep_expert(e_idx)
                for tg in range(n_tg if n_tg is not None else NTG):
                    t0 = tg * TG
                    pend = None
                    pi = 0
                    for fb in range(NFB):
                        hbi = fb % 2
                        mm_group(p, ph[hbi][:, :], ph_r[hbi],
                                 [(w1[:, k, fb * 128:(fb + 1) * 128],
                                   xnT[:, k, t0:t0 + TG], [w1_r, xnT_res])
                                  for k in range(NKC)], tag="ffs1")
                        p.op("act", lambda e, _h=hbi, _f=fb: e.activation(
                                 rt[_h][:, 0:TG], ph[_h][:, :], AF.Relu,
                                 bias=b1c[:, _f:_f + 1]),
                             reads=[ph_r[hbi], b1c_r], writes=[rt_r[hbi]], tag="ffrelu")
                        p.op("dve", lambda e, _h=hbi: e.tensor_tensor(
                                 rt[_h][:, TG:2 * TG], rt[_h][:, 0:TG], rt[_h][:, 0:TG],
                                 ALU.mult),
                             reads=[rt_r[hbi]], writes=[rt_r[hbi]], tag="ffsq")
                        if pend is not None:
                            emit_s2_l0(pend)
                        pend = fb
                        if fb % 2 == 1 and pi < len(pending):
                            pending[pi]()
                            pi += 1
                    while pi < len(pending):
                        pending[pi]()
                        pi += 1
                    emit_s2_l0(pend)
                    par = tg % 2
                    for tc2 in range(2):
                        for dh in range(2):
                            iY = tc2 * 2 + dh
                            p.op("dve", lambda e, _i=iY, _y=par * 2 + tc2, _d=dh:
                                 e.tensor_copy(
                                     ybraw[_y][:, _d * 512:(_d + 1) * 512], pY[_i][:, :]),
                                 reads=[pY_r[iY]], writes=[ybraw_r[par * 2 + tc2]],
                                 tag="ffraw")
                    p.op("dve", lambda e, _p=par: e.tensor_copy(srow_f[_p][:, :],
                                                                pstat[:, :]),
                         reads=[pstat_r], writes=[srow_f_r[par]], tag="stf32")
                    pending = make_tail(tg, e_idx, par)
            for f in pending:
                f()
            pending = []
            p.wait_all_dma()
            p.flush()




def emit_wprep(p, pairs):
    ress = []
    for (src_ap, dst_ap, n_e) in pairs:
        r = Res(p, "wm")
        for e in range(n_e):
            p.dma("pool", dst_ap[e], src_ap[e], writes=[r], tag="wcast")
        ress.append(r)
    return ress



def emit_attn_l0(p, ctx, nc, aps, x_in, x_out, o_scr, uid="a0", n_packs=8,
                 wprep_cb=None):
    xnT = alloc(ctx, nc, f"{uid}_xnT", [128, NKC, T], BF16)
    xnT_res = Res(p, "xnT")
    ident = p.consts["identbf"]
    identr = p.consts_res["identbf"]

    with ExitStack() as c1:
        emit_ln_xnT(p, c1, nc, x_in, xnT, xnT_res, uid=uid, tc0=0, ntc=NT)
        p.wait_all_dma()
        p.flush()

    with ExitStack() as c2:
        wd, wd_r = {}, {}
        for s, nm in (("q", "Wq_down"), ("k", "Wk_down"), ("v", "Wv_down")):
            wd[s], wd_r[s] = rtile(p, c2, nc, f"{uid}_w{s}d", [128, NKC, D], BF16)
            p.dma("pool", wd[s][:, :, :], aps[nm].rearrange("(k p) d -> p k d", p=128),
                  writes=[wd_r[s]], tag=f"wd{s}")
        uq, uq_r = rtile(p, c2, nc, f"{uid}_uq", [128, 64], BF16)
        uk, uk_r = rtile(p, c2, nc, f"{uid}_uk", [128, 64], BF16)
        uqr, uqr_r = rtile(p, c2, nc, f"{uid}_uqr", [128, 32], BF16)
        ukr, ukr_r = rtile(p, c2, nc, f"{uid}_ukr", [128, 32], BF16)
        uv, uv_r = rtile(p, c2, nc, f"{uid}_uv", [128, 64], BF16)
        p.dma("pool", uq[0:64, 0:32], aps["Wq_up_c"], writes=[uq_r])
        p.dma("pool", uq[0:64, 32:64], aps["Wq_up_e"], writes=[uq_r])
        p.dma("pool", uk[0:64, 0:32], aps["Wk_up_c"], writes=[uk_r])
        p.dma("pool", uk[0:64, 32:64], aps["Wk_up_e"], writes=[uk_r])
        p.dma("pool", uv[0:64, :], aps["Wv_up"], writes=[uv_r])
        p.op("dve", lambda e: e.tensor_scalar(uq[0:64, :], uq[0:64, :], 0.125, None, ALU.mult),
             reads=[uq_r], writes=[uq_r], tag="uqscale")
        p.op("dve", lambda e: e.tensor_scalar(uqr[0:64, 0:32:2], uq[0:64, 33:64:2], -1.0,
                                              None, ALU.mult),
             reads=[uq_r], writes=[uqr_r], tag="uqrA")
        p.op("dve", lambda e: e.tensor_copy(uqr[0:64, 1:32:2], uq[0:64, 32:63:2]),
             reads=[uq_r], writes=[uqr_r], tag="uqrB")
        p.op("dve", lambda e: e.tensor_scalar(ukr[0:64, 0:32:2], uk[0:64, 33:64:2], -1.0,
                                              None, ALU.mult),
             reads=[uk_r], writes=[ukr_r], tag="ukrA")
        p.op("dve", lambda e: e.tensor_copy(ukr[0:64, 1:32:2], uk[0:64, 32:63:2]),
             reads=[uk_r], writes=[ukr_r], tag="ukrB")
        for (buf, bufr, w) in ((uq, uq_r, 64), (uk, uk_r, 64), (uqr, uqr_r, 32),
                               (ukr, ukr_r, 32), (uv, uv_r, 64)):
            p.op("dve", lambda e, _b=buf, _w=w: e.tensor_copy(_b[64:128, 0:_w],
                                                              _b[0:64, 0:_w]),
                 reads=[bufr], writes=[bufr], tag="udup")
        ctab, ctab_r = rtile(p, c2, nc, f"{uid}_ctab", [32, T], BF16)
        stab, stab_r = rtile(p, c2, nc, f"{uid}_stab", [32, T], BF16)
        p.dma("pool", ctab[:, :], p.host_in["c_cos0"], writes=[ctab_r])
        p.dma("pool", stab[:, :], p.host_in["c_sin0"], writes=[stab_r])
        kmask, kmask_r = rtile(p, c2, nc, f"{uid}_kmask", [128, 4, 512], BF16)
        p.dma("pool", kmask[:, :, :], p.host_in["c_kmask"], writes=[kmask_r])
        if wprep_cb is not None:
            wprep_cb()

        lat, lat_r = {}, {}
        for s in ("q", "k", "v"):
            lat[s], lat_r[s] = rtile(p, c2, nc, f"{uid}_lat{s}", [128, T], BF16)

        QT0, QT0_r = rtile(p, c2, nc, f"{uid}_QT", [64, T], BF16)
        KT0, KT0_r = rtile(p, c2, nc, f"{uid}_KT", [64, T], BF16)
        Vt0, Vt0_r = rtile(p, c2, nc, f"{uid}_Vt", [128, 32 * 65], BF16)
        QT, QT_r = [QT0, QT0], [QT0_r, QT0_r]
        KT, KT_r = [KT0, KT0], [KT0_r, KT0_r]
        Vt, Vt_r = [Vt0, Vt0], [Vt0_r, Vt0_r]
        pbig, pbig_r = rtile(p, c2, nc, f"{uid}_pbig", [128, 512], F32, psum=True)
        pup, pup_r = pbig, pbig_r
        pub, pub_r = None, None
        re1, re1_r = rtile(p, c2, nc, f"{uid}_re1", [32, 512], F32)
        re2, re2_r = rtile(p, c2, nc, f"{uid}_re2", [32, 512], F32)
        pV, pV_r = pbig, pbig_r
        pS, pS_r = rtile(p, c2, nc, f"{uid}_pS", [128, 512], F32, psum=True, n=3)
        Pt, Pt_r = rtile(p, c2, nc, f"{uid}_Pt", [128, 512], BF16, n=3)
        pO0, pO0_r = rtile(p, c2, nc, f"{uid}_pO0", [128, 512], F32, psum=True)
        pOx, pOx_r = rtile(p, c2, nc, f"{uid}_pO", [128, 65], F32, psum=True, n=3)
        pO = [pO0] + pOx
        pO_r = [pO0_r] + pOx_r
        rcp, rcp_r = rtile(p, c2, nc, f"{uid}_rcp", [128, 4], F32)
        obuf, obuf_r = rtile(p, c2, nc, f"{uid}_ob", [128, 4, 64], BF16, n=2)

        pub, pub_r = pO[0], pO0_r
        for hp in range(n_packs):
            for s in ("q", "k", "v"):
                for tg in range(8):
                    pb = tg % 2
                    mm_group(p, pS[pb][:, :], pS_r[pb],
                             [(wd[s][:, k, hp * 128:(hp + 1) * 128],
                               xnT[:, k, tg * 512:(tg + 1) * 512], [wd_r[s], xnT_res])
                              for k in range(NKC)], tag="lat")
                    p.op("dve", lambda e, _s=s, _t=tg, _p=pb: e.tensor_copy(
                             lat[_s][:, _t * 512:(_t + 1) * 512], pS[_p][:, :]),
                         reads=[pS_r[pb]], writes=[lat_r[s]], tag="latev")
            for h2 in range(2):
                h = hp * 2 + h2
                hb = h % 2
                lo = h2 * 64
                for (U, Ur, Uresm, Urres, dst, dstr, sname) in (
                        (uq, uqr, uq_r, uqr_r, QT[hb], QT_r[hb], "q"),
                        (uk, ukr, uk_r, ukr_r, KT[hb], KT_r[hb], "k")):
                    for tg in range(8):
                        t5 = slice(tg * 512, (tg + 1) * 512)
                        mm_group(p, pup[0:64, :], pup_r,
                                 [(U[lo:lo + 64, :], lat[sname][lo:lo + 64, t5],
                                   [Uresm, lat_r[sname]])], tag="up")
                        mm_group(p, pub[0:32, 0:512], pub_r,
                                 [(Ur[lo:lo + 64, :], lat[sname][lo:lo + 64, t5],
                                   [Urres, lat_r[sname]])], tag="upr")
                        p.op("dve", lambda e, _t=t5, _d=dst: e.tensor_copy(
                                 _d[0:32, _t], pup[0:32, 0:512]),
                             reads=[pup_r], writes=[dstr], tag="upc")
                        p.op("dve", lambda e, _t=t5: e.tensor_tensor(
                                 re1[:, :], pup[32:64, 0:512], ctab[:, _t], ALU.mult),
                             reads=[pup_r, ctab_r], writes=[re1_r], tag="ropeC")
                        p.op("dve", lambda e, _t=t5: e.tensor_tensor(
                                 re2[:, :], pub[0:32, 0:512], stab[:, _t], ALU.mult),
                             reads=[pub_r, stab_r], writes=[re2_r], tag="ropeS")
                        p.op("dve", lambda e, _t=t5, _d=dst: e.tensor_add(
                                 _d[32:64, _t], re1[:, :], re2[:, :]),
                             reads=[re1_r, re2_r], writes=[dstr], tag="ropeA")
                p.op("dve", lambda e, _b=hb: e.memset(Vt[_b][:, 64:32 * 65:65], 1.0),
                     writes=[Vt_r[hb]], tag="vones")
                for tc in range(NT):
                    mm_group(p, pV[:, 0:64], pV_r,
                             [(lat["v"][lo:lo + 64, tc * 128:(tc + 1) * 128],
                               uv[lo:lo + 64, :], [lat_r["v"], uv_r])], tag="vup")
                    p.op("dve", lambda e, _tc=tc, _b=hb: e.tensor_copy(
                             Vt[_b][:, _tc * 65:_tc * 65 + 64], pV[:, 0:64]),
                         reads=[pV_r], writes=[Vt_r[hb]], tag="vev")
                for qg in range(8):
                    nkt = 4 * (qg + 1)
                    q5 = slice(qg * 512, (qg + 1) * 512)

                    def emit_av(kt, _nkt=nkt, _hb=hb):
                        sb_ = kt % 3
                        for qs in range(4):
                            o = p.op("pe", lambda e, _s=sb_, _q=qs, _kt=kt, _b=_hb:
                                     e.matmul(pO[_q][:, 0:65],
                                              Pt[_s][:, _q * 128:(_q + 1) * 128],
                                              Vt[_b][:, _kt * 65:(_kt + 1) * 65],
                                              start=(_kt == 0), stop=(_kt == _nkt - 1)),
                                     reads=[Pt_r[sb_], Vt_r[_hb]],
                                     writes=[pO_r[qs]] if kt == 0 else [], tag="av")
                            if kt == _nkt - 1:
                                pO_r[qs].set_writer(o)

                    pend = None
                    for kt in range(nkt):
                        sb_ = kt % 3
                        mm_group(p, pS[sb_][:, :], pS_r[sb_],
                                 [(KT[hb][:, kt * 128:(kt + 1) * 128], QT[hb][:, q5],
                                   [KT_r[hb], QT_r[hb]])], tag="sT")
                        if kt >= qg * 4:
                            v = kt - qg * 4
                            p.op("dve", lambda e, _s=sb_, _v=v: e.tensor_tensor(
                                     pS[_s][:, :], pS[_s][:, :], kmask[:, _v, :], ALU.add),
                                 reads=[pS_r[sb_], kmask_r], writes=[pS_r[sb_]], tag="mask")
                        p.op("act", lambda e, _s=sb_: e.activation(Pt[_s][:, :], pS[_s][:, :],
                                                                   AF.Exp),
                             reads=[pS_r[sb_]], writes=[Pt_r[sb_]], tag="exp")
                        if pend is not None:
                            emit_av(pend)
                        pend = kt
                    emit_av(pend)
                    ob = qg % 2
                    for qs in range(4):
                        p.op("dve", lambda e, _q=qs: e.reciprocal(rcp[:, _q:_q + 1],
                                                                  pO[_q][:, 64:65]),
                             reads=[pO_r[qs]], writes=[rcp_r], tag="orcp")
                        p.op("dve", lambda e, _q=qs, _o=ob: e.tensor_scalar(
                                 obuf[_o][:, _q, :], pO[_q][:, 0:64], rcp[:, _q:_q + 1],
                                 None, ALU.mult),
                             reads=[pO_r[qs], rcp_r], writes=[obuf_r[ob]], tag="oev")
                    p.dma("sp",
                          o_scr[qg * 512:(qg + 1) * 512, h * 64:(h + 1) * 64]
                          .rearrange("(qs p) d -> p qs d", p=128),
                          obuf[ob][:, :, :], reads=[obuf_r[ob]], tag="ost")
        p.wait_all_dma()
        p.flush()

    with ExitStack() as c3:
        wc, wc_r = rtile(p, c3, nc, f"{uid}_wc", [128, NKC, D], BF16)
        p.dma("pool", wc[:, :, :], aps["Wc"].rearrange("(k p) d -> p k d", p=128),
              writes=[wc_r], tag="wcld")
        otl, otl_r = rtile(p, c3, nc, f"{uid}_otl", [128, D], BF16, n=2)
        oT, oT_r = rtile(p, c3, nc, f"{uid}_oT", [128, NKC, 128], BF16, n=2)
        ptr, ptres = rtile(p, c3, nc, f"{uid}_ptr", [128, 128], BF16, psum=True, n=2)
        pX, pX_r = rtile(p, c3, nc, f"{uid}_pX", [128, 512], F32, psum=True, n=2)
        xtl, xtl_r = rtile(p, c3, nc, f"{uid}_xtl3", [128, D], F32, n=2)
        xo, xo_r = rtile(p, c3, nc, f"{uid}_xo", [128, D], F32, n=2)
        for tc in range(NT):
            b = tc % 2
            ts = slice(tc * 128, (tc + 1) * 128)
            p.dma("sp", otl[b][:, :], o_scr[ts, :], writes=[otl_r[b]], tag="old")
            p.dma("sp", xtl[b][:, :], x_in[ts, :], writes=[xtl_r[b]], tag="xld")
            for k in range(NKC):
                pb = k % 2
                p.op("pe", lambda e, _b=b, _k=k, _p=pb: e.transpose(
                         ptr[_p][:, :], otl[_b][:, _k * 128:(_k + 1) * 128], ident[:, :]),
                     reads=[otl_r[b], identr], writes=[ptres[pb]], tag="otr")
                p.op("dve", lambda e, _b=b, _k=k, _p=pb: e.tensor_copy(
                         oT[_b][:, _k, :], ptr[_p][:, :]),
                     reads=[ptres[pb]], writes=[oT_r[b]], tag="otev")
            for dh in range(2):
                pxb = dh
                mm_group(p, pX[pxb][:, :], pX_r[pxb],
                         [(oT[b][:, k, :], wc[:, k, dh * 512:(dh + 1) * 512],
                           [oT_r[b], wc_r]) for k in range(NKC)], tag="wcmm")
                p.op("dve", lambda e, _b=b, _d=dh, _p=pxb: e.tensor_add(
                         xo[_b][:, _d * 512:(_d + 1) * 512],
                         xtl[_b][:, _d * 512:(_d + 1) * 512], pX[_p][:, :]),
                     reads=[pX_r[pxb], xtl_r[b]], writes=[xo_r[b]], tag="resadd")
            p.dma("pool", x_out[ts, :], xo[b][:, :], reads=[xo_r[b]], tag="xst")
        p.wait_all_dma()
        p.flush()


def l0_rope_tables():
    inv = 1.0 / (10000.0 ** (np.arange(0, 32, 2, dtype=np.float64) / 32.0))
    ang = np.arange(T, dtype=np.float64)[None, :] * inv[:, None]
    ang2 = np.concatenate([ang, ang], axis=0)
    idx = np.arange(32) % 16
    angj = np.arange(T, dtype=np.float64)[None, :] * inv[idx][:, None]
    return (np.cos(angj).astype(np.float32), np.sin(angj).astype(np.float32))


def causal_masks():
    k = np.arange(128)[:, None, None]
    v = np.arange(4)[None, :, None]
    q = np.arange(512)[None, None, :]
    km = np.where(k <= q - 128 * v, 0.0, -30000.0).astype(np.float32)
    qq = np.arange(128)[:, None, None]
    kq = np.arange(512)[None, None, :]
    qm = np.where(kq <= qq + 128 * v, 0.0, -30000.0).astype(np.float32)
    return km, qm


def build_attn_l0_program(n_packs=8):
    nc = bass.Bass("TRN2", target_bir_lowering=False)
    hin = declare_host_inputs(nc, {
        "x": [T, D], "Wq_down": [D, D], "Wk_down": [D, D], "Wv_down": [D, D],
        "Wq_up_c": [64, 32], "Wq_up_e": [64, 32], "Wk_up_c": [64, 32],
        "Wk_up_e": [64, 32], "Wv_up": [64, 64], "Wc": [D, D],
        "c_ident": [128, 128], "c_identf": [128, 128], "c_eps": [128, 1], "c_nhalf": [128, 1], "c_phalf": [128, 1],
        "c_cos0": [32, T], "c_sin0": [32, T], "c_kmask": [128, 4, 512],
    })
    out = nc.dram_tensor("out", [T, D], F32, kind="ExternalOutput").ap()
    o_scr = nc.dram_tensor("o_scr", [T, D], BF16, kind="Internal").ap()
    with ExitStack() as ctx:
        p = Prog(nc)
        ctx.enter_context(p._stack)
        p.host_in = hin
        load_consts(p, ctx, nc)
        emit_attn_l0(p, ctx, nc, hin, hin["x"], out, o_scr, n_packs=n_packs)
        p.flush()
    return nc





def emit_attn_l1(p, ctx, nc, aps, x_in, x_out, o_scr, uid="a1", n_heads=NH):
    xnT = alloc(ctx, nc, f"{uid}_xnT", [128, NKC, T], BF16)
    xnT_res = Res(p, "xnT")
    ident = p.consts["identbf"]
    identr = p.consts_res["identbf"]
    identf = p.consts["identf"]
    identfr = p.consts_res["identf"]
    SC = 1.0 / np.sqrt(96.0)

    with ExitStack() as c1:
        emit_ln_xnT(p, c1, nc, x_in, xnT, xnT_res, uid=uid, tc0=0, ntc=NT)
        p.wait_all_dma()
        p.flush()

    with ExitStack() as c2:
        wkvd, wkvd_r = rtile(p, c2, nc, f"{uid}_wkvd", [128, NKC, 256], BF16)
        p.dma("pool", wkvd[:, :, :], aps["Wkvd"].rearrange("(k p) c -> p k c", p=128),
              writes=[wkvd_r], tag="wkvd")
        wku, wku_r = rtile(p, c2, nc, f"{uid}_wku", [128, 2, D], BF16)
        p.dma("pool", wku[:, :, :], aps["Wku"].rearrange("(m p) c -> p m c", p=128),
              writes=[wku_r], tag="wku")
        wvu, wvu_r = rtile(p, c2, nc, f"{uid}_wvu", [128, 2, D], BF16)
        p.dma("pool", wvu[:, :, :], aps["Wvu"].rearrange("(m p) c -> p m c", p=128),
              writes=[wvu_r], tag="wvu")
        wkra, wkra_r = rtile(p, c2, nc, f"{uid}_wkra", [128, NKC, 16], BF16)
        wkrb, wkrb_r = rtile(p, c2, nc, f"{uid}_wkrb", [128, NKC, 16], BF16)
        p.dma("pool", wkra[:, :, :],
              bass.AP(aps["Wkr"].tensor, 0, [[32, 128], [32 * 128, 8], [1, 16]]),
              writes=[wkra_r], tag="wkra")
        p.dma("pool", wkrb[:, :, :],
              bass.AP(aps["Wkr"].tensor, 16, [[32, 128], [32 * 128, 8], [1, 16]]),
              writes=[wkrb_r], tag="wkrb")
        c2tab, c2tab_r = rtile(p, c2, nc, f"{uid}_c2t", [64, T], BF16)
        s2tab, s2tab_r = rtile(p, c2, nc, f"{uid}_s2t", [64, T], BF16)
        p.dma("pool", c2tab[:, :], p.host_in["c_cos1"], writes=[c2tab_r])
        p.dma("pool", s2tab[:, :], p.host_in["c_sin1"], writes=[s2tab_r])
        kmask, kmask_r = rtile(p, c2, nc, f"{uid}_km", [128, 4, 512], BF16)
        p.dma("pool", kmask[:, :, :], p.host_in["c_kmask"], writes=[kmask_r])
        qmask, qmask_r = rtile(p, c2, nc, f"{uid}_qm", [128, 4, 512], BF16)
        p.dma("pool", qmask[:, :, :], p.host_in["c_qmask"], writes=[qmask_r])
        lob, lob_r = rtile(p, c2, nc, f"{uid}_lob", [1, 16], F32)
        p.dma("sp", lob[:, :], aps["lobo"].rearrange("(o h) -> o h", o=1), writes=[lob_r])
        lobe, lobe_r = rtile(p, c2, nc, f"{uid}_lobe", [1, 16], BF16)
        p.op("act", lambda e: e.activation(lobe[:, :], lob[:, :], AF.Exp),
             reads=[lob_r], writes=[lobe_r], tag="lobexp")
        onescol, onescol_r = rtile(p, c2, nc, f"{uid}_oc", [1, 128], BF16)
        p.op("dve", lambda e: e.memset(onescol[:, :], 1.0), writes=[onescol_r])
        Cb, Cb_r = rtile(p, c2, nc, f"{uid}_Cb", [128, 16], F32)

        latT, latT_r = rtile(p, c2, nc, f"{uid}_latT", [128, 2, T], BF16)
        kr1, kr1_r = rtile(p, c2, nc, f"{uid}_kr1", [16, T], BF16)
        kr2, kr2_r = rtile(p, c2, nc, f"{uid}_kr2", [16, T], BF16)
        rk1, rk1_r = rtile(p, c2, nc, f"{uid}_rk1", [16, 512], F32)
        rk2, rk2_r = rtile(p, c2, nc, f"{uid}_rk2", [16, 512], F32)

        with ExitStack() as c2a:
            pP, pP_r = rtile(p, c2a, nc, f"{uid}_pP", [128, 512], F32, psum=True, n=2)
            pK1, pK1_r = rtile(p, c2a, nc, f"{uid}_pK1", [16, 512], F32, psum=True)
            pK2, pK2_r = rtile(p, c2a, nc, f"{uid}_pK2", [16, 512], F32, psum=True)
            pCb, pCb_r = rtile(p, c2a, nc, f"{uid}_pCb", [128, 16], F32, psum=True)
            mm_group(p, pCb[:, :], pCb_r, [(onescol[:, :], lobe[:, :],
                                            [onescol_r, lobe_r])], tag="cbmm")
            p.op("dve", lambda e: e.tensor_copy(Cb[:, :], pCb[:, :]),
                 reads=[pCb_r], writes=[Cb_r], tag="cbev")
            for mc in range(2):
                for tg in range(8):
                    pb = tg % 2
                    mm_group(p, pP[pb][:, :], pP_r[pb],
                             [(wkvd[:, k, mc * 128:(mc + 1) * 128],
                               xnT[:, k, tg * 512:(tg + 1) * 512], [wkvd_r, xnT_res])
                              for k in range(NKC)], tag="latmm")
                    p.op("dve", lambda e, _m=mc, _t=tg, _p=pb: e.tensor_copy(
                             latT[:, _m, _t * 512:(_t + 1) * 512], pP[_p][:, :]),
                         reads=[pP_r[pb]], writes=[latT_r], tag="latev")
            for tg in range(8):
                t5 = slice(tg * 512, (tg + 1) * 512)
                mm_group(p, pK1[:, :], pK1_r,
                         [(wkra[:, k, :], xnT[:, k, t5], [wkra_r, xnT_res])
                          for k in range(NKC)], tag="krmm1")
                mm_group(p, pK2[:, :], pK2_r,
                         [(wkrb[:, k, :], xnT[:, k, t5], [wkrb_r, xnT_res])
                          for k in range(NKC)], tag="krmm2")
                p.op("dve", lambda e, _t=t5: e.tensor_tensor(rk1[:, :], pK1[:, :],
                                                             c2tab[0:16, _t], ALU.mult),
                     reads=[pK1_r, c2tab_r], writes=[rk1_r], tag="krc")
                p.op("dve", lambda e, _t=t5: e.tensor_tensor(rk2[:, :], pK2[:, :],
                                                             s2tab[0:16, _t], ALU.mult),
                     reads=[pK2_r, s2tab_r], writes=[rk2_r], tag="krs")
                p.op("dve", lambda e, _t=t5: e.tensor_sub(kr1[:, _t], rk1[:, :], rk2[:, :]),
                     reads=[rk1_r, rk2_r], writes=[kr1_r], tag="kr1f")
                p.op("dve", lambda e, _t=t5: e.tensor_tensor(rk1[:, :], pK2[:, :],
                                                             c2tab[0:16, _t], ALU.mult),
                     reads=[pK2_r, c2tab_r], writes=[rk1_r], tag="krc2")
                p.op("dve", lambda e, _t=t5: e.tensor_tensor(rk2[:, :], pK1[:, :],
                                                             s2tab[0:16, _t], ALU.mult),
                     reads=[pK1_r, s2tab_r], writes=[rk2_r], tag="krs2")
                p.op("dve", lambda e, _t=t5: e.tensor_add(kr2[:, _t], rk1[:, :], rk2[:, :]),
                     reads=[rk1_r, rk2_r], writes=[kr2_r], tag="kr2f")
            p.wait_all_dma()
            p.flush()

        wqp, wqp_r = rtile(p, c2, nc, f"{uid}_wqp", [128, NKC, 256], BF16)
        wqrp, wqrp_r = rtile(p, c2, nc, f"{uid}_wqrp", [128, NKC, 128], BF16)
        wqrm, wqrm_r = rtile(p, c2, nc, f"{uid}_wqrm", [128, NKC, 128], BF16)
        qr1, qr1_r = rtile(p, c2, nc, f"{uid}_qr1", [64, T], BF16)
        qr2, qr2_r = rtile(p, c2, nc, f"{uid}_qr2", [64, T], BF16)
        rq1, rq1_r = rtile(p, c2, nc, f"{uid}_rq1", [64, 512], F32)
        rq2, rq2_r = rtile(p, c2, nc, f"{uid}_rq2", [64, 512], F32)
        QT, QT_r = rtile(p, c2, nc, f"{uid}_QT", [97, T], BF16)
        KT, KT_r = rtile(p, c2, nc, f"{uid}_KT", [97, T], BF16)
        Vt, Vt_r = rtile(p, c2, nc, f"{uid}_Vt", [128, 32 * 65], BF16)
        m_all, m_all_r = rtile(p, c2, nc, f"{uid}_mall", [128, 32], F32)
        mrow, mrow_r = rtile(p, c2, nc, f"{uid}_mrow", [32, 128], BF16)
        mtmp, mtmp_r = rtile(p, c2, nc, f"{uid}_mtmp", [128, 2], F32)
        rcp, rcp_r = rtile(p, c2, nc, f"{uid}_rcp", [128, 4], F32)
        obuf, obuf_r = rtile(p, c2, nc, f"{uid}_ob", [128, 4, 64], BF16, n=2)
        with ExitStack() as c2b:
            pS, pS_r = rtile(p, c2b, nc, f"{uid}_pS", [128, 512], F32, psum=True, n=3)
            Pt, Pt_r = rtile(p, c2b, nc, f"{uid}_Pt", [128, 512], BF16, n=3)
            pO, pO_r = rtile(p, c2b, nc, f"{uid}_pO", [128, 65], F32, psum=True, n=4)
            pM, pM_r = rtile(p, c2b, nc, f"{uid}_pM", [32, 128], F32, psum=True)

            for hq in range(cdiv(n_heads, 4)):
                p.dma("pool", wqp[:, :, :],
                      bass.AP(aps["Wq"].tensor, hq * 256,
                              [[D, 128], [D * 128, NKC], [1, 256]]),
                      writes=[wqp_r], tag="wqp")
                p.op("dve", lambda e: e.tensor_scalar(
                         bass.AP(wqp, 0, [[NKC * 256, 128], [1, NKC * 256]]),
                         bass.AP(wqp, 0, [[NKC * 256, 128], [1, NKC * 256]]),
                         SC, None, ALU.mult),
                     reads=[wqp_r], writes=[wqp_r], tag="wqscale")
                p.dma("pool", wqrp[:, :, :],
                      bass.AP(aps["Wqr"].tensor, hq * 128,
                              [[512, 128], [512 * 128, NKC], [1, 128]]),
                      writes=[wqrp_r], tag="wqrp")
                for k in range(NKC):
                    p.op("dve", lambda e, _k=k: e.tensor_scalar(
                             wqrm[:, _k, :],
                             bass.AP(wqrp, _k * 128,
                                     [[NKC * 128, 128], [16, 2], [32, 4], [1, 16]]),
                             SC, None, ALU.mult),
                         reads=[wqrp_r], writes=[wqrm_r], tag="wqperm")
                for tg in range(8):
                    t5 = slice(tg * 512, (tg + 1) * 512)
                    pb = tg % 2
                    mm_group(p, pS[pb][:, :], pS_r[pb],
                             [(wqrm[:, k, :], xnT[:, k, t5], [wqrm_r, xnT_res])
                              for k in range(NKC)], tag="qrmm")
                    p.op("dve", lambda e, _t=t5, _p=pb: e.tensor_tensor(
                             rq1[:, :], pS[_p][0:64, :], c2tab[:, _t], ALU.mult),
                         reads=[pS_r[pb], c2tab_r], writes=[rq1_r], tag="qrc")
                    p.op("dve", lambda e, _t=t5, _p=pb: e.tensor_tensor(
                             rq2[:, :], pS[_p][64:128, :], s2tab[:, _t], ALU.mult),
                         reads=[pS_r[pb], s2tab_r], writes=[rq2_r], tag="qrs")
                    p.op("dve", lambda e, _t=t5: e.tensor_sub(qr1[:, _t], rq1[:, :], rq2[:, :]),
                         reads=[rq1_r, rq2_r], writes=[qr1_r], tag="qr1f")
                    p.op("dve", lambda e, _t=t5, _p=pb: e.tensor_tensor(
                             rq1[:, :], pS[_p][64:128, :], c2tab[:, _t], ALU.mult),
                         reads=[pS_r[pb], c2tab_r], writes=[rq1_r], tag="qrc2")
                    p.op("dve", lambda e, _t=t5, _p=pb: e.tensor_tensor(
                             rq2[:, :], pS[_p][0:64, :], s2tab[:, _t], ALU.mult),
                         reads=[pS_r[pb], s2tab_r], writes=[rq2_r], tag="qrs2")
                    p.op("dve", lambda e, _t=t5: e.tensor_add(qr2[:, _t], rq1[:, :], rq2[:, :]),
                         reads=[rq1_r, rq2_r], writes=[qr2_r], tag="qr2f")
                for h4 in range(min(4, n_heads - hq * 4)):
                    h = hq * 4 + h4
                    for tg in range(8):
                        t5 = slice(tg * 512, (tg + 1) * 512)
                        pb = tg % 2
                        mm_group(p, pS[pb][0:64, :], pS_r[pb],
                                 [(wqp[:, k, h4 * 64:(h4 + 1) * 64], xnT[:, k, t5],
                                   [wqp_r, xnT_res]) for k in range(NKC)], tag="qcmm")
                        p.op("dve", lambda e, _t=t5, _p=pb: e.tensor_copy(
                                 QT[0:64, _t], pS[_p][0:64, 0:512]),
                             reads=[pS_r[pb]], writes=[QT_r], tag="qcev")
                    p.dma("pool", QT[64:80, :], qr1[h4 * 16:(h4 + 1) * 16, :],
                          reads=[qr1_r], writes=[QT_r], tag="qrcp1")
                    p.dma("pool", QT[80:96, :], qr2[h4 * 16:(h4 + 1) * 16, :],
                          reads=[qr2_r], writes=[QT_r], tag="qrcp2")
                    for tg in range(8):
                        t5 = slice(tg * 512, (tg + 1) * 512)
                        pb = tg % 2
                        mm_group(p, pS[pb][0:64, :], pS_r[pb],
                                 [(wku[:, mc, h * 64:(h + 1) * 64], latT[:, mc, t5],
                                   [wku_r, latT_r]) for mc in range(2)], tag="kcmm")
                        p.op("dve", lambda e, _t=t5, _p=pb: e.tensor_copy(
                                 KT[0:64, _t], pS[_p][0:64, 0:512]),
                             reads=[pS_r[pb]], writes=[KT_r], tag="kcev")
                    p.dma("pool", KT[64:80, :], kr1[:, :], reads=[kr1_r], writes=[KT_r],
                          tag="krcp1")
                    p.dma("pool", KT[80:96, :], kr2[:, :], reads=[kr2_r], writes=[KT_r],
                          tag="krcp2")
                    p.dma("pool", KT[96:97, :], p.host_in["c_ones"], writes=[KT_r],
                          tag="onescp")
                    p.op("dve", lambda e: e.memset(Vt[:, 64:32 * 65:65], 1.0),
                         writes=[Vt_r], tag="vones")
                    for tc in range(NT):
                        mm_group(p, pS[0][:, 0:64], pS_r[0],
                                 [(latT[:, mc, tc * 128:(tc + 1) * 128],
                                   wvu[:, mc, h * 64:(h + 1) * 64],
                                   [latT_r, wvu_r]) for mc in range(2)], tag="vmm")
                        p.op("dve", lambda e, _tc=tc: e.tensor_copy(
                                 Vt[:, _tc * 65:_tc * 65 + 64], pS[0][:, 0:64]),
                             reads=[pS_r[0]], writes=[Vt_r], tag="vev")
                    for qb in range(NT):
                        nk4 = qb // 4 + 1
                        for k4 in range(nk4):
                            N = min(512, (qb + 1) * 128 - k4 * 512)
                            sb_ = k4 % 3
                            mm_group(p, pS[sb_][:, 0:N], pS_r[sb_],
                                     [(QT[0:96, qb * 128:(qb + 1) * 128],
                                       KT[0:96, k4 * 512:k4 * 512 + N],
                                       [QT_r, KT_r])], tag="s1")
                            if k4 == nk4 - 1:
                                o = qb - 4 * k4
                                p.op("dve", lambda e, _s=sb_, _o=o, _N=N: e.tensor_tensor(
                                         pS[_s][:, 0:_N], pS[_s][:, 0:_N],
                                         qmask[:, _o, 0:_N], ALU.add),
                                     reads=[pS_r[sb_], qmask_r], writes=[pS_r[sb_]],
                                     tag="qmask")
                            p.op("dve", lambda e, _s=sb_, _N=N, _c=(k4 % 2): e.reduce_max(
                                     mtmp[:, _c:_c + 1], pS[_s][:, 0:_N], axis=AX.X),
                                 reads=[pS_r[sb_]], writes=[mtmp_r], tag="p1max")
                            if k4 == 0:
                                p.op("dve", lambda e, _qb=qb: e.tensor_copy(
                                         m_all[:, _qb:_qb + 1], mtmp[:, 0:1]),
                                     reads=[mtmp_r], writes=[m_all_r], tag="p1cp")
                            else:
                                p.op("dve", lambda e, _qb=qb, _c=(k4 % 2): e.tensor_max(
                                         m_all[:, _qb:_qb + 1], m_all[:, _qb:_qb + 1],
                                         mtmp[:, _c:_c + 1]),
                                     reads=[mtmp_r, m_all_r], writes=[m_all_r], tag="p1mx")
                    p.op("pe", lambda e: e.transpose(pM[:, :], m_all[:, :],
                                                     identf[0:128, 0:128]),
                         reads=[m_all_r, identfr], writes=[pM_r], tag="mtr")
                    p.op("dve", lambda e: e.tensor_scalar(mrow[:, :], pM[:, :], -1.0,
                                                          None, ALU.mult),
                         reads=[pM_r], writes=[mrow_r], tag="mneg")
                    p.dma("pool", QT[96:97, :], mrow[:, :], reads=[mrow_r], writes=[QT_r],
                          tag="mcp")
                    for qg in range(8):
                        nkt = 4 * (qg + 1)
                        q5 = slice(qg * 512, (qg + 1) * 512)

                        def emit_av(kt, _nkt=nkt):
                            sb_ = kt % 3
                            for qs in range(4):
                                o = p.op("pe", lambda e, _s=sb_, _q=qs, _kt=kt:
                                         e.matmul(pO[_q][:, :],
                                                  Pt[_s][:, _q * 128:(_q + 1) * 128],
                                                  Vt[:, _kt * 65:(_kt + 1) * 65],
                                                  start=(_kt == 0), stop=(_kt == _nkt - 1)),
                                         reads=[Pt_r[sb_], Vt_r],
                                         writes=[pO_r[qs]] if kt == 0 else [], tag="av")
                                if kt == _nkt - 1:
                                    pO_r[qs].set_writer(o)

                        pend = None
                        for kt in range(nkt):
                            sb_ = kt % 3
                            mm_group(p, pS[sb_][:, :], pS_r[sb_],
                                     [(KT[:, kt * 128:(kt + 1) * 128], QT[:, q5],
                                       [KT_r, QT_r])], tag="sT")
                            if kt >= qg * 4:
                                v = kt - qg * 4
                                p.op("dve", lambda e, _s=sb_, _v=v: e.tensor_tensor(
                                         pS[_s][:, :], pS[_s][:, :], kmask[:, _v, :],
                                         ALU.add),
                                     reads=[pS_r[sb_], kmask_r], writes=[pS_r[sb_]],
                                     tag="mask")
                            p.op("act", lambda e, _s=sb_: e.activation(
                                     Pt[_s][:, :], pS[_s][:, :], AF.Exp),
                                 reads=[pS_r[sb_]], writes=[Pt_r[sb_]], tag="exp")
                            if pend is not None:
                                emit_av(pend)
                            pend = kt
                        emit_av(pend)
                        ob = qg % 2
                        for qs in range(4):
                            p.op("dve", lambda e, _q=qs, _h=h: e.tensor_scalar(
                                     rcp[:, _q:_q + 1], pO[_q][:, 64:65],
                                     Cb[:, _h:_h + 1], None, ALU.add),
                                 reads=[pO_r[qs], Cb_r], writes=[rcp_r], tag="odenC")
                            p.op("dve", lambda e, _q=qs: e.reciprocal(
                                     rcp[:, _q:_q + 1], rcp[:, _q:_q + 1]),
                                 reads=[rcp_r], writes=[rcp_r], tag="orcp")
                            p.op("dve", lambda e, _q=qs, _o=ob: e.tensor_scalar(
                                     obuf[_o][:, _q, :], pO[_q][:, 0:64],
                                     rcp[:, _q:_q + 1], None, ALU.mult),
                                 reads=[pO_r[qs], rcp_r], writes=[obuf_r[ob]], tag="oev")
                        p.dma("sp",
                              o_scr[qg * 512:(qg + 1) * 512, h * 64:(h + 1) * 64]
                              .rearrange("(qs p) d -> p qs d", p=128),
                              obuf[ob][:, :, :], reads=[obuf_r[ob]], tag="ost")
            p.wait_all_dma()
            p.flush()

    with ExitStack() as c3:
        wo, wo_r = rtile(p, c3, nc, f"{uid}_wo", [128, NKC, D], BF16)
        p.dma("pool", wo[:, :, :], aps["Wo"].rearrange("(k p) d -> p k d", p=128),
              writes=[wo_r], tag="wold")
        otl, otl_r = rtile(p, c3, nc, f"{uid}_otl", [128, D], BF16, n=2)
        oT, oT_r = rtile(p, c3, nc, f"{uid}_oT", [128, NKC, 128], BF16, n=2)
        ptr, ptres = rtile(p, c3, nc, f"{uid}_ptr", [128, 128], BF16, psum=True, n=2)
        pX, pX_r = rtile(p, c3, nc, f"{uid}_pX", [128, 512], F32, psum=True, n=2)
        xtl, xtl_r = rtile(p, c3, nc, f"{uid}_xtl3", [128, D], F32, n=2)
        xo, xo_r = rtile(p, c3, nc, f"{uid}_xo", [128, D], F32, n=2)
        for tc in range(NT):
            b = tc % 2
            ts = slice(tc * 128, (tc + 1) * 128)
            p.dma("sp", otl[b][:, :], o_scr[ts, :], writes=[otl_r[b]], tag="old")
            p.dma("sp", xtl[b][:, :], x_in[ts, :], writes=[xtl_r[b]], tag="xld")
            for k in range(NKC):
                pb = k % 2
                p.op("pe", lambda e, _b=b, _k=k, _p=pb: e.transpose(
                         ptr[_p][:, :], otl[_b][:, _k * 128:(_k + 1) * 128], ident[:, :]),
                     reads=[otl_r[b], identr], writes=[ptres[pb]], tag="otr")
                p.op("dve", lambda e, _b=b, _k=k, _p=pb: e.tensor_copy(
                         oT[_b][:, _k, :], ptr[_p][:, :]),
                     reads=[ptres[pb]], writes=[oT_r[b]], tag="otev")
            for dh in range(2):
                mm_group(p, pX[dh][:, :], pX_r[dh],
                         [(oT[b][:, k, :], wo[:, k, dh * 512:(dh + 1) * 512],
                           [oT_r[b], wo_r]) for k in range(NKC)], tag="womm")
                p.op("dve", lambda e, _b=b, _d=dh: e.tensor_add(
                         xo[_b][:, _d * 512:(_d + 1) * 512],
                         xtl[_b][:, _d * 512:(_d + 1) * 512], pX[_d][:, :]),
                     reads=[pX_r[dh], xtl_r[b]], writes=[xo_r[b]], tag="resadd")
            p.dma("pool", x_out[ts, :], xo[b][:, :], reads=[xo_r[b]], tag="xst")
        p.wait_all_dma()
        p.flush()


def l1_rope_tables():
    inv = 1.0 / (10000.0 ** (np.arange(16, dtype=np.float64) / 16.0))
    ang = inv[:, None] * np.arange(T, dtype=np.float64)[None, :]
    c = np.cos(ang).astype(np.float32)
    s = np.sin(ang).astype(np.float32)
    return np.tile(c, (4, 1)), np.tile(s, (4, 1))


def build_attn_l1_program(n_heads=NH):
    nc = bass.Bass("TRN2", target_bir_lowering=False)
    hin = declare_host_inputs(nc, {
        "x": [T, D], "Wq": [D, D], "Wqr": [D, 512], "Wkr": [D, 32],
        "Wkvd": [D, 256], "Wku": [256, D], "Wvu": [256, D], "Wo": [D, D],
        "lobo": [16],
        "c_ident": [128, 128], "c_identf": [128, 128], "c_eps": [128, 1], "c_nhalf": [128, 1], "c_phalf": [128, 1],
        "c_cos1": [64, T], "c_sin1": [64, T], "c_kmask": [128, 4, 512],
        "c_qmask": [128, 4, 512], "c_ones": [1, T],
    })
    out = nc.dram_tensor("out", [T, D], F32, kind="ExternalOutput").ap()
    o_scr = nc.dram_tensor("o_scr", [T, D], BF16, kind="Internal").ap()
    with ExitStack() as ctx:
        p = Prog(nc)
        ctx.enter_context(p._stack)
        p.host_in = hin
        load_consts(p, ctx, nc)
        emit_attn_l1(p, ctx, nc, hin, hin["x"], out, o_scr, n_heads=n_heads)
        p.flush()
    return nc




def host_consts():
    cos0, sin0 = l0_rope_tables()
    km, qm = causal_masks()
    return {
        "c_ident": np.eye(128, dtype=np.float32),
        "c_identf": np.eye(128, dtype=np.float32),
        "c_eps": np.full((128, 1), 1e-5, dtype=np.float32),
        "c_nhalf": np.full((128, 1), -0.5, dtype=np.float32),
        "c_phalf": np.full((128, 1), 0.5, dtype=np.float32),
        "c_cos0": cos0, "c_sin0": sin0, "c_kmask": km, "c_qmask": qm,
        "c_cos1": l1_rope_tables()[0], "c_sin1": l1_rope_tables()[1],
        "c_ones": np.ones((1, T), dtype=np.float32),
    }


def load_consts(p, ctx, nc):
    identbf = alloc(ctx, nc, "identbf", [128, 128], BF16)
    p.consts["identbf"] = identbf
    p.consts_res["identbf"] = Res(p, "identbf")
    p.dma("pool", identbf[:, :], p.host_in["c_ident"], writes=[p.consts_res["identbf"]])
    identf = alloc(ctx, nc, "identf", [128, 128], F32)
    p.consts["identf"] = identf
    p.consts_res["identf"] = Res(p, "identf")
    p.dma("sp", identf[:, :], p.host_in["c_identf"], writes=[p.consts_res["identf"]])
    p.dump = alloc(ctx, nc, "dumpbuf", [128, D], F32)
    epst = alloc(ctx, nc, "epst", [128, 1], F32)
    p.consts["eps"] = epst
    p.consts_res["eps"] = Res(p, "eps")
    p.dma("sp", epst[:, :], p.host_in["c_eps"], writes=[p.consts_res["eps"]])
    for nm in ("nhalf", "phalf"):
        tt = alloc(ctx, nc, nm, [128, 1], F32)
        p.consts[nm] = tt
        p.consts_res[nm] = Res(p, nm)
        p.dma("sp", tt[:, :], p.host_in["c_" + nm], writes=[p.consts_res[nm]])


def declare_host_inputs(nc, specs):
    aps = {}
    for name, shape in specs.items():
        aps[name] = nc.dram_tensor(name, list(shape), F32, kind="ExternalInput").ap()
    return aps


def build_ffn_l0_program(n_experts=NE, n_tg=None):
    nc = bass.Bass("TRN2", target_bir_lowering=False)
    hin = declare_host_inputs(nc, {
        "x": [T, D], "Wr": [D, 8], "br": [8],
        "W1": [NE, D, FF], "b1": [NE, FF], "g1": [NE, FF], "be1": [NE, FF],
        "W2": [NE, FF, D], "b2": [NE, D], "g2": [NE, D], "be2": [NE, D],
        "c_ident": [128, 128], "c_identf": [128, 128], "c_eps": [128, 1], "c_nhalf": [128, 1], "c_phalf": [128, 1],
    })
    out = nc.dram_tensor("out", [T, D], F32, kind="ExternalOutput").ap()
    w1m = nc.dram_tensor("w1m", [NE, D, FF], BF16, kind="Internal").ap()
    w2m = nc.dram_tensor("w2m", [NE, FF, D], BF16, kind="Internal").ap()
    with ExitStack() as ctx:
        p = Prog(nc)
        ctx.enter_context(p._stack)
        p.host_in = hin
        load_consts(p, ctx, nc)
        emit_wprep(p, [(hin["W1"], w1m, NE), (hin["W2"], w2m, NE)])
        aps = dict(hin)
        aps["w1m"], aps["w2m"] = w1m, w2m
        emit_ffn_l0(p, ctx, nc, aps, hin["x"], out, n_experts=n_experts, n_tg=n_tg)
        p.flush()
    return nc


def build_ffn_l1_program(n_experts=NE, n_tg=None):
    nc = bass.Bass("TRN2", target_bir_lowering=False)
    hin = declare_host_inputs(nc, {
        "x": [T, D], "wg_red": [16, D], "wg": [8, 16],
        "w1": [NE, D, FF], "b1": [NE, FF], "w2": [NE, FF, D], "b2": [NE, D],
        "c_ident": [128, 128], "c_identf": [128, 128], "c_eps": [128, 1], "c_nhalf": [128, 1], "c_phalf": [128, 1],
    })
    out = nc.dram_tensor("out", [T, D], F32, kind="ExternalOutput").ap()
    w1m = nc.dram_tensor("w1m", [NE, D, FF], BF16, kind="Internal").ap()
    w2m = nc.dram_tensor("w2m", [NE, FF, D], BF16, kind="Internal").ap()
    with ExitStack() as ctx:
        p = Prog(nc)
        ctx.enter_context(p._stack)
        p.host_in = hin
        load_consts(p, ctx, nc)
        emit_wprep(p, [(hin["w1"], w1m, NE), (hin["w2"], w2m, NE)])
        aps = dict(hin)
        aps["w1m"], aps["w2m"] = w1m, w2m
        emit_ffn_l1(p, ctx, nc, aps, hin["x"], out, n_experts=n_experts, n_tg=n_tg)
        p.flush()
    return nc


def build_fused_program():
    nc = bass.Bass("TRN2", target_bir_lowering=False)
    hin = declare_host_inputs(nc, {
        "x": [T, D],
        "a0_Wq_down": [D, D], "a0_Wk_down": [D, D], "a0_Wv_down": [D, D],
        "a0_Wq_up_c": [64, 32], "a0_Wq_up_e": [64, 32], "a0_Wk_up_c": [64, 32],
        "a0_Wk_up_e": [64, 32], "a0_Wv_up": [64, 64], "a0_Wc": [D, D],
        "f0_Wr": [D, 8], "f0_br": [8], "f0_W1": [NE, D, FF], "f0_b1": [NE, FF],
        "f0_g1": [NE, FF], "f0_be1": [NE, FF], "f0_W2": [NE, FF, D],
        "f0_b2": [NE, D], "f0_g2": [NE, D], "f0_be2": [NE, D],
        "a1_Wq": [D, D], "a1_Wqr": [D, 512], "a1_Wkr": [D, 32],
        "a1_Wkvd": [D, 256], "a1_Wku": [256, D], "a1_Wvu": [256, D],
        "a1_Wo": [D, D], "a1_lobo": [16],
        "f1_wg_red": [16, D], "f1_wg": [8, 16], "f1_w1": [NE, D, FF],
        "f1_b1": [NE, FF], "f1_w2": [NE, FF, D], "f1_b2": [NE, D],
        "c_ident": [128, 128], "c_identf": [128, 128], "c_eps": [128, 1], "c_nhalf": [128, 1], "c_phalf": [128, 1],
        "c_cos0": [32, T], "c_sin0": [32, T], "c_kmask": [128, 4, 512],
        "c_qmask": [128, 4, 512], "c_cos1": [64, T], "c_sin1": [64, T],
        "c_ones": [1, T],
    })
    out = nc.dram_tensor("out", [T, D], F32, kind="ExternalOutput").ap()
    o_scr = nc.dram_tensor("o_scr", [T, D], BF16, kind="Internal").ap()
    x1 = nc.dram_tensor("x1s", [T, D], F32, kind="Internal").ap()
    x2 = nc.dram_tensor("x2s", [T, D], F32, kind="Internal").ap()
    x3 = nc.dram_tensor("x3s", [T, D], F32, kind="Internal").ap()
    f0w1m = nc.dram_tensor("f0w1m", [NE, D, FF], BF16, kind="Internal").ap()
    f0w2m = nc.dram_tensor("f0w2m", [NE, FF, D], BF16, kind="Internal").ap()
    f1w1m = nc.dram_tensor("f1w1m", [NE, D, FF], BF16, kind="Internal").ap()
    f1w2m = nc.dram_tensor("f1w2m", [NE, FF, D], BF16, kind="Internal").ap()

    def sub(d, pre):
        return {k[len(pre):]: v for k, v in d.items() if k.startswith(pre)}

    with ExitStack() as ctx:
        p = Prog(nc)
        ctx.enter_context(p._stack)
        p.host_in = hin
        load_consts(p, ctx, nc)

        def _wprep():
            emit_wprep(p, [(hin["f0_W1"], f0w1m, NE), (hin["f0_W2"], f0w2m, NE),
                           (hin["f1_w1"], f1w1m, NE), (hin["f1_w2"], f1w2m, NE)])
        a1aps = sub(hin, "a1_")
        a1aps["lobo"] = hin["a1_lobo"]
        f0aps = sub(hin, "f0_")
        f0aps["w1m"], f0aps["w2m"] = f0w1m, f0w2m
        f1aps = sub(hin, "f1_")
        f1aps["w1m"], f1aps["w2m"] = f1w1m, f1w2m
        with ExitStack() as c:
            emit_attn_l0(p, c, nc, sub(hin, "a0_"), hin["x"], x1, o_scr,
                         wprep_cb=_wprep)
        with ExitStack() as c:
            emit_ffn_l0(p, c, nc, f0aps, x1, x2)
        with ExitStack() as c:
            emit_attn_l1(p, c, nc, a1aps, x2, x3, o_scr)
        with ExitStack() as c:
            emit_ffn_l1(p, c, nc, f1aps, x3, out)
        p.flush()
    return nc


_PROGS = {}


def _get_programs():
    if "p" not in _PROGS:
        _PROGS["p"] = (build_attn_l0_program(), build_ffn_l0_program(),
                       build_attn_l1_program(), build_ffn_l1_program())
    return _PROGS["p"]


def _f32(a):
    return np.ascontiguousarray(a, dtype=np.float32)


def _get_fused():
    if "f" not in _PROGS:
        _PROGS["f"] = build_fused_program()
    return _PROGS["f"]


def kernel(**inputs):
    import os
    n = 8
    consts = host_consts()
    if not os.environ.get("KERNEL_UNFUSED"):
        nc = _get_fused()
        wmap = {}
        for pre, src_pre, keys in (
            ("a0_", "L0_attn_", ("Wq_down", "Wk_down", "Wv_down", "Wq_up_c",
                                 "Wq_up_e", "Wk_up_c", "Wk_up_e", "Wv_up", "Wc")),
            ("f0_", "L0_ffn_", ("Wr", "br", "W1", "b1", "g1", "be1", "W2",
                                "b2", "g2", "be2")),
            ("f1_", "L1_ffn_", ("wg_red", "wg", "w1", "b1", "w2", "b2")),
        ):
            for k in keys:
                wmap[pre + k] = _f32(inputs[src_pre + k])
        for k, k2 in (("Wq", "Wq"), ("Wqr", "Wqr"), ("Wkr", "Wkr"),
                      ("Wkvd", "Wkvd"), ("Wku", "Wku"), ("Wvu", "Wvu"),
                      ("Wo", "Wo"), ("lobo", "lobo_log")):
            wmap["a1_" + k] = _f32(inputs["L1_attn_" + k2])
        wmap.update(consts)
        in_maps = []
        for c in range(n):
            m = {"x": _f32(inputs["L0_attn_x"][c])}
            m.update(wmap)
            in_maps.append(m)
        res = run_bass_kernel_spmd(nc, in_maps, core_ids=list(range(n)))
        return np.stack([res.results[c]["out"] for c in range(n)], axis=0)

    nc_a0, nc_f0, nc_a1, nc_f1 = _get_programs()

    def launch(nc, per_core_x, wmap):
        in_maps = []
        for c in range(n):
            m = {"x": per_core_x[c]}
            m.update(wmap)
            m.update(consts)
            in_maps.append(m)
        res = run_bass_kernel_spmd(nc, in_maps, core_ids=list(range(n)))
        return [res.results[c]["out"] for c in range(n)]

    x = [_f32(inputs["L0_attn_x"][c]) for c in range(n)]
    w_a0 = {k: _f32(inputs["L0_attn_" + k]) for k in
            ("Wq_down", "Wk_down", "Wv_down", "Wq_up_c", "Wq_up_e",
             "Wk_up_c", "Wk_up_e", "Wv_up", "Wc")}
    x = launch(nc_a0, x, w_a0)
    w_f0 = {k: _f32(inputs["L0_ffn_" + k]) for k in
            ("Wr", "br", "W1", "b1", "g1", "be1", "W2", "b2", "g2", "be2")}
    x = launch(nc_f0, x, w_f0)
    w_a1 = {k: _f32(inputs["L1_attn_" + k2]) for k, k2 in
            (("Wq", "Wq"), ("Wqr", "Wqr"), ("Wkr", "Wkr"), ("Wkvd", "Wkvd"),
             ("Wku", "Wku"), ("Wvu", "Wvu"), ("Wo", "Wo"), ("lobo", "lobo_log"))}
    x = launch(nc_a1, x, w_a1)
    w_f1 = {k: _f32(inputs["L1_ffn_" + k]) for k in
            ("wg_red", "wg", "w1", "b1", "w2", "b2")}
    x = launch(nc_f1, x, w_f1)
    return np.stack(x, axis=0)


# version 08
# speedup vs baseline: 1.0113x; 1.0113x over previous
import numpy as np
import ml_dtypes
from contextlib import ExitStack
import concourse.bass as bass
import concourse.mybir as mybir
from concourse.bass_utils import run_bass_kernel_spmd

F32 = mybir.dt.float32
BF16 = mybir.dt.bfloat16
AF = mybir.ActivationFunctionType
ALU = mybir.AluOpType
AX = mybir.AxisListType
bfdt = ml_dtypes.bfloat16

T = 4096
D = 1024
NH = 16
DHEAD = 64
FF = 4096
NE = 8
NKC = D // 128
NT = T // 128


ENGINES = ("pe", "act", "dve", "pool", "sp")


class Op:
    __slots__ = ("eng", "fn", "waits", "inc", "done", "is_dma", "tag",
                 "need_inc", "emitted", "dma_key", "dma_count")

    def __init__(self, eng, fn, waits, is_dma=False, tag=""):
        self.eng = eng
        self.fn = fn
        self.waits = waits
        self.inc = None
        self.done = None
        self.is_dma = is_dma
        self.tag = tag
        self.need_inc = False
        self.emitted = False


class Prog:
    def __init__(self, nc):
        self.nc = nc
        self.q = {e: [] for e in ENGINES}
        self.sems = {}
        self.counts = {e: 0 for e in ENGINES}
        self.counts["dma_sp"] = 0
        self.counts["dma_gp"] = 0
        self._stack = ExitStack()
        for key in list(self.counts):
            self.sems[key] = self._stack.enter_context(nc.semaphore(f"s_{key}"))
        self.watermark = {e: {} for e in ENGINES}
        self.fence = {}
        self.consts = {}
        self.consts_res = {}
        self.host_in = {}

    def op(self, eng, fn, reads=(), writes=(), tag=""):
        waits = []
        for r in reads:
            if r is not None:
                waits += r._read_edges(eng)
        for w in writes:
            if w is not None:
                waits += w._write_edges(eng)
        o = Op(eng, fn, waits, tag=tag)
        self.q[eng].append(o)
        for r in reads:
            if r is not None:
                r._note_read(o)
        for w in writes:
            if w is not None:
                w._note_write(o)
        return o

    def dma(self, queue, out_ap, in_ap, reads=(), writes=(), tag="",
            accum=False, slow=False):
        eng = queue
        key = "dma_sp" if queue == "sp" else "dma_gp"
        waits = []
        for r in reads:
            if r is not None:
                waits += r._read_edges(eng)
        for w in writes:
            if w is not None:
                waits += w._write_edges(eng)

        def fn(e, _o=out_ap, _i=in_ap, _a=accum, _s=slow):
            kw = {}
            if _a:
                kw["accum_op"] = ALU.add
            if _s:
                kw["allow_slow_non_contiguous"] = True
            return e.dma_start(out=_o, in_=_i, **kw)

        o = Op(eng, fn, waits, is_dma=True, tag=tag)
        self.counts[key] += 16
        o.inc = (key, 16)
        o.done = None
        o.dma_key = key
        o.dma_count = self.counts[key]
        self.q[eng].append(o)
        for r in reads:
            if r is not None:
                r._note_read(o)
        for w in writes:
            if w is not None:
                w._note_write(o)
        return o

    def milestone(self, o):
        if o.is_dma:
            key = o.dma_key
            last = self.fence.get(key)
            if last is not None and last[0] >= o.dma_count:
                return last[1]
            total = self.counts[key]
            def _fence_fn(e, _k=key, _v=total):
                e.wait_ge(self.sems[_k], _v)
                return e.nop()
            f = Op(o.eng, _fence_fn, [], tag=f"fence_{key}")
            f.need_inc = True
            self.q[o.eng].append(f)
            self.fence[key] = (total, f)
            return f
        o.need_inc = True
        return o

    def wait_all_dma(self):
        for queue, key in (("sp", "dma_sp"), ("pool", "dma_gp")):
            v = self.counts[key]
            if v:
                self.op(queue, lambda e, _k=key, _v=v: e.wait_ge(self.sems[_k], _v),
                        tag="drain")

    def flush(self):
        nc = self.nc
        if not any(self.q.values()):
            return
        for eng_name in ENGINES:
            for o in self.q[eng_name]:
                if o.need_inc and o.done is None:
                    self.counts[eng_name] += 1
                    o.inc = (eng_name, 1)
                    o.done = (eng_name, self.counts[eng_name])
        with nc.Block() as blk:
            for eng_name in ENGINES:
                ops = self.q[eng_name]
                if not ops:
                    continue
                section = {"pe": blk.tensor, "act": blk.scalar, "dve": blk.vector,
                           "pool": blk.gpsimd, "sp": blk.sync}[eng_name]
                wm = self.watermark[eng_name]

                def body(e, _ops=ops, _wm=wm):
                    for o in _ops:
                        for w in o.waits:
                            if isinstance(w, Op):
                                if w.done is None:
                                    assert w.emitted, \
                                        f"unresolved same-flush edge to {w.tag}"
                                    continue
                                k, v = w.done
                            else:
                                k, v = w
                            if _wm.get(k, 0) < v:
                                e.wait_ge(self.sems[k], v)
                                _wm[k] = v
                        ins = o.fn(e)
                        if o.inc is not None:
                            assert ins is not None, f"op {o.tag} milestone but no ins"
                            ins.then_inc(self.sems[o.inc[0]], o.inc[1])
                        o.emitted = True
                section(body)
        self.q = {e: [] for e in ENGINES}
        self.fence = {}


class Res:
    __slots__ = ("prog", "name", "writers", "readers")

    def __init__(self, prog, name=""):
        self.prog = prog
        self.name = name
        self.writers = {}
        self.readers = {}

    def _read_edges(self, eng):
        ws = []
        for w in self.writers.values():
            if w.eng == "pe" and eng == "pe":
                continue
            ws.append(self.prog.milestone(w))
        return ws

    def _write_edges(self, eng):
        ws = []
        for w in self.writers.values():
            if w.eng != eng and not (w.eng == "pe" and eng == "pe"):
                ws.append(self.prog.milestone(w))
        for r in self.readers.values():
            if r.eng != eng and not (r.eng == "pe" and eng == "pe"):
                ws.append(self.prog.milestone(r))
        return ws

    def _note_read(self, op):
        self.readers[op.eng] = op

    def _note_write(self, op):
        self.writers[op.eng] = op
        self.readers = {}

    def set_writer(self, op):
        self.writers = {op.eng: op}
        self.readers = {}


def cdiv(a, b):
    return (a + b - 1) // b


_ALLOC_N = [0]


def alloc(ctx, nc, name, shape, dt=F32, psum=False):
    _ALLOC_N[0] += 1
    name = f"{name}_u{_ALLOC_N[0]}"
    if psum:
        return ctx.enter_context(nc.psum_tensor(name, shape, dt))
    return ctx.enter_context(nc.sbuf_tensor(name, shape, dt))


def rtile(p, ctx, nc, name, shape, dt=F32, psum=False, n=1):
    if n == 1:
        return alloc(ctx, nc, name, shape, dt, psum), Res(p, name)
    bufs = [alloc(ctx, nc, f"{name}{i}", shape, dt, psum) for i in range(n)]
    ress = [Res(p, f"{name}{i}") for i in range(n)]
    return bufs, ress


def mm_group(p, psum_ap, pres, mms, tag=""):
    n = len(mms)
    last = None
    for i, (lt, rh, reads) in enumerate(mms):
        wr = [pres] if i == 0 else []
        last = p.op("pe",
                    lambda e, _l=lt, _r=rh, _s=(i == 0), _e=(i == n - 1), _o=psum_ap:
                    e.matmul(_o, _l, _r, start=_s, stop=_e),
                    reads=reads, writes=wr, tag=tag)
    pres.set_writer(last)
    return last



def emit_ln_xnT(p, ctx, nc, x_hbm_ap, xnT, xnT_res, uid, tc0=0, ntc=NT, G=4):
    xt, xtr = rtile(p, ctx, nc, f"{uid}lx", [128, D], F32, n=G)
    sq = p.dump
    st, str_ = rtile(p, ctx, nc, f"{uid}lst", [128, 8], F32, n=G)
    xn, xnr = rtile(p, ctx, nc, f"{uid}lxn", [128, D], BF16, n=G)
    ptr, ptres = rtile(p, ctx, nc, f"{uid}lpt", [128, 128], BF16, psum=True, n=2)
    ident = p.consts["identbf"]
    identr = p.consts_res["identbf"]

    for g0 in range(0, ntc, G):
        js = [j for j in range(G) if g0 + j < ntc]
        for j in js:
            tc = tc0 + g0 + j
            p.dma("sp", xt[j][:, :], x_hbm_ap[tc * 128:(tc + 1) * 128, :],
                  writes=[xtr[j]], tag="lnload")
        for j in js:
            p.op("dve", lambda e, _j=j: e.reduce_sum(st[_j][:, 0:1], xt[_j][:, :],
                                                     axis=AX.X),
                 reads=[xtr[j]], writes=[str_[j]], tag="lnsum")
        for j in js:
            p.op("act", lambda e, _j=j: e.activation(sq[:, :], xt[_j][:, :], AF.Square,
                                                     accum_out=st[_j][:, 1:2]),
                 reads=[xtr[j]], writes=[str_[j]], tag="lnsq")
        for j in js:
            stt = st[j]
            p.op("dve", lambda e, _s=stt: e.tensor_scalar(_s[:, 2:3], _s[:, 0:1],
                                                          1.0 / D, None, ALU.mult),
                 reads=[str_[j]], writes=[str_[j]], tag="lnmean")
            p.op("dve", lambda e, _s=stt: e.tensor_tensor(_s[:, 3:4], _s[:, 2:3],
                                                          _s[:, 2:3], ALU.mult),
                 reads=[str_[j]], writes=[str_[j]], tag="lnmsq")
            p.op("dve", lambda e, _s=stt: e.tensor_scalar(_s[:, 4:5], _s[:, 1:2],
                                                          1.0 / D, _s[:, 3:4],
                                                          ALU.mult, ALU.subtract),
                 reads=[str_[j]], writes=[str_[j]], tag="lnvar")
        for j in js:
            stt = st[j]
            p.op("act", lambda e, _s=stt: e.activation(_s[:, 5:6], _s[:, 4:5], AF.Ln,
                                                       bias=p.consts["eps"][:, 0:1]),
                 reads=[str_[j], p.consts_res["eps"]], writes=[str_[j]], tag="lnln")
            p.op("act", lambda e, _s=stt: e.activation(_s[:, 6:7], _s[:, 5:6], AF.Exp,
                                                       scale=p.consts["nhalf"][:, 0:1]),
                 reads=[str_[j], p.consts_res["nhalf"]], writes=[str_[j]], tag="lnexp")
        for j in js:
            stt = st[j]
            p.op("dve", lambda e, _s=stt: e.scalar_tensor_tensor(_s[:, 7:8], _s[:, 2:3],
                                                                 -1.0, _s[:, 6:7],
                                                                 ALU.mult, ALU.mult),
                 reads=[str_[j]], writes=[str_[j]], tag="lnnmb")
        for j in js:
            stt = st[j]
            p.op("act", lambda e, _j=j, _s=stt: e.activation(xn[_j][:, :], xt[_j][:, :],
                                                             AF.Identity,
                                                             bias=_s[:, 7:8],
                                                             scale=_s[:, 6:7]),
                 reads=[xtr[j], str_[j]], writes=[xnr[j]], tag="lnapply")
        for j in js:
            tl = g0 + j
            for k in range(NKC):
                pb = (j * NKC + k) % 2
                p.op("pe", lambda e, _j=j, _k=k, _p=pb:
                     e.transpose(ptr[_p][:, :], xn[_j][:, _k * 128:(_k + 1) * 128],
                                 ident[:, :]),
                     reads=[xnr[j], identr], writes=[ptres[pb]], tag="lntr")
                p.op("dve", lambda e, _k=k, _tl=tl, _p=pb:
                     e.tensor_copy(xnT[:, _k, _tl * 128:(_tl + 1) * 128], ptr[_p][:, :]),
                     reads=[ptres[pb]], writes=[xnT_res], tag="lnev")



TH = T // 2
NFB_G = FF // 128
NTH = TH // 128


def emit_ffn_l1(p, ctx, nc, aps, x_in, x_out, uid="f1", n_experts=NE, n_tg=None):
    w1, w1_r = rtile(p, ctx, nc, f"{uid}_w1", [128, NKC, FF], BF16)
    w2, w2_r = rtile(p, ctx, nc, f"{uid}_w2", [128, NFB_G, D], BF16)
    b1c, b1c_r = rtile(p, ctx, nc, f"{uid}_b1c", [128, NFB_G], F32)
    b1row, b1row_r = rtile(p, ctx, nc, f"{uid}_b1r", [32, 128], F32)

    def load_expert(e_idx):
        p.dma("sp", w1[:, :, :],
              aps["w1m"][e_idx].rearrange("(k p) f -> p k f", p=128),
              writes=[w1_r], tag=f"w1e{e_idx}")
        p.dma("sp", w2[:, :, :],
              aps["w2m"][e_idx].rearrange("(k p) f -> p k f", p=128),
              writes=[w2_r], tag=f"w2e{e_idx}")
        p.dma("pool", b1row[:, :],
              aps["b1"][e_idx].rearrange("(a b) -> a b", b=128),
              writes=[b1row_r], tag=f"b1e{e_idx}")
        p.op("pe", lambda e: e.transpose(p.pb1f1[:, 0:32], b1row[:, :],
                                         p.consts["identf"][0:32, 0:32]),
             reads=[b1row_r, p.consts_res["identf"]], writes=[p.pb1f1_r], tag="b1tr")
        p.op("dve", lambda e: e.tensor_copy(b1c[:, :], p.pb1f1[:, 0:32]),
             reads=[p.pb1f1_r], writes=[b1c_r], tag="b1ev")

    xnT = alloc(ctx, nc, f"{uid}_xnT", [128, NKC, TH], BF16)
    xnT_res = Res(p, "xnT")
    gates = alloc(ctx, nc, f"{uid}_gates", [128, NT, NE], F32)
    gates_r = Res(p, "gates")
    ident = p.consts["identbf"]
    identr = p.consts_res["identbf"]

    for half in range(2):
        hb0 = half * TH
        htc0 = half * NTH
        with ExitStack() as c1:
            p.pb1f1, p.pb1f1_r = rtile(p, c1, nc, f"{uid}_pb1", [128, 32], F32,
                                       psum=True)
            emit_ln_xnT(p, c1, nc, x_in, xnT, xnT_res, uid=uid, tc0=htc0, ntc=NTH,
                        G=3)
            load_expert(0)
            wgr, wgr_r = rtile(p, c1, nc, f"{uid}_wgr", [16, D], BF16)
            p.dma("pool", wgr[:, :], aps["wg_red"], writes=[wgr_r])
            wg, wg_r = rtile(p, c1, nc, f"{uid}_wg", [8, 16], F32)
            p.dma("pool", wg[:, :], aps["wg"], writes=[wg_r])
            gsc, gsc_r = rtile(p, c1, nc, f"{uid}_gsc", [8, 20], F32)
            wgu, wgu_r = rtile(p, c1, nc, f"{uid}_wgu", [8, 16], BF16)
            p.op("dve", lambda e: e.tensor_tensor(gsc[:, 0:16], wg[:, :], wg[:, :], ALU.mult),
                 reads=[wg_r], writes=[gsc_r])
            p.op("dve", lambda e: e.reduce_sum(gsc[:, 16:17], gsc[:, 0:16], axis=AX.X),
                 reads=[gsc_r], writes=[gsc_r])
            p.op("act", lambda e: e.activation(gsc[:, 17:18], gsc[:, 16:17], AF.Ln),
                 reads=[gsc_r], writes=[gsc_r])
            p.op("act", lambda e: e.activation(gsc[:, 17:18], gsc[:, 17:18], AF.Exp,
                                               scale=p.consts["phalf"][0:8, 0:1]),
                 reads=[gsc_r, p.consts_res["phalf"]], writes=[gsc_r])
            p.op("dve", lambda e: e.tensor_scalar_max(gsc[:, 18:19], gsc[:, 17:18], 1e-4),
                 reads=[gsc_r], writes=[gsc_r])
            p.op("dve", lambda e: e.reciprocal(gsc[:, 19:20], gsc[:, 18:19]),
                 reads=[gsc_r], writes=[gsc_r])
            p.op("dve", lambda e: e.tensor_scalar(wgu[:, :], wg[:, :], gsc[:, 19:20], None, ALU.mult),
                 reads=[wg_r, gsc_r], writes=[wgu_r])
            pgt, pgt_r = rtile(p, c1, nc, f"{uid}_pgt", [128, 128], BF16, psum=True)
            wguT, wguT_r = rtile(p, c1, nc, f"{uid}_wguT", [16, 8], BF16)
            p.op("pe", lambda e: e.transpose(pgt[0:16, 0:8], wgu[:, :], ident[0:8, 0:8]),
                 reads=[wgu_r, identr], writes=[pgt_r])
            p.op("dve", lambda e: e.tensor_copy(wguT[:, :], pgt[0:16, 0:8]),
                 reads=[pgt_r], writes=[wguT_r])
            pgc, pgc_r = rtile(p, c1, nc, f"{uid}_pgc", [8, 512], F32, psum=True)
            wgcT, wgcT_r = rtile(p, c1, nc, f"{uid}_wgcT", [8, D], BF16)
            for h2 in range(2):
                mm_group(p, pgc[:, :], pgc_r,
                         [(wguT[:, :], wgr[:, h2 * 512:(h2 + 1) * 512], [wguT_r, wgr_r])])
                p.op("dve", lambda e, _h=h2: e.tensor_copy(wgcT[:, _h * 512:(_h + 1) * 512],
                                                           pgc[:, :]),
                     reads=[pgc_r], writes=[wgcT_r])
            wgc, wgc_r = rtile(p, c1, nc, f"{uid}_wgc", [128, NKC, 8], BF16)
            for k in range(NKC):
                p.op("pe", lambda e, _k=k: e.transpose(pgt[0:128, 0:8],
                                                       wgcT[:, _k * 128:(_k + 1) * 128],
                                                       ident[0:8, 0:8]),
                     reads=[wgcT_r, identr], writes=[pgt_r])
                p.op("dve", lambda e, _k=k: e.tensor_copy(wgc[:, _k, :], pgt[0:128, 0:8]),
                     reads=[pgt_r], writes=[wgc_r])

            gatesT, gatesT_r = rtile(p, c1, nc, f"{uid}_gT", [8, TH], BF16)
            b2sb, b2sb_r = rtile(p, c1, nc, f"{uid}_b2sb", [8, D], BF16)
            p.dma("pool", b2sb[:, :], aps["b2"], writes=[b2sb_r])
            plog, plog_r = rtile(p, c1, nc, f"{uid}_plog", [128, 8], F32, psum=True, n=2)
            gl, gl_r = rtile(p, c1, nc, f"{uid}_gl", [128, 8], F32)
            gsc2, gsc2_r = rtile(p, c1, nc, f"{uid}_gsc2", [128, 4], F32, n=2)
            gexp = alloc(c1, nc, f"{uid}_gexp", [128, 8], F32)
            gbf, gbf_r = rtile(p, c1, nc, f"{uid}_gbf", [128, 8], BF16, n=2)
            pacc0, pacc0_r = rtile(p, c1, nc, f"{uid}_pacc", [128, 512], F32, psum=True)
            pacc, pacc_r = [pacc0, pacc0], [pacc0_r, pacc0_r]
            xacc0, xacc0_r = rtile(p, c1, nc, f"{uid}_xacc", [128, D], F32)
            xacc, xacc_r = [xacc0, xacc0], [xacc0_r, xacc0_r]
            xtl0, xtl0_r = rtile(p, c1, nc, f"{uid}_xtl", [128, D], F32)
            xtl, xtl_r = [xtl0, xtl0], [xtl0_r, xtl0_r]

            for tl in range(NTH):
                tc = htc0 + tl
                pb = tl % 2
                gts = slice(tc * 128, (tc + 1) * 128)
                lts = slice(tl * 128, (tl + 1) * 128)
                g2c, g2r = gsc2[pb], gsc2_r[pb]
                mm_group(p, plog[pb][:, :], plog_r[pb],
                         [(xnT[:, k, lts], wgc[:, k, :], [xnT_res, wgc_r])
                          for k in range(NKC)], tag="glog")
                p.op("dve", lambda e, _p=pb, _g=g2c: e.reduce_max(_g[:, 0:1], plog[_p][:, :],
                                                                  axis=AX.X, negate=True),
                     reads=[plog_r[pb]], writes=[g2r], tag="gmax")
                p.op("act", lambda e, _p=pb, _g=g2c: e.activation(gexp[:, :], plog[_p][:, :],
                                                                  AF.Exp, bias=_g[:, 0:1],
                                                                  accum_out=_g[:, 1:2]),
                     reads=[plog_r[pb], g2r], writes=[g2r], tag="gexp")
                p.op("dve", lambda e, _g=g2c: e.reciprocal(_g[:, 2:3], _g[:, 1:2]),
                     reads=[g2r], writes=[g2r], tag="gscore")
                p.op("dve", lambda e, _g=g2c: e.tensor_scalar(_g[:, 3:4], _g[:, 0:1], -1.0,
                                                              None, ALU.mult),
                     reads=[g2r], writes=[g2r], tag="gneg")
                p.op("dve", lambda e, _p=pb, _g=g2c: e.tensor_scalar(gl[:, :], plog[_p][:, :],
                                                                     _g[:, 3:4], None, ALU.is_ge),
                     reads=[plog_r[pb], g2r], writes=[gl_r], tag="gmask")
                p.op("dve", lambda e, _tc=tc, _g=g2c: e.tensor_scalar(gates[:, _tc, :], gl[:, :],
                                                                      _g[:, 2:3], None, ALU.mult),
                     reads=[gl_r, g2r], writes=[gates_r], tag="gfin")
                p.op("dve", lambda e, _tc=tc, _p=pb: e.tensor_copy(gbf[_p][:, :], gates[:, _tc, :]),
                     reads=[gates_r], writes=[gbf_r[pb]], tag="gcast")
                p.op("pe", lambda e, _p=pb: e.transpose(pgt[0:8, 0:128], gbf[_p][:, :], ident[:, :]),
                     reads=[gbf_r[pb], identr], writes=[pgt_r], tag="gtr")
                p.op("dve", lambda e, _tl=tl: e.tensor_copy(gatesT[:, _tl * 128:(_tl + 1) * 128],
                                                            pgt[0:8, 0:128]),
                     reads=[pgt_r], writes=[gatesT_r], tag="gtev")
                p.dma("sp", xtl[pb][:, :], x_in[gts, :], writes=[xtl_r[pb]], tag="gxload")
                for dh in range(2):
                    mm_group(p, pacc[pb][:, :], pacc_r[pb],
                             [(gatesT[:, lts], b2sb[:, dh * 512:(dh + 1) * 512],
                               [gatesT_r, b2sb_r])], tag="gaccmm")
                    p.op("dve", lambda e, _p=pb, _d=dh: e.tensor_add(
                             xacc[_p][:, _d * 512:(_d + 1) * 512],
                             xtl[_p][:, _d * 512:(_d + 1) * 512], pacc[_p][:, :]),
                         reads=[pacc_r[pb], xtl_r[pb]], writes=[xacc_r[pb]], tag="gaccadd")
                p.dma("pool", x_out[gts, :], xacc[pb][:, :], reads=[xacc_r[pb]], tag="gaccst")
            p.wait_all_dma()
            p.flush()

        TG = 256
        NTG = TH // TG
        NFB = FF // 128
        with ExitStack() as c2:
            pb1, pb1_r = rtile(p, c2, nc, f"{uid}_pb1b", [128, 32], F32, psum=True)
            p.pb1f1, p.pb1f1_r = pb1, pb1_r
            ph, ph_r = rtile(p, c2, nc, f"{uid}_ph", [128, TG], F32, psum=True, n=2)
            rt, rt_r = rtile(p, c2, nc, f"{uid}_rt", [128, TG], BF16, n=2)
            pY, pY_r = rtile(p, c2, nc, f"{uid}_pY", [128, 512], F32, psum=True, n=4)
            ytl, ytl_r = rtile(p, c2, nc, f"{uid}_ytl", [128, 512], F32, n=4)

            def emit_s2_l1(fb):
                hbi = fb % 2
                for tc2 in range(2):
                    for dh in range(2):
                        iY = tc2 * 2 + dh
                        last = p.op("pe",
                                    lambda e, _h=hbi, _f=fb, _t=tc2, _d=dh, _i=iY:
                                    e.matmul(pY[_i][:, :],
                                             rt[_h][:, _t * 128:(_t + 1) * 128],
                                             w2[:, _f, _d * 512:(_d + 1) * 512],
                                             start=(_f == 0), stop=(_f == NFB - 1)),
                                    reads=[rt_r[hbi], w2_r],
                                    writes=[pY_r[iY]] if fb == 0 else [], tag="ffs2")
                        if fb == NFB - 1:
                            pY_r[iY].set_writer(last)

            for e_idx in range(n_experts):
                if e_idx > 0:
                    load_expert(e_idx)
                for tg in range(n_tg if n_tg is not None else NTG):
                    t0 = tg * TG
                    pend = None
                    for fb in range(NFB):
                        hbi = fb % 2
                        mm_group(p, ph[hbi][:, :], ph_r[hbi],
                                 [(w1[:, k, fb * 128:(fb + 1) * 128],
                                   xnT[:, k, t0:t0 + TG], [w1_r, xnT_res])
                                  for k in range(NKC)], tag="ffs1")
                        p.op("act", lambda e, _h=hbi, _f=fb: e.activation(
                                 rt[_h][:, :], ph[_h][:, :], AF.Gelu,
                                 bias=b1c[:, _f:_f + 1]),
                             reads=[ph_r[hbi], b1c_r], writes=[rt_r[hbi]], tag="ffgelu")
                        if pend is not None:
                            emit_s2_l1(pend)
                        pend = fb
                    emit_s2_l1(pend)
                    for tc2 in range(2):
                        tci = htc0 + tg * 2 + tc2
                        for dh in range(2):
                            iY = tc2 * 2 + dh
                            p.op("dve", lambda e, _i=iY, _tc=tci, _e=e_idx: e.tensor_scalar(
                                     ytl[_i][:, :], pY[_i][:, :], gates[:, _tc, _e:_e + 1],
                                     None, ALU.mult),
                                 reads=[pY_r[iY], gates_r], writes=[ytl_r[iY]], tag="ffev")
                            p.dma("pool",
                                  x_out[tci * 128:(tci + 1) * 128, dh * 512:(dh + 1) * 512],
                                  ytl[iY][:, :], reads=[ytl_r[iY]], tag="ffacc", accum=True)
            p.wait_all_dma()
            p.flush()



def emit_ffn_l0(p, ctx, nc, aps, x_in, x_out, uid="f0", n_experts=NE, n_tg=None):
    xnT = alloc(ctx, nc, f"{uid}_xnT", [128, NKC, TH], BF16)
    xnT_res = Res(p, "xnT")
    gates = alloc(ctx, nc, f"{uid}_gates", [128, NT, NE], F32)
    gates_r = Res(p, "gates")
    ident = p.consts["identbf"]
    identr = p.consts_res["identbf"]
    identf = p.consts["identf"]
    identfr = p.consts_res["identf"]
    epsap = p.consts["eps"]
    epsr = p.consts_res["eps"]

    w1, w1_r = rtile(p, ctx, nc, f"{uid}_w1", [128, NKC, FF], BF16)
    w2, w2_r = rtile(p, ctx, nc, f"{uid}_w2", [128, NFB_G, D], BF16)
    vrow, vrow_r = rtile(p, ctx, nc, f"{uid}_vr", [32, 3 * 128], F32)
    g2row, g2row_r = rtile(p, ctx, nc, f"{uid}_g2row", [1, D], BF16)
    crow_bf, crow_bf_r = rtile(p, ctx, nc, f"{uid}_crowb", [1, D], BF16)

    def load_weights(e_idx):
        p.dma("sp", w1[:, :, :],
              aps["w1m"][e_idx].rearrange("(k p) f -> p k f", p=128),
              writes=[w1_r], tag=f"w1e{e_idx}")
        p.dma("sp", w2[:, :, :],
              aps["w2m"][e_idx].rearrange("(k p) f -> p k f", p=128),
              writes=[w2_r], tag=f"w2e{e_idx}")
        for j, nm in enumerate(("b1", "g1", "be1")):
            p.dma("pool", vrow[:, j * 128:(j + 1) * 128],
                  aps[nm][e_idx].rearrange("(a b) -> a b", b=128),
                  writes=[vrow_r], tag=f"{nm}e{e_idx}")
        p.dma("pool", g2row[:, :], aps["g2"][e_idx].rearrange("(o d) -> o d", o=1),
              writes=[g2row_r], tag=f"g2e{e_idx}")
        p.dma("pool", crow_bf[:, :], aps["b2"][e_idx].rearrange("(o d) -> o d", o=1),
              writes=[crow_bf_r], tag=f"b2e{e_idx}")

    for half in range(2):
        hb0 = half * TH
        htc0 = half * NTH
        with ExitStack() as c1:
            emit_ln_xnT(p, c1, nc, x_in, xnT, xnT_res, uid=uid, tc0=htc0, ntc=NTH,
                        G=3)
            load_weights(0)
            wr, wr_r = rtile(p, c1, nc, f"{uid}_wr", [128, NKC, 8], BF16)
            p.dma("pool", wr[:, :, :], aps["Wr"].rearrange("(k p) e -> p k e", p=128),
                  writes=[wr_r])
            brrow, brrow_r = rtile(p, c1, nc, f"{uid}_br", [1, 8], BF16)
            p.dma("pool", brrow[:, :], aps["br"].rearrange("(o e) -> o e", o=1),
                  writes=[brrow_r])
            onesrow, onesrow_r = rtile(p, c1, nc, f"{uid}_o1r", [1, 128], BF16)
            p.op("dve", lambda e: e.memset(onesrow[:, :], 1.0), writes=[onesrow_r])
            be2sb, be2sb_r = rtile(p, c1, nc, f"{uid}_be2", [8, D], BF16)
            p.dma("pool", be2sb[:, :], aps["be2"], writes=[be2sb_r])
            gatesT, gatesT_r = rtile(p, c1, nc, f"{uid}_gT", [8, TH], BF16)
            pgt, pgt_r = rtile(p, c1, nc, f"{uid}_pgt", [128, 128], BF16, psum=True)
            plog, plog_r = rtile(p, c1, nc, f"{uid}_plog", [128, 8], F32, psum=True, n=2)
            gp_, gp_r = rtile(p, c1, nc, f"{uid}_gp", [128, 8], F32, n=2)
            gq, gq_r = rtile(p, c1, nc, f"{uid}_gq", [128, 8], F32, n=2)
            gsc2, gsc2_r = rtile(p, c1, nc, f"{uid}_gsc2", [128, 8], F32, n=2)
            gexp = alloc(c1, nc, f"{uid}_gexp", [128, 8], F32)
            gbf, gbf_r = rtile(p, c1, nc, f"{uid}_gbf", [128, 8], BF16, n=2)
            pacc, pacc_r = rtile(p, c1, nc, f"{uid}_pacc", [128, 512], F32, psum=True, n=2)
            xacc0, xacc0_r = rtile(p, c1, nc, f"{uid}_xacc", [128, D], F32)
            xacc, xacc_r = [xacc0, xacc0], [xacc0_r, xacc0_r]
            xtl0, xtl0_r = rtile(p, c1, nc, f"{uid}_xtl", [128, D], F32)
            xtl, xtl_r = [xtl0, xtl0], [xtl0_r, xtl0_r]

            for tl in range(NTH):
                tc = htc0 + tl
                pb = tl % 2
                gts = slice(tc * 128, (tc + 1) * 128)
                lts = slice(tl * 128, (tl + 1) * 128)
                g2c, g2r = gsc2[pb], gsc2_r[pb]
                mms = [(xnT[:, k, lts], wr[:, k, :], [xnT_res, wr_r]) for k in range(NKC)]
                mms.append((onesrow[:, :], brrow[:, :], [onesrow_r, brrow_r]))
                mm_group(p, plog[pb][:, :], plog_r[pb], mms, tag="glog")
                p.op("dve", lambda e, _p=pb, _g=g2c: e.reduce_max(_g[:, 0:1], plog[_p][:, :],
                                                                  axis=AX.X, negate=True),
                     reads=[plog_r[pb]], writes=[g2r], tag="gmax")
                p.op("act", lambda e, _p=pb, _g=g2c: e.activation(gexp[:, :], plog[_p][:, :],
                                                                  AF.Exp, bias=_g[:, 0:1],
                                                                  accum_out=_g[:, 1:2]),
                     reads=[plog_r[pb], g2r], writes=[g2r], tag="gexp")
                p.op("dve", lambda e, _g=g2c: e.reciprocal(_g[:, 2:3], _g[:, 1:2]),
                     reads=[g2r], writes=[g2r], tag="grec")
                p.op("dve", lambda e, _p=pb, _g=g2c: e.tensor_scalar(gp_[_p][:, :], gexp[:, :],
                                                                     _g[:, 2:3], None, ALU.mult),
                     reads=[g2r], writes=[gp_r[pb]], tag="gprob")
                p.op("dve", lambda e, _p=pb, _g=g2c: e.reduce_max(_g[:, 3:4], gp_[_p][:, :],
                                                                  axis=AX.X),
                     reads=[gp_r[pb]], writes=[g2r], tag="gm1")
                p.op("dve", lambda e, _p=pb, _g=g2c: e.tensor_scalar(gq[_p][:, :], gp_[_p][:, :],
                                                                     _g[:, 3:4], None, ALU.is_ge),
                     reads=[gp_r[pb], g2r], writes=[gq_r[pb]], tag="geq")
                p.op("dve", lambda e, _p=pb: e.scalar_tensor_tensor(gq[_p][:, :], gq[_p][:, :],
                                                                    -1e9, gp_[_p][:, :],
                                                                    ALU.mult, ALU.add),
                     reads=[gq_r[pb], gp_r[pb]], writes=[gq_r[pb]], tag="gp2")
                p.op("dve", lambda e, _p=pb, _g=g2c: e.reduce_max(_g[:, 4:5], gq[_p][:, :],
                                                                  axis=AX.X),
                     reads=[gq_r[pb]], writes=[g2r], tag="gm2")
                p.op("dve", lambda e, _p=pb, _g=g2c: e.tensor_scalar(gq[_p][:, :], gp_[_p][:, :],
                                                                     _g[:, 4:5], None, ALU.is_ge),
                     reads=[gp_r[pb], g2r], writes=[gq_r[pb]], tag="gkeep")
                p.op("dve", lambda e, _g=g2c: e.tensor_tensor(_g[:, 5:6], _g[:, 3:4],
                                                              _g[:, 4:5], ALU.add),
                     reads=[g2r], writes=[g2r], tag="gs12")
                p.op("dve", lambda e, _g=g2c: e.reciprocal(_g[:, 6:7], _g[:, 5:6]),
                     reads=[g2r], writes=[g2r], tag="gr12")
                p.op("dve", lambda e, _p=pb: e.tensor_tensor(gq[_p][:, :], gq[_p][:, :],
                                                             gp_[_p][:, :], ALU.mult),
                     reads=[gq_r[pb], gp_r[pb]], writes=[gq_r[pb]], tag="gmulp")
                p.op("dve", lambda e, _p=pb, _tc=tc, _g=g2c: e.tensor_scalar(
                         gates[:, _tc, :], gq[_p][:, :], _g[:, 6:7], None, ALU.mult),
                     reads=[gq_r[pb], g2r], writes=[gates_r], tag="gfin")
                p.op("dve", lambda e, _tc=tc, _p=pb: e.tensor_copy(gbf[_p][:, :], gates[:, _tc, :]),
                     reads=[gates_r], writes=[gbf_r[pb]], tag="gcast")
                p.op("pe", lambda e, _p=pb: e.transpose(pgt[0:8, 0:128], gbf[_p][:, :], ident[:, :]),
                     reads=[gbf_r[pb], identr], writes=[pgt_r], tag="gtr")
                p.op("dve", lambda e, _tl=tl: e.tensor_copy(gatesT[:, _tl * 128:(_tl + 1) * 128],
                                                            pgt[0:8, 0:128]),
                     reads=[pgt_r], writes=[gatesT_r], tag="gtev")
                p.dma("sp", xtl[pb][:, :], x_in[gts, :], writes=[xtl_r[pb]], tag="gxload")
                for dh in range(2):
                    mm_group(p, pacc[pb][:, :], pacc_r[pb],
                             [(gatesT[:, lts], be2sb[:, dh * 512:(dh + 1) * 512],
                               [gatesT_r, be2sb_r])], tag="gaccmm")
                    p.op("dve", lambda e, _p=pb, _d=dh: e.tensor_add(
                             xacc[_p][:, _d * 512:(_d + 1) * 512],
                             xtl[_p][:, _d * 512:(_d + 1) * 512], pacc[_p][:, :]),
                         reads=[pacc_r[pb], xtl_r[pb]], writes=[xacc_r[pb]], tag="gaccadd")
                p.dma("pool", x_out[gts, :], xacc[pb][:, :], reads=[xacc_r[pb]], tag="gaccst")
            p.wait_all_dma()
            p.flush()

        TG = 256
        NTG = TH // TG
        NFB = FF // 128
        with ExitStack() as c2:
            pvc, pvc_r = rtile(p, c2, nc, f"{uid}_pvc", [128, 32], F32, psum=True)
            pstt, pstt_r = pvc, pvc_r
            b1c, b1c_r = rtile(p, c2, nc, f"{uid}_b1c", [128, NFB], F32)
            g1c, g1c_r = rtile(p, c2, nc, f"{uid}_g1c", [128, NFB], F32)
            be1c, be1c_r = rtile(p, c2, nc, f"{uid}_be1c", [128, NFB], BF16)
            negc1, negc1_r = rtile(p, c2, nc, f"{uid}_negc1", [1, D], BF16)
            c2b2b, c2b2b_r = rtile(p, c2, nc, f"{uid}_c2b2b", [128, D], BF16)
            g2b, g2b_r = rtile(p, c2, nc, f"{uid}_g2b", [128, D], BF16)
            ones128, ones128_r = rtile(p, c2, nc, f"{uid}_o128", [128, 1], BF16)
            p.op("dve", lambda e: e.memset(ones128[:, :], 1.0), writes=[ones128_r])
            onescol, onescol_r = rtile(p, c2, nc, f"{uid}_oc1", [1, 128], BF16)
            p.op("dve", lambda e: e.memset(onescol[:, :], 1.0), writes=[onescol_r])
            prow, prow_r = rtile(p, c2, nc, f"{uid}_prow", [1, 512], F32, psum=True)
            ph, ph_r = rtile(p, c2, nc, f"{uid}_ph", [128, TG], F32, psum=True, n=2)
            rt, rt_r = rtile(p, c2, nc, f"{uid}_rt", [128, 2 * TG], BF16, n=2)
            pstat, pstat_r = prow, prow_r
            srow_f, srow_f_r = rtile(p, c2, nc, f"{uid}_srf", [1, 512], F32, n=2)
            stf, stf_r = rtile(p, c2, nc, f"{uid}_stf", [128, 16], F32, n=2)
            ybraw, ybraw_r = rtile(p, c2, nc, f"{uid}_ybr", [128, D], F32, n=4)
            nc1b, nc1b_r = rtile(p, c2, nc, f"{uid}_nc1b", [128, D], BF16)
            pY, pY_r = rtile(p, c2, nc, f"{uid}_pY", [128, 512], F32, psum=True, n=4)
            ybuf, ybuf_r = rtile(p, c2, nc, f"{uid}_yb", [128, D], F32)
            yst, yst_r = rtile(p, c2, nc, f"{uid}_yst", [128, 10], F32, n=2)
            ysq = p.dump

            def prep_expert(e_idx):
                for j, (dst, dstr) in enumerate(((b1c, b1c_r), (g1c, g1c_r), (be1c, be1c_r))):
                    p.op("pe", lambda e, _j=j: e.transpose(pvc[:, 0:32],
                                                           vrow[:, _j * 128:(_j + 1) * 128],
                                                           identf[0:32, 0:32]),
                         reads=[vrow_r, identfr], writes=[pvc_r], tag="vtr")
                    p.op("dve", lambda e, _d=dst: e.tensor_copy(_d[:, :], pvc[:, 0:32]),
                         reads=[pvc_r], writes=[dstr], tag="vev")
                for dh in range(2):
                    mm_group(p, prow[:, :], prow_r,
                             [(be1c[:, c:c + 1], w2[:, c, dh * 512:(dh + 1) * 512],
                               [be1c_r, w2_r]) for c in range(NFB)], tag="c2mm")
                    p.op("dve", lambda e, _d=dh: e.tensor_add(
                             crow_bf[:, _d * 512:(_d + 1) * 512],
                             crow_bf[:, _d * 512:(_d + 1) * 512], prow[:, :]),
                         reads=[prow_r, crow_bf_r], writes=[crow_bf_r], tag="c2add")
                for c in range(NFB):
                    p.op("dve", lambda e, _c=c: e.tensor_scalar(w2[:, _c, :], w2[:, _c, :],
                                                                g1c[:, _c:_c + 1], None, ALU.mult),
                         reads=[w2_r, g1c_r], writes=[w2_r], tag="w2scale")
                for dh in range(2):
                    mm_group(p, prow[:, :], prow_r,
                             [(ones128[:, :], w2[:, c, dh * 512:(dh + 1) * 512],
                               [ones128_r, w2_r]) for c in range(NFB)], tag="c1mm")
                    p.op("dve", lambda e, _d=dh: e.tensor_scalar(
                             negc1[:, _d * 512:(_d + 1) * 512], prow[:, :], -1.0 / FF, None,
                             ALU.mult),
                         reads=[prow_r], writes=[negc1_r], tag="c1ev")
                for q in range(4):
                    hb2 = q % 2
                    mm_group(p, ph[hb2][:, :], ph_r[hb2],
                             [(onescol[:, :], crow_bf[:, q * 256:(q + 1) * 256],
                               [onescol_r, crow_bf_r])], tag="bc1")
                    p.op("dve", lambda e, _q=q, _h=hb2: e.tensor_copy(
                             c2b2b[:, _q * 256:(_q + 1) * 256], ph[_h][:, :]),
                         reads=[ph_r[hb2]], writes=[c2b2b_r], tag="bc1ev")
                    mm_group(p, ph[hb2][:, :], ph_r[hb2],
                             [(onescol[:, :], g2row[:, q * 256:(q + 1) * 256],
                               [onescol_r, g2row_r])], tag="bc2")
                    p.op("dve", lambda e, _q=q, _h=hb2: e.tensor_copy(
                             g2b[:, _q * 256:(_q + 1) * 256], ph[_h][:, :]),
                         reads=[ph_r[hb2]], writes=[g2b_r], tag="bc2ev")
                    mm_group(p, ph[hb2][:, :], ph_r[hb2],
                             [(onescol[:, :], negc1[:, q * 256:(q + 1) * 256],
                               [onescol_r, negc1_r])], tag="bc3")
                    p.op("dve", lambda e, _q=q, _h=hb2: e.tensor_copy(
                             nc1b[:, _q * 256:(_q + 1) * 256], ph[_h][:, :]),
                         reads=[ph_r[hb2]], writes=[nc1b_r], tag="bc3ev")

            def emit_s2_l0(fb):
                hbi = fb % 2
                o = p.op("pe", lambda e, _h=hbi, _f=fb: e.matmul(
                             pstat[:, :], ones128[:, :], rt[_h][:, :],
                             start=(_f == 0), stop=(_f == NFB - 1)),
                         reads=[ones128_r, rt_r[hbi]],
                         writes=[pstat_r] if fb == 0 else [], tag="ffstat")
                if fb == NFB - 1:
                    pstat_r.set_writer(o)
                for tc2 in range(2):
                    for dh in range(2):
                        iY = tc2 * 2 + dh
                        o = p.op("pe",
                                 lambda e, _h=hbi, _f=fb, _t=tc2, _d=dh, _i=iY:
                                 e.matmul(pY[_i][:, :],
                                          rt[_h][:, _t * 128:(_t + 1) * 128],
                                          w2[:, _f, _d * 512:(_d + 1) * 512],
                                          start=(_f == 0), stop=(_f == NFB - 1)),
                                 reads=[rt_r[hbi], w2_r],
                                 writes=[pY_r[iY]] if fb == 0 else [], tag="ffs2")
                        if fb == NFB - 1:
                            pY_r[iY].set_writer(o)

            def make_tail(tg, e_idx, par):
                sr, srr = srow_f[par], srow_f_r[par]
                st_, str2 = stf[par], stf_r[par]
                pieces = []

                def pc_sttr():
                    _lasttr = None
                    for tc2 in range(2):
                        for j, off in ((0, tc2 * 128), (1, 256 + tc2 * 128)):
                            col = tc2 * 2 + j
                            _lasttr = p.op("pe", lambda e, _c=col, _o=off: e.transpose(
                                     pstt[:, _c:_c + 1], sr[0:1, _o:_o + 128],
                                     identf[0:1, 0:1]),
                                 reads=[srr, identfr],
                                 writes=[pstt_r] if col == 0 else [], tag="sttr")
                    pstt_r.set_writer(_lasttr)
                    p.op("dve", lambda e: e.tensor_copy(st_[:, 0:4], pstt[:, 0:4]),
                         reads=[pstt_r], writes=[str2], tag="stev")
                pieces.append(pc_sttr)

                def pc_stats1():
                    p.op("dve", lambda e: e.tensor_scalar(st_[:, 4:6], st_[:, 0:4:2],
                                                          1.0 / FF, None, ALU.mult),
                         reads=[str2], writes=[str2], tag="stmu")
                    p.op("dve", lambda e: e.tensor_tensor(st_[:, 6:8], st_[:, 4:6],
                                                          st_[:, 4:6], ALU.mult),
                         reads=[str2], writes=[str2], tag="stmusq")
                    p.op("dve", lambda e: e.tensor_scalar(st_[:, 8:10], st_[:, 1:4:2],
                                                          1.0 / FF, None, ALU.mult),
                         reads=[str2], writes=[str2], tag="stvar1")
                    p.op("dve", lambda e: e.tensor_tensor(st_[:, 8:10], st_[:, 8:10],
                                                          st_[:, 6:8], ALU.subtract),
                         reads=[str2], writes=[str2], tag="stvar")
                pieces.append(pc_stats1)

                def pc_stats2():
                    p.op("act", lambda e: e.activation(st_[:, 10:12], st_[:, 8:10], AF.Ln,
                                                       bias=epsap[:, 0:1]),
                         reads=[str2, epsr], writes=[str2], tag="stln")
                    p.op("act", lambda e: e.activation(st_[:, 12:14], st_[:, 10:12], AF.Exp,
                                                       scale=p.consts["nhalf"][:, 0:1]),
                         reads=[str2, p.consts_res["nhalf"]], writes=[str2], tag="stexp")
                pieces.append(pc_stats2)

                def pc_stats3():
                    p.op("dve", lambda e: e.tensor_tensor(st_[:, 14:16], st_[:, 0:4:2],
                                                          st_[:, 12:14], ALU.mult),
                         reads=[str2], writes=[str2], tag="stcoef")
                pieces.append(pc_stats3)

                for tc2 in range(2):
                    tci = htc0 + tg * 2 + tc2
                    yb_i = par * 2 + tc2
                    ys, ysr = yst[tc2], yst_r[tc2]

                    def pc_y1(tc2=tc2, yb_i=yb_i):
                        p.op("dve", lambda e, _t=tc2, _y=yb_i: e.tensor_scalar(
                                 ybuf[:, :], ybraw[_y][:, :], st_[:, 12 + _t:13 + _t],
                                 None, ALU.mult),
                             reads=[ybraw_r[yb_i], str2], writes=[ybuf_r], tag="ffy1")
                        p.op("dve", lambda e, _t=tc2: e.scalar_tensor_tensor(
                                 ybuf[:, :], nc1b[:, :], st_[:, 14 + _t:15 + _t],
                                 ybuf[:, :], ALU.mult, ALU.add),
                             reads=[nc1b_r, str2, ybuf_r], writes=[ybuf_r], tag="ffy2")
                        p.op("dve", lambda e: e.tensor_add(ybuf[:, :], ybuf[:, :],
                                                           c2b2b[:, :]),
                             reads=[ybuf_r, c2b2b_r], writes=[ybuf_r], tag="ffy3")
                    pieces.append(pc_y1)

                    def pc_y2(ys=ys, ysr=ysr):
                        p.op("act", lambda e: e.activation(ybuf[:, :], ybuf[:, :], AF.Relu),
                             reads=[ybuf_r], writes=[ybuf_r], tag="ffyrelu")
                        p.op("dve", lambda e, _s=ys: e.reduce_sum(_s[:, 0:1], ybuf[:, :],
                                                                  axis=AX.X),
                             reads=[ybuf_r], writes=[ysr], tag="ln2sum")
                        p.op("act", lambda e, _s=ys: e.activation(ysq[:, :], ybuf[:, :],
                                                                  AF.Square,
                                                                  accum_out=_s[:, 1:2]),
                             reads=[ybuf_r], writes=[ysr], tag="ln2sq")
                    pieces.append(pc_y2)

                    def pc_y3(ys=ys, ysr=ysr):
                        p.op("dve", lambda e, _s=ys: e.tensor_scalar(_s[:, 2:3], _s[:, 0:1],
                                                                     1.0 / D, None, ALU.mult),
                             reads=[ysr], writes=[ysr], tag="ln2mu")
                        p.op("dve", lambda e, _s=ys: e.tensor_tensor(_s[:, 3:4], _s[:, 2:3],
                                                                     _s[:, 2:3], ALU.mult),
                             reads=[ysr], writes=[ysr], tag="ln2musq")
                        p.op("dve", lambda e, _s=ys: e.tensor_scalar(_s[:, 4:5], _s[:, 1:2],
                                                                     1.0 / D, _s[:, 3:4],
                                                                     ALU.mult, ALU.subtract),
                             reads=[ysr], writes=[ysr], tag="ln2var")
                    pieces.append(pc_y3)

                    def pc_y4(ys=ys, ysr=ysr, tci=tci, e_idx=e_idx):
                        p.op("act", lambda e, _s=ys: e.activation(_s[:, 5:6], _s[:, 4:5],
                                                                  AF.Ln, bias=epsap[:, 0:1]),
                             reads=[ysr, epsr], writes=[ysr], tag="ln2ln")
                        p.op("act", lambda e, _s=ys: e.activation(_s[:, 6:7], _s[:, 5:6],
                                                                  AF.Exp,
                                                                  scale=p.consts["nhalf"][:, 0:1]),
                             reads=[ysr, p.consts_res["nhalf"]], writes=[ysr], tag="ln2exp")
                        p.op("dve", lambda e, _s=ys, _tc=tci, _e=e_idx: e.tensor_tensor(
                                 _s[:, 7:8], _s[:, 6:7], gates[:, _tc, _e:_e + 1], ALU.mult),
                             reads=[ysr, gates_r], writes=[ysr], tag="ln2A")
                        p.op("dve", lambda e, _s=ys: e.scalar_tensor_tensor(
                                 _s[:, 8:9], _s[:, 2:3], -1.0, _s[:, 7:8], ALU.mult, ALU.mult),
                             reads=[ysr], writes=[ysr], tag="ln2B")
                    pieces.append(pc_y4)

                    for dh in range(2):
                        def pc_y5(ys=ys, ysr=ysr, tci=tci, dh=dh):
                            p.op("dve", lambda e, _d=dh, _s=ys: e.tensor_scalar(
                                     ybuf[:, _d * 512:(_d + 1) * 512],
                                     ybuf[:, _d * 512:(_d + 1) * 512],
                                     _s[:, 7:8], _s[:, 8:9], ALU.mult, ALU.add),
                                 reads=[ybuf_r, ysr], writes=[ybuf_r], tag="ffg1")
                            p.op("pool", lambda e, _d=dh: e.tensor_tensor(
                                     ybuf[:, _d * 512:(_d + 1) * 512],
                                     ybuf[:, _d * 512:(_d + 1) * 512],
                                     g2b[:, _d * 512:(_d + 1) * 512], ALU.mult),
                                 reads=[ybuf_r, g2b_r], writes=[ybuf_r], tag="ffg2")
                            p.dma("pool",
                                  x_out[tci * 128:(tci + 1) * 128, dh * 512:(dh + 1) * 512],
                                  ybuf[:, dh * 512:(dh + 1) * 512],
                                  reads=[ybuf_r], tag="ffacc", accum=True)
                        pieces.append(pc_y5)
                return pieces

            pending = []
            for e_idx in range(n_experts):
                if e_idx > 0:
                    load_weights(e_idx)
                    for f in pending:
                        f()
                    pending = []
                prep_expert(e_idx)
                for tg in range(n_tg if n_tg is not None else NTG):
                    t0 = tg * TG
                    pend = None
                    pi = 0
                    for fb in range(NFB):
                        hbi = fb % 2
                        mm_group(p, ph[hbi][:, :], ph_r[hbi],
                                 [(w1[:, k, fb * 128:(fb + 1) * 128],
                                   xnT[:, k, t0:t0 + TG], [w1_r, xnT_res])
                                  for k in range(NKC)], tag="ffs1")
                        p.op("act", lambda e, _h=hbi, _f=fb: e.activation(
                                 rt[_h][:, 0:TG], ph[_h][:, :], AF.Relu,
                                 bias=b1c[:, _f:_f + 1]),
                             reads=[ph_r[hbi], b1c_r], writes=[rt_r[hbi]], tag="ffrelu")
                        p.op("dve", lambda e, _h=hbi: e.tensor_tensor(
                                 rt[_h][:, TG:2 * TG], rt[_h][:, 0:TG], rt[_h][:, 0:TG],
                                 ALU.mult),
                             reads=[rt_r[hbi]], writes=[rt_r[hbi]], tag="ffsq")
                        if pend is not None:
                            emit_s2_l0(pend)
                        pend = fb
                        if fb % 2 == 1 and pi < len(pending):
                            pending[pi]()
                            pi += 1
                    while pi < len(pending):
                        pending[pi]()
                        pi += 1
                    emit_s2_l0(pend)
                    par = tg % 2
                    for tc2 in range(2):
                        for dh in range(2):
                            iY = tc2 * 2 + dh
                            p.op("dve", lambda e, _i=iY, _y=par * 2 + tc2, _d=dh:
                                 e.tensor_copy(
                                     ybraw[_y][:, _d * 512:(_d + 1) * 512], pY[_i][:, :]),
                                 reads=[pY_r[iY]], writes=[ybraw_r[par * 2 + tc2]],
                                 tag="ffraw")
                    p.op("dve", lambda e, _p=par: e.tensor_copy(srow_f[_p][:, :],
                                                                pstat[:, :]),
                         reads=[pstat_r], writes=[srow_f_r[par]], tag="stf32")
                    pending = make_tail(tg, e_idx, par)
            for f in pending:
                f()
            pending = []
            p.wait_all_dma()
            p.flush()




def emit_wprep(p, pairs):
    ress = []
    for (src_ap, dst_ap, n_e) in pairs:
        r = Res(p, "wm")
        for e in range(n_e):
            p.dma("pool", dst_ap[e], src_ap[e], writes=[r], tag="wcast")
        ress.append(r)
    return ress



def emit_attn_l0(p, ctx, nc, aps, x_in, x_out, o_scr, uid="a0", n_packs=8,
                 wprep_cb=None):
    xnT = alloc(ctx, nc, f"{uid}_xnT", [128, NKC, T], BF16)
    xnT_res = Res(p, "xnT")
    ident = p.consts["identbf"]
    identr = p.consts_res["identbf"]

    with ExitStack() as c1:
        emit_ln_xnT(p, c1, nc, x_in, xnT, xnT_res, uid=uid, tc0=0, ntc=NT)
        p.wait_all_dma()
        p.flush()

    with ExitStack() as c2:
        wd, wd_r = {}, {}
        for s, nm in (("q", "Wq_down"), ("k", "Wk_down"), ("v", "Wv_down")):
            wd[s], wd_r[s] = rtile(p, c2, nc, f"{uid}_w{s}d", [128, NKC, D], BF16)
            p.dma("pool", wd[s][:, :, :], aps[nm].rearrange("(k p) d -> p k d", p=128),
                  writes=[wd_r[s]], tag=f"wd{s}")
        uq, uq_r = rtile(p, c2, nc, f"{uid}_uq", [128, 64], BF16)
        uk, uk_r = rtile(p, c2, nc, f"{uid}_uk", [128, 64], BF16)
        uqr, uqr_r = rtile(p, c2, nc, f"{uid}_uqr", [128, 32], BF16)
        ukr, ukr_r = rtile(p, c2, nc, f"{uid}_ukr", [128, 32], BF16)
        uv, uv_r = rtile(p, c2, nc, f"{uid}_uv", [128, 64], BF16)
        p.dma("pool", uq[0:64, 0:32], aps["Wq_up_c"], writes=[uq_r])
        p.dma("pool", uq[0:64, 32:64], aps["Wq_up_e"], writes=[uq_r])
        p.dma("pool", uk[0:64, 0:32], aps["Wk_up_c"], writes=[uk_r])
        p.dma("pool", uk[0:64, 32:64], aps["Wk_up_e"], writes=[uk_r])
        p.dma("pool", uv[0:64, :], aps["Wv_up"], writes=[uv_r])
        p.op("dve", lambda e: e.tensor_scalar(uq[0:64, :], uq[0:64, :], 0.125, None, ALU.mult),
             reads=[uq_r], writes=[uq_r], tag="uqscale")
        p.op("dve", lambda e: e.tensor_scalar(uqr[0:64, 0:32:2], uq[0:64, 33:64:2], -1.0,
                                              None, ALU.mult),
             reads=[uq_r], writes=[uqr_r], tag="uqrA")
        p.op("dve", lambda e: e.tensor_copy(uqr[0:64, 1:32:2], uq[0:64, 32:63:2]),
             reads=[uq_r], writes=[uqr_r], tag="uqrB")
        p.op("dve", lambda e: e.tensor_scalar(ukr[0:64, 0:32:2], uk[0:64, 33:64:2], -1.0,
                                              None, ALU.mult),
             reads=[uk_r], writes=[ukr_r], tag="ukrA")
        p.op("dve", lambda e: e.tensor_copy(ukr[0:64, 1:32:2], uk[0:64, 32:63:2]),
             reads=[uk_r], writes=[ukr_r], tag="ukrB")
        for (buf, bufr, w) in ((uq, uq_r, 64), (uk, uk_r, 64), (uqr, uqr_r, 32),
                               (ukr, ukr_r, 32), (uv, uv_r, 64)):
            p.op("dve", lambda e, _b=buf, _w=w: e.tensor_copy(_b[64:128, 0:_w],
                                                              _b[0:64, 0:_w]),
                 reads=[bufr], writes=[bufr], tag="udup")
        ctab, ctab_r = rtile(p, c2, nc, f"{uid}_ctab", [32, T], BF16)
        stab, stab_r = rtile(p, c2, nc, f"{uid}_stab", [32, T], BF16)
        p.dma("pool", ctab[:, :], p.host_in["c_cos0"], writes=[ctab_r])
        p.dma("pool", stab[:, :], p.host_in["c_sin0"], writes=[stab_r])
        kmask, kmask_r = rtile(p, c2, nc, f"{uid}_kmask", [128, 4, 512], BF16)
        p.dma("pool", kmask[:, :, :], p.host_in["c_kmask"], writes=[kmask_r])

        lat, lat_r = {}, {}
        for s in ("q", "k", "v"):
            lat[s], lat_r[s] = rtile(p, c2, nc, f"{uid}_lat{s}", [128, T], BF16)

        QT0, QT0_r = rtile(p, c2, nc, f"{uid}_QT", [64, T], BF16)
        KT0, KT0_r = rtile(p, c2, nc, f"{uid}_KT", [64, T], BF16)
        Vt0, Vt0_r = rtile(p, c2, nc, f"{uid}_Vt", [128, 32 * 65], BF16)
        QT, QT_r = [QT0, QT0], [QT0_r, QT0_r]
        KT, KT_r = [KT0, KT0], [KT0_r, KT0_r]
        Vt, Vt_r = [Vt0, Vt0], [Vt0_r, Vt0_r]
        pbig, pbig_r = rtile(p, c2, nc, f"{uid}_pbig", [128, 512], F32, psum=True)
        pup, pup_r = pbig, pbig_r
        pub, pub_r = None, None
        re1, re1_r = rtile(p, c2, nc, f"{uid}_re1", [32, 512], F32)
        re2, re2_r = rtile(p, c2, nc, f"{uid}_re2", [32, 512], F32)
        pV, pV_r = pbig, pbig_r
        pS, pS_r = rtile(p, c2, nc, f"{uid}_pS", [128, 512], F32, psum=True, n=3)
        Pt, Pt_r = rtile(p, c2, nc, f"{uid}_Pt", [128, 512], BF16, n=3)
        pO0, pO0_r = rtile(p, c2, nc, f"{uid}_pO0", [128, 512], F32, psum=True)
        pOx, pOx_r = rtile(p, c2, nc, f"{uid}_pO", [128, 65], F32, psum=True, n=3)
        pO = [pO0] + pOx
        pO_r = [pO0_r] + pOx_r
        rcp, rcp_r = rtile(p, c2, nc, f"{uid}_rcp", [128, 4], F32)
        obuf, obuf_r = rtile(p, c2, nc, f"{uid}_ob", [128, 4, 64], BF16, n=2)

        pub, pub_r = pO[0], pO0_r
        for hp in range(n_packs):
            for s in ("q", "k", "v"):
                for tg in range(8):
                    pb = tg % 2
                    mm_group(p, pS[pb][:, :], pS_r[pb],
                             [(wd[s][:, k, hp * 128:(hp + 1) * 128],
                               xnT[:, k, tg * 512:(tg + 1) * 512], [wd_r[s], xnT_res])
                              for k in range(NKC)], tag="lat")
                    p.op("dve", lambda e, _s=s, _t=tg, _p=pb: e.tensor_copy(
                             lat[_s][:, _t * 512:(_t + 1) * 512], pS[_p][:, :]),
                         reads=[pS_r[pb]], writes=[lat_r[s]], tag="latev")
            for h2 in range(2):
                h = hp * 2 + h2
                hb = h % 2
                lo = h2 * 64
                for (U, Ur, Uresm, Urres, dst, dstr, sname) in (
                        (uq, uqr, uq_r, uqr_r, QT[hb], QT_r[hb], "q"),
                        (uk, ukr, uk_r, ukr_r, KT[hb], KT_r[hb], "k")):
                    for tg in range(8):
                        t5 = slice(tg * 512, (tg + 1) * 512)
                        mm_group(p, pup[0:64, :], pup_r,
                                 [(U[lo:lo + 64, :], lat[sname][lo:lo + 64, t5],
                                   [Uresm, lat_r[sname]])], tag="up")
                        mm_group(p, pub[0:32, 0:512], pub_r,
                                 [(Ur[lo:lo + 64, :], lat[sname][lo:lo + 64, t5],
                                   [Urres, lat_r[sname]])], tag="upr")
                        p.op("dve", lambda e, _t=t5, _d=dst: e.tensor_copy(
                                 _d[0:32, _t], pup[0:32, 0:512]),
                             reads=[pup_r], writes=[dstr], tag="upc")
                        p.op("dve", lambda e, _t=t5: e.tensor_tensor(
                                 re1[:, :], pup[32:64, 0:512], ctab[:, _t], ALU.mult),
                             reads=[pup_r, ctab_r], writes=[re1_r], tag="ropeC")
                        p.op("dve", lambda e, _t=t5: e.tensor_tensor(
                                 re2[:, :], pub[0:32, 0:512], stab[:, _t], ALU.mult),
                             reads=[pub_r, stab_r], writes=[re2_r], tag="ropeS")
                        p.op("dve", lambda e, _t=t5, _d=dst: e.tensor_add(
                                 _d[32:64, _t], re1[:, :], re2[:, :]),
                             reads=[re1_r, re2_r], writes=[dstr], tag="ropeA")
                p.op("dve", lambda e, _b=hb: e.memset(Vt[_b][:, 64:32 * 65:65], 1.0),
                     writes=[Vt_r[hb]], tag="vones")
                for tc in range(NT):
                    mm_group(p, pV[:, 0:64], pV_r,
                             [(lat["v"][lo:lo + 64, tc * 128:(tc + 1) * 128],
                               uv[lo:lo + 64, :], [lat_r["v"], uv_r])], tag="vup")
                    p.op("dve", lambda e, _tc=tc, _b=hb: e.tensor_copy(
                             Vt[_b][:, _tc * 65:_tc * 65 + 64], pV[:, 0:64]),
                         reads=[pV_r], writes=[Vt_r[hb]], tag="vev")
                for qg in range(8):
                    nkt = 4 * (qg + 1)
                    q5 = slice(qg * 512, (qg + 1) * 512)

                    def emit_av(kt, _nkt=nkt, _hb=hb):
                        sb_ = kt % 3
                        for qs in range(4):
                            o = p.op("pe", lambda e, _s=sb_, _q=qs, _kt=kt, _b=_hb:
                                     e.matmul(pO[_q][:, 0:65],
                                              Pt[_s][:, _q * 128:(_q + 1) * 128],
                                              Vt[_b][:, _kt * 65:(_kt + 1) * 65],
                                              start=(_kt == 0), stop=(_kt == _nkt - 1)),
                                     reads=[Pt_r[sb_], Vt_r[_hb]],
                                     writes=[pO_r[qs]] if kt == 0 else [], tag="av")
                            if kt == _nkt - 1:
                                pO_r[qs].set_writer(o)

                    pend = None
                    for kt in range(nkt):
                        sb_ = kt % 3
                        mm_group(p, pS[sb_][:, :], pS_r[sb_],
                                 [(KT[hb][:, kt * 128:(kt + 1) * 128], QT[hb][:, q5],
                                   [KT_r[hb], QT_r[hb]])], tag="sT")
                        if kt >= qg * 4:
                            v = kt - qg * 4
                            p.op("dve", lambda e, _s=sb_, _v=v: e.tensor_tensor(
                                     pS[_s][:, :], pS[_s][:, :], kmask[:, _v, :], ALU.add),
                                 reads=[pS_r[sb_], kmask_r], writes=[pS_r[sb_]], tag="mask")
                        p.op("act", lambda e, _s=sb_: e.activation(Pt[_s][:, :], pS[_s][:, :],
                                                                   AF.Exp),
                             reads=[pS_r[sb_]], writes=[Pt_r[sb_]], tag="exp")
                        if pend is not None:
                            emit_av(pend)
                        pend = kt
                    emit_av(pend)
                    ob = qg % 2
                    for qs in range(4):
                        p.op("dve", lambda e, _q=qs: e.reciprocal(rcp[:, _q:_q + 1],
                                                                  pO[_q][:, 64:65]),
                             reads=[pO_r[qs]], writes=[rcp_r], tag="orcp")
                        p.op("dve", lambda e, _q=qs, _o=ob: e.tensor_scalar(
                                 obuf[_o][:, _q, :], pO[_q][:, 0:64], rcp[:, _q:_q + 1],
                                 None, ALU.mult),
                             reads=[pO_r[qs], rcp_r], writes=[obuf_r[ob]], tag="oev")
                    p.dma("sp",
                          o_scr[qg * 512:(qg + 1) * 512, h * 64:(h + 1) * 64]
                          .rearrange("(qs p) d -> p qs d", p=128),
                          obuf[ob][:, :, :], reads=[obuf_r[ob]], tag="ost")
        if wprep_cb is not None:
            wprep_cb()
        p.wait_all_dma()
        p.flush()

    with ExitStack() as c3:
        wc, wc_r = rtile(p, c3, nc, f"{uid}_wc", [128, NKC, D], BF16)
        p.dma("pool", wc[:, :, :], aps["Wc"].rearrange("(k p) d -> p k d", p=128),
              writes=[wc_r], tag="wcld")
        otl, otl_r = rtile(p, c3, nc, f"{uid}_otl", [128, D], BF16, n=2)
        oT, oT_r = rtile(p, c3, nc, f"{uid}_oT", [128, NKC, 128], BF16, n=2)
        ptr, ptres = rtile(p, c3, nc, f"{uid}_ptr", [128, 128], BF16, psum=True, n=2)
        pX, pX_r = rtile(p, c3, nc, f"{uid}_pX", [128, 512], F32, psum=True, n=2)
        xtl, xtl_r = rtile(p, c3, nc, f"{uid}_xtl3", [128, D], F32, n=2)
        xo, xo_r = rtile(p, c3, nc, f"{uid}_xo", [128, D], F32, n=2)
        for tc in range(NT):
            b = tc % 2
            ts = slice(tc * 128, (tc + 1) * 128)
            p.dma("sp", otl[b][:, :], o_scr[ts, :], writes=[otl_r[b]], tag="old")
            p.dma("sp", xtl[b][:, :], x_in[ts, :], writes=[xtl_r[b]], tag="xld")
            for k in range(NKC):
                pb = k % 2
                p.op("pe", lambda e, _b=b, _k=k, _p=pb: e.transpose(
                         ptr[_p][:, :], otl[_b][:, _k * 128:(_k + 1) * 128], ident[:, :]),
                     reads=[otl_r[b], identr], writes=[ptres[pb]], tag="otr")
                p.op("dve", lambda e, _b=b, _k=k, _p=pb: e.tensor_copy(
                         oT[_b][:, _k, :], ptr[_p][:, :]),
                     reads=[ptres[pb]], writes=[oT_r[b]], tag="otev")
            for dh in range(2):
                pxb = dh
                mm_group(p, pX[pxb][:, :], pX_r[pxb],
                         [(oT[b][:, k, :], wc[:, k, dh * 512:(dh + 1) * 512],
                           [oT_r[b], wc_r]) for k in range(NKC)], tag="wcmm")
                p.op("dve", lambda e, _b=b, _d=dh, _p=pxb: e.tensor_add(
                         xo[_b][:, _d * 512:(_d + 1) * 512],
                         xtl[_b][:, _d * 512:(_d + 1) * 512], pX[_p][:, :]),
                     reads=[pX_r[pxb], xtl_r[b]], writes=[xo_r[b]], tag="resadd")
            p.dma("pool", x_out[ts, :], xo[b][:, :], reads=[xo_r[b]], tag="xst")
        p.wait_all_dma()
        p.flush()


def l0_rope_tables():
    inv = 1.0 / (10000.0 ** (np.arange(0, 32, 2, dtype=np.float64) / 32.0))
    ang = np.arange(T, dtype=np.float64)[None, :] * inv[:, None]
    ang2 = np.concatenate([ang, ang], axis=0)
    idx = np.arange(32) % 16
    angj = np.arange(T, dtype=np.float64)[None, :] * inv[idx][:, None]
    return (np.cos(angj).astype(np.float32), np.sin(angj).astype(np.float32))


def causal_masks():
    k = np.arange(128)[:, None, None]
    v = np.arange(4)[None, :, None]
    q = np.arange(512)[None, None, :]
    km = np.where(k <= q - 128 * v, 0.0, -30000.0).astype(np.float32)
    qq = np.arange(128)[:, None, None]
    kq = np.arange(512)[None, None, :]
    qm = np.where(kq <= qq + 128 * v, 0.0, -30000.0).astype(np.float32)
    return km, qm


def build_attn_l0_program(n_packs=8):
    nc = bass.Bass("TRN2", target_bir_lowering=False)
    hin = declare_host_inputs(nc, {
        "x": [T, D], "Wq_down": [D, D], "Wk_down": [D, D], "Wv_down": [D, D],
        "Wq_up_c": [64, 32], "Wq_up_e": [64, 32], "Wk_up_c": [64, 32],
        "Wk_up_e": [64, 32], "Wv_up": [64, 64], "Wc": [D, D],
        "c_ident": [128, 128], "c_identf": [128, 128], "c_eps": [128, 1], "c_nhalf": [128, 1], "c_phalf": [128, 1],
        "c_cos0": [32, T], "c_sin0": [32, T], "c_kmask": [128, 4, 512],
    })
    out = nc.dram_tensor("out", [T, D], F32, kind="ExternalOutput").ap()
    o_scr = nc.dram_tensor("o_scr", [T, D], BF16, kind="Internal").ap()
    with ExitStack() as ctx:
        p = Prog(nc)
        ctx.enter_context(p._stack)
        p.host_in = hin
        load_consts(p, ctx, nc)
        emit_attn_l0(p, ctx, nc, hin, hin["x"], out, o_scr, n_packs=n_packs)
        p.flush()
    return nc





def emit_attn_l1(p, ctx, nc, aps, x_in, x_out, o_scr, uid="a1", n_heads=NH):
    xnT = alloc(ctx, nc, f"{uid}_xnT", [128, NKC, T], BF16)
    xnT_res = Res(p, "xnT")
    ident = p.consts["identbf"]
    identr = p.consts_res["identbf"]
    identf = p.consts["identf"]
    identfr = p.consts_res["identf"]
    SC = 1.0 / np.sqrt(96.0)

    with ExitStack() as c1:
        emit_ln_xnT(p, c1, nc, x_in, xnT, xnT_res, uid=uid, tc0=0, ntc=NT)
        p.wait_all_dma()
        p.flush()

    with ExitStack() as c2:
        wkvd, wkvd_r = rtile(p, c2, nc, f"{uid}_wkvd", [128, NKC, 256], BF16)
        p.dma("pool", wkvd[:, :, :], aps["Wkvd"].rearrange("(k p) c -> p k c", p=128),
              writes=[wkvd_r], tag="wkvd")
        wku, wku_r = rtile(p, c2, nc, f"{uid}_wku", [128, 2, D], BF16)
        p.dma("pool", wku[:, :, :], aps["Wku"].rearrange("(m p) c -> p m c", p=128),
              writes=[wku_r], tag="wku")
        wvu, wvu_r = rtile(p, c2, nc, f"{uid}_wvu", [128, 2, D], BF16)
        p.dma("pool", wvu[:, :, :], aps["Wvu"].rearrange("(m p) c -> p m c", p=128),
              writes=[wvu_r], tag="wvu")
        wkra, wkra_r = rtile(p, c2, nc, f"{uid}_wkra", [128, NKC, 16], BF16)
        wkrb, wkrb_r = rtile(p, c2, nc, f"{uid}_wkrb", [128, NKC, 16], BF16)
        p.dma("pool", wkra[:, :, :],
              bass.AP(aps["Wkr"].tensor, 0, [[32, 128], [32 * 128, 8], [1, 16]]),
              writes=[wkra_r], tag="wkra")
        p.dma("pool", wkrb[:, :, :],
              bass.AP(aps["Wkr"].tensor, 16, [[32, 128], [32 * 128, 8], [1, 16]]),
              writes=[wkrb_r], tag="wkrb")
        c2tab, c2tab_r = rtile(p, c2, nc, f"{uid}_c2t", [64, T], BF16)
        s2tab, s2tab_r = rtile(p, c2, nc, f"{uid}_s2t", [64, T], BF16)
        p.dma("pool", c2tab[:, :], p.host_in["c_cos1"], writes=[c2tab_r])
        p.dma("pool", s2tab[:, :], p.host_in["c_sin1"], writes=[s2tab_r])
        kmask, kmask_r = rtile(p, c2, nc, f"{uid}_km", [128, 4, 512], BF16)
        p.dma("pool", kmask[:, :, :], p.host_in["c_kmask"], writes=[kmask_r])
        qmask, qmask_r = rtile(p, c2, nc, f"{uid}_qm", [128, 4, 512], BF16)
        p.dma("pool", qmask[:, :, :], p.host_in["c_qmask"], writes=[qmask_r])
        lob, lob_r = rtile(p, c2, nc, f"{uid}_lob", [1, 16], F32)
        p.dma("sp", lob[:, :], aps["lobo"].rearrange("(o h) -> o h", o=1), writes=[lob_r])
        lobe, lobe_r = rtile(p, c2, nc, f"{uid}_lobe", [1, 16], BF16)
        p.op("act", lambda e: e.activation(lobe[:, :], lob[:, :], AF.Exp),
             reads=[lob_r], writes=[lobe_r], tag="lobexp")
        onescol, onescol_r = rtile(p, c2, nc, f"{uid}_oc", [1, 128], BF16)
        p.op("dve", lambda e: e.memset(onescol[:, :], 1.0), writes=[onescol_r])
        Cb, Cb_r = rtile(p, c2, nc, f"{uid}_Cb", [128, 16], F32)

        latT, latT_r = rtile(p, c2, nc, f"{uid}_latT", [128, 2, T], BF16)
        kr1, kr1_r = rtile(p, c2, nc, f"{uid}_kr1", [16, T], BF16)
        kr2, kr2_r = rtile(p, c2, nc, f"{uid}_kr2", [16, T], BF16)
        rk1, rk1_r = rtile(p, c2, nc, f"{uid}_rk1", [16, 512], F32)
        rk2, rk2_r = rtile(p, c2, nc, f"{uid}_rk2", [16, 512], F32)

        with ExitStack() as c2a:
            pP, pP_r = rtile(p, c2a, nc, f"{uid}_pP", [128, 512], F32, psum=True, n=2)
            pK1, pK1_r = rtile(p, c2a, nc, f"{uid}_pK1", [16, 512], F32, psum=True)
            pK2, pK2_r = rtile(p, c2a, nc, f"{uid}_pK2", [16, 512], F32, psum=True)
            pCb, pCb_r = rtile(p, c2a, nc, f"{uid}_pCb", [128, 16], F32, psum=True)
            mm_group(p, pCb[:, :], pCb_r, [(onescol[:, :], lobe[:, :],
                                            [onescol_r, lobe_r])], tag="cbmm")
            p.op("dve", lambda e: e.tensor_copy(Cb[:, :], pCb[:, :]),
                 reads=[pCb_r], writes=[Cb_r], tag="cbev")
            for mc in range(2):
                for tg in range(8):
                    pb = tg % 2
                    mm_group(p, pP[pb][:, :], pP_r[pb],
                             [(wkvd[:, k, mc * 128:(mc + 1) * 128],
                               xnT[:, k, tg * 512:(tg + 1) * 512], [wkvd_r, xnT_res])
                              for k in range(NKC)], tag="latmm")
                    p.op("dve", lambda e, _m=mc, _t=tg, _p=pb: e.tensor_copy(
                             latT[:, _m, _t * 512:(_t + 1) * 512], pP[_p][:, :]),
                         reads=[pP_r[pb]], writes=[latT_r], tag="latev")
            for tg in range(8):
                t5 = slice(tg * 512, (tg + 1) * 512)
                mm_group(p, pK1[:, :], pK1_r,
                         [(wkra[:, k, :], xnT[:, k, t5], [wkra_r, xnT_res])
                          for k in range(NKC)], tag="krmm1")
                mm_group(p, pK2[:, :], pK2_r,
                         [(wkrb[:, k, :], xnT[:, k, t5], [wkrb_r, xnT_res])
                          for k in range(NKC)], tag="krmm2")
                p.op("dve", lambda e, _t=t5: e.tensor_tensor(rk1[:, :], pK1[:, :],
                                                             c2tab[0:16, _t], ALU.mult),
                     reads=[pK1_r, c2tab_r], writes=[rk1_r], tag="krc")
                p.op("dve", lambda e, _t=t5: e.tensor_tensor(rk2[:, :], pK2[:, :],
                                                             s2tab[0:16, _t], ALU.mult),
                     reads=[pK2_r, s2tab_r], writes=[rk2_r], tag="krs")
                p.op("dve", lambda e, _t=t5: e.tensor_sub(kr1[:, _t], rk1[:, :], rk2[:, :]),
                     reads=[rk1_r, rk2_r], writes=[kr1_r], tag="kr1f")
                p.op("dve", lambda e, _t=t5: e.tensor_tensor(rk1[:, :], pK2[:, :],
                                                             c2tab[0:16, _t], ALU.mult),
                     reads=[pK2_r, c2tab_r], writes=[rk1_r], tag="krc2")
                p.op("dve", lambda e, _t=t5: e.tensor_tensor(rk2[:, :], pK1[:, :],
                                                             s2tab[0:16, _t], ALU.mult),
                     reads=[pK1_r, s2tab_r], writes=[rk2_r], tag="krs2")
                p.op("dve", lambda e, _t=t5: e.tensor_add(kr2[:, _t], rk1[:, :], rk2[:, :]),
                     reads=[rk1_r, rk2_r], writes=[kr2_r], tag="kr2f")
            p.wait_all_dma()
            p.flush()

        wqp, wqp_r = rtile(p, c2, nc, f"{uid}_wqp", [128, NKC, 256], BF16)
        wqrp, wqrp_r = rtile(p, c2, nc, f"{uid}_wqrp", [128, NKC, 128], BF16)
        wqrm, wqrm_r = rtile(p, c2, nc, f"{uid}_wqrm", [128, NKC, 128], BF16)
        qr1, qr1_r = rtile(p, c2, nc, f"{uid}_qr1", [64, T], BF16)
        qr2, qr2_r = rtile(p, c2, nc, f"{uid}_qr2", [64, T], BF16)
        rq1, rq1_r = rtile(p, c2, nc, f"{uid}_rq1", [64, 512], F32)
        rq2, rq2_r = rtile(p, c2, nc, f"{uid}_rq2", [64, 512], F32)
        QT, QT_r = rtile(p, c2, nc, f"{uid}_QT", [97, T], BF16)
        KT, KT_r = rtile(p, c2, nc, f"{uid}_KT", [97, T], BF16)
        Vt, Vt_r = rtile(p, c2, nc, f"{uid}_Vt", [128, 32 * 65], BF16)
        m_all, m_all_r = rtile(p, c2, nc, f"{uid}_mall", [128, 32], F32)
        mrow, mrow_r = rtile(p, c2, nc, f"{uid}_mrow", [32, 128], BF16)
        mtmp, mtmp_r = rtile(p, c2, nc, f"{uid}_mtmp", [128, 2], F32)
        rcp, rcp_r = rtile(p, c2, nc, f"{uid}_rcp", [128, 4], F32)
        obuf, obuf_r = rtile(p, c2, nc, f"{uid}_ob", [128, 4, 64], BF16, n=2)
        with ExitStack() as c2b:
            pS, pS_r = rtile(p, c2b, nc, f"{uid}_pS", [128, 512], F32, psum=True, n=3)
            Pt, Pt_r = rtile(p, c2b, nc, f"{uid}_Pt", [128, 512], BF16, n=3)
            pO, pO_r = rtile(p, c2b, nc, f"{uid}_pO", [128, 65], F32, psum=True, n=4)
            pM, pM_r = rtile(p, c2b, nc, f"{uid}_pM", [32, 128], F32, psum=True)

            for hq in range(cdiv(n_heads, 4)):
                p.dma("pool", wqp[:, :, :],
                      bass.AP(aps["Wq"].tensor, hq * 256,
                              [[D, 128], [D * 128, NKC], [1, 256]]),
                      writes=[wqp_r], tag="wqp")
                p.op("dve", lambda e: e.tensor_scalar(
                         bass.AP(wqp, 0, [[NKC * 256, 128], [1, NKC * 256]]),
                         bass.AP(wqp, 0, [[NKC * 256, 128], [1, NKC * 256]]),
                         SC, None, ALU.mult),
                     reads=[wqp_r], writes=[wqp_r], tag="wqscale")
                p.dma("pool", wqrp[:, :, :],
                      bass.AP(aps["Wqr"].tensor, hq * 128,
                              [[512, 128], [512 * 128, NKC], [1, 128]]),
                      writes=[wqrp_r], tag="wqrp")
                for k in range(NKC):
                    p.op("dve", lambda e, _k=k: e.tensor_scalar(
                             wqrm[:, _k, :],
                             bass.AP(wqrp, _k * 128,
                                     [[NKC * 128, 128], [16, 2], [32, 4], [1, 16]]),
                             SC, None, ALU.mult),
                         reads=[wqrp_r], writes=[wqrm_r], tag="wqperm")
                for tg in range(8):
                    t5 = slice(tg * 512, (tg + 1) * 512)
                    pb = tg % 2
                    mm_group(p, pS[pb][:, :], pS_r[pb],
                             [(wqrm[:, k, :], xnT[:, k, t5], [wqrm_r, xnT_res])
                              for k in range(NKC)], tag="qrmm")
                    p.op("dve", lambda e, _t=t5, _p=pb: e.tensor_tensor(
                             rq1[:, :], pS[_p][0:64, :], c2tab[:, _t], ALU.mult),
                         reads=[pS_r[pb], c2tab_r], writes=[rq1_r], tag="qrc")
                    p.op("dve", lambda e, _t=t5, _p=pb: e.tensor_tensor(
                             rq2[:, :], pS[_p][64:128, :], s2tab[:, _t], ALU.mult),
                         reads=[pS_r[pb], s2tab_r], writes=[rq2_r], tag="qrs")
                    p.op("dve", lambda e, _t=t5: e.tensor_sub(qr1[:, _t], rq1[:, :], rq2[:, :]),
                         reads=[rq1_r, rq2_r], writes=[qr1_r], tag="qr1f")
                    p.op("dve", lambda e, _t=t5, _p=pb: e.tensor_tensor(
                             rq1[:, :], pS[_p][64:128, :], c2tab[:, _t], ALU.mult),
                         reads=[pS_r[pb], c2tab_r], writes=[rq1_r], tag="qrc2")
                    p.op("dve", lambda e, _t=t5, _p=pb: e.tensor_tensor(
                             rq2[:, :], pS[_p][0:64, :], s2tab[:, _t], ALU.mult),
                         reads=[pS_r[pb], s2tab_r], writes=[rq2_r], tag="qrs2")
                    p.op("dve", lambda e, _t=t5: e.tensor_add(qr2[:, _t], rq1[:, :], rq2[:, :]),
                         reads=[rq1_r, rq2_r], writes=[qr2_r], tag="qr2f")
                for h4 in range(min(4, n_heads - hq * 4)):
                    h = hq * 4 + h4
                    for tg in range(8):
                        t5 = slice(tg * 512, (tg + 1) * 512)
                        pb = tg % 2
                        mm_group(p, pS[pb][0:64, :], pS_r[pb],
                                 [(wqp[:, k, h4 * 64:(h4 + 1) * 64], xnT[:, k, t5],
                                   [wqp_r, xnT_res]) for k in range(NKC)], tag="qcmm")
                        p.op("dve", lambda e, _t=t5, _p=pb: e.tensor_copy(
                                 QT[0:64, _t], pS[_p][0:64, 0:512]),
                             reads=[pS_r[pb]], writes=[QT_r], tag="qcev")
                    p.dma("pool", QT[64:80, :], qr1[h4 * 16:(h4 + 1) * 16, :],
                          reads=[qr1_r], writes=[QT_r], tag="qrcp1")
                    p.dma("pool", QT[80:96, :], qr2[h4 * 16:(h4 + 1) * 16, :],
                          reads=[qr2_r], writes=[QT_r], tag="qrcp2")
                    for tg in range(8):
                        t5 = slice(tg * 512, (tg + 1) * 512)
                        pb = tg % 2
                        mm_group(p, pS[pb][0:64, :], pS_r[pb],
                                 [(wku[:, mc, h * 64:(h + 1) * 64], latT[:, mc, t5],
                                   [wku_r, latT_r]) for mc in range(2)], tag="kcmm")
                        p.op("dve", lambda e, _t=t5, _p=pb: e.tensor_copy(
                                 KT[0:64, _t], pS[_p][0:64, 0:512]),
                             reads=[pS_r[pb]], writes=[KT_r], tag="kcev")
                    p.dma("pool", KT[64:80, :], kr1[:, :], reads=[kr1_r], writes=[KT_r],
                          tag="krcp1")
                    p.dma("pool", KT[80:96, :], kr2[:, :], reads=[kr2_r], writes=[KT_r],
                          tag="krcp2")
                    p.dma("pool", KT[96:97, :], p.host_in["c_ones"], writes=[KT_r],
                          tag="onescp")
                    p.op("dve", lambda e: e.memset(Vt[:, 64:32 * 65:65], 1.0),
                         writes=[Vt_r], tag="vones")
                    for tc in range(NT):
                        mm_group(p, pS[0][:, 0:64], pS_r[0],
                                 [(latT[:, mc, tc * 128:(tc + 1) * 128],
                                   wvu[:, mc, h * 64:(h + 1) * 64],
                                   [latT_r, wvu_r]) for mc in range(2)], tag="vmm")
                        p.op("dve", lambda e, _tc=tc: e.tensor_copy(
                                 Vt[:, _tc * 65:_tc * 65 + 64], pS[0][:, 0:64]),
                             reads=[pS_r[0]], writes=[Vt_r], tag="vev")
                    for qb in range(NT):
                        nk4 = qb // 4 + 1
                        for k4 in range(nk4):
                            N = min(512, (qb + 1) * 128 - k4 * 512)
                            sb_ = k4 % 3
                            mm_group(p, pS[sb_][:, 0:N], pS_r[sb_],
                                     [(QT[0:96, qb * 128:(qb + 1) * 128],
                                       KT[0:96, k4 * 512:k4 * 512 + N],
                                       [QT_r, KT_r])], tag="s1")
                            if k4 == nk4 - 1:
                                o = qb - 4 * k4
                                p.op("dve", lambda e, _s=sb_, _o=o, _N=N: e.tensor_tensor(
                                         pS[_s][:, 0:_N], pS[_s][:, 0:_N],
                                         qmask[:, _o, 0:_N], ALU.add),
                                     reads=[pS_r[sb_], qmask_r], writes=[pS_r[sb_]],
                                     tag="qmask")
                            p.op("dve", lambda e, _s=sb_, _N=N, _c=(k4 % 2): e.reduce_max(
                                     mtmp[:, _c:_c + 1], pS[_s][:, 0:_N], axis=AX.X),
                                 reads=[pS_r[sb_]], writes=[mtmp_r], tag="p1max")
                            if k4 == 0:
                                p.op("dve", lambda e, _qb=qb: e.tensor_copy(
                                         m_all[:, _qb:_qb + 1], mtmp[:, 0:1]),
                                     reads=[mtmp_r], writes=[m_all_r], tag="p1cp")
                            else:
                                p.op("dve", lambda e, _qb=qb, _c=(k4 % 2): e.tensor_max(
                                         m_all[:, _qb:_qb + 1], m_all[:, _qb:_qb + 1],
                                         mtmp[:, _c:_c + 1]),
                                     reads=[mtmp_r, m_all_r], writes=[m_all_r], tag="p1mx")
                    p.op("pe", lambda e: e.transpose(pM[:, :], m_all[:, :],
                                                     identf[0:128, 0:128]),
                         reads=[m_all_r, identfr], writes=[pM_r], tag="mtr")
                    p.op("dve", lambda e: e.tensor_scalar(mrow[:, :], pM[:, :], -1.0,
                                                          None, ALU.mult),
                         reads=[pM_r], writes=[mrow_r], tag="mneg")
                    p.dma("pool", QT[96:97, :], mrow[:, :], reads=[mrow_r], writes=[QT_r],
                          tag="mcp")
                    for qg in range(8):
                        nkt = 4 * (qg + 1)
                        q5 = slice(qg * 512, (qg + 1) * 512)

                        def emit_av(kt, _nkt=nkt):
                            sb_ = kt % 3
                            for qs in range(4):
                                o = p.op("pe", lambda e, _s=sb_, _q=qs, _kt=kt:
                                         e.matmul(pO[_q][:, :],
                                                  Pt[_s][:, _q * 128:(_q + 1) * 128],
                                                  Vt[:, _kt * 65:(_kt + 1) * 65],
                                                  start=(_kt == 0), stop=(_kt == _nkt - 1)),
                                         reads=[Pt_r[sb_], Vt_r],
                                         writes=[pO_r[qs]] if kt == 0 else [], tag="av")
                                if kt == _nkt - 1:
                                    pO_r[qs].set_writer(o)

                        pend = None
                        for kt in range(nkt):
                            sb_ = kt % 3
                            mm_group(p, pS[sb_][:, :], pS_r[sb_],
                                     [(KT[:, kt * 128:(kt + 1) * 128], QT[:, q5],
                                       [KT_r, QT_r])], tag="sT")
                            if kt >= qg * 4:
                                v = kt - qg * 4
                                p.op("dve", lambda e, _s=sb_, _v=v: e.tensor_tensor(
                                         pS[_s][:, :], pS[_s][:, :], kmask[:, _v, :],
                                         ALU.add),
                                     reads=[pS_r[sb_], kmask_r], writes=[pS_r[sb_]],
                                     tag="mask")
                            p.op("act", lambda e, _s=sb_: e.activation(
                                     Pt[_s][:, :], pS[_s][:, :], AF.Exp),
                                 reads=[pS_r[sb_]], writes=[Pt_r[sb_]], tag="exp")
                            if pend is not None:
                                emit_av(pend)
                            pend = kt
                        emit_av(pend)
                        ob = qg % 2
                        for qs in range(4):
                            p.op("dve", lambda e, _q=qs, _h=h: e.tensor_scalar(
                                     rcp[:, _q:_q + 1], pO[_q][:, 64:65],
                                     Cb[:, _h:_h + 1], None, ALU.add),
                                 reads=[pO_r[qs], Cb_r], writes=[rcp_r], tag="odenC")
                            p.op("dve", lambda e, _q=qs: e.reciprocal(
                                     rcp[:, _q:_q + 1], rcp[:, _q:_q + 1]),
                                 reads=[rcp_r], writes=[rcp_r], tag="orcp")
                            p.op("dve", lambda e, _q=qs, _o=ob: e.tensor_scalar(
                                     obuf[_o][:, _q, :], pO[_q][:, 0:64],
                                     rcp[:, _q:_q + 1], None, ALU.mult),
                                 reads=[pO_r[qs], rcp_r], writes=[obuf_r[ob]], tag="oev")
                        p.dma("sp",
                              o_scr[qg * 512:(qg + 1) * 512, h * 64:(h + 1) * 64]
                              .rearrange("(qs p) d -> p qs d", p=128),
                              obuf[ob][:, :, :], reads=[obuf_r[ob]], tag="ost")
            p.wait_all_dma()
            p.flush()

    with ExitStack() as c3:
        wo, wo_r = rtile(p, c3, nc, f"{uid}_wo", [128, NKC, D], BF16)
        p.dma("pool", wo[:, :, :], aps["Wo"].rearrange("(k p) d -> p k d", p=128),
              writes=[wo_r], tag="wold")
        otl, otl_r = rtile(p, c3, nc, f"{uid}_otl", [128, D], BF16, n=2)
        oT, oT_r = rtile(p, c3, nc, f"{uid}_oT", [128, NKC, 128], BF16, n=2)
        ptr, ptres = rtile(p, c3, nc, f"{uid}_ptr", [128, 128], BF16, psum=True, n=2)
        pX, pX_r = rtile(p, c3, nc, f"{uid}_pX", [128, 512], F32, psum=True, n=2)
        xtl, xtl_r = rtile(p, c3, nc, f"{uid}_xtl3", [128, D], F32, n=2)
        xo, xo_r = rtile(p, c3, nc, f"{uid}_xo", [128, D], F32, n=2)
        for tc in range(NT):
            b = tc % 2
            ts = slice(tc * 128, (tc + 1) * 128)
            p.dma("sp", otl[b][:, :], o_scr[ts, :], writes=[otl_r[b]], tag="old")
            p.dma("sp", xtl[b][:, :], x_in[ts, :], writes=[xtl_r[b]], tag="xld")
            for k in range(NKC):
                pb = k % 2
                p.op("pe", lambda e, _b=b, _k=k, _p=pb: e.transpose(
                         ptr[_p][:, :], otl[_b][:, _k * 128:(_k + 1) * 128], ident[:, :]),
                     reads=[otl_r[b], identr], writes=[ptres[pb]], tag="otr")
                p.op("dve", lambda e, _b=b, _k=k, _p=pb: e.tensor_copy(
                         oT[_b][:, _k, :], ptr[_p][:, :]),
                     reads=[ptres[pb]], writes=[oT_r[b]], tag="otev")
            for dh in range(2):
                mm_group(p, pX[dh][:, :], pX_r[dh],
                         [(oT[b][:, k, :], wo[:, k, dh * 512:(dh + 1) * 512],
                           [oT_r[b], wo_r]) for k in range(NKC)], tag="womm")
                p.op("dve", lambda e, _b=b, _d=dh: e.tensor_add(
                         xo[_b][:, _d * 512:(_d + 1) * 512],
                         xtl[_b][:, _d * 512:(_d + 1) * 512], pX[_d][:, :]),
                     reads=[pX_r[dh], xtl_r[b]], writes=[xo_r[b]], tag="resadd")
            p.dma("pool", x_out[ts, :], xo[b][:, :], reads=[xo_r[b]], tag="xst")
        p.wait_all_dma()
        p.flush()


def l1_rope_tables():
    inv = 1.0 / (10000.0 ** (np.arange(16, dtype=np.float64) / 16.0))
    ang = inv[:, None] * np.arange(T, dtype=np.float64)[None, :]
    c = np.cos(ang).astype(np.float32)
    s = np.sin(ang).astype(np.float32)
    return np.tile(c, (4, 1)), np.tile(s, (4, 1))


def build_attn_l1_program(n_heads=NH):
    nc = bass.Bass("TRN2", target_bir_lowering=False)
    hin = declare_host_inputs(nc, {
        "x": [T, D], "Wq": [D, D], "Wqr": [D, 512], "Wkr": [D, 32],
        "Wkvd": [D, 256], "Wku": [256, D], "Wvu": [256, D], "Wo": [D, D],
        "lobo": [16],
        "c_ident": [128, 128], "c_identf": [128, 128], "c_eps": [128, 1], "c_nhalf": [128, 1], "c_phalf": [128, 1],
        "c_cos1": [64, T], "c_sin1": [64, T], "c_kmask": [128, 4, 512],
        "c_qmask": [128, 4, 512], "c_ones": [1, T],
    })
    out = nc.dram_tensor("out", [T, D], F32, kind="ExternalOutput").ap()
    o_scr = nc.dram_tensor("o_scr", [T, D], BF16, kind="Internal").ap()
    with ExitStack() as ctx:
        p = Prog(nc)
        ctx.enter_context(p._stack)
        p.host_in = hin
        load_consts(p, ctx, nc)
        emit_attn_l1(p, ctx, nc, hin, hin["x"], out, o_scr, n_heads=n_heads)
        p.flush()
    return nc




def host_consts():
    cos0, sin0 = l0_rope_tables()
    km, qm = causal_masks()
    return {
        "c_ident": np.eye(128, dtype=np.float32),
        "c_identf": np.eye(128, dtype=np.float32),
        "c_eps": np.full((128, 1), 1e-5, dtype=np.float32),
        "c_nhalf": np.full((128, 1), -0.5, dtype=np.float32),
        "c_phalf": np.full((128, 1), 0.5, dtype=np.float32),
        "c_cos0": cos0, "c_sin0": sin0, "c_kmask": km, "c_qmask": qm,
        "c_cos1": l1_rope_tables()[0], "c_sin1": l1_rope_tables()[1],
        "c_ones": np.ones((1, T), dtype=np.float32),
    }


def load_consts(p, ctx, nc):
    identbf = alloc(ctx, nc, "identbf", [128, 128], BF16)
    p.consts["identbf"] = identbf
    p.consts_res["identbf"] = Res(p, "identbf")
    p.dma("pool", identbf[:, :], p.host_in["c_ident"], writes=[p.consts_res["identbf"]])
    identf = alloc(ctx, nc, "identf", [128, 128], F32)
    p.consts["identf"] = identf
    p.consts_res["identf"] = Res(p, "identf")
    p.dma("sp", identf[:, :], p.host_in["c_identf"], writes=[p.consts_res["identf"]])
    p.dump = alloc(ctx, nc, "dumpbuf", [128, D], F32)
    epst = alloc(ctx, nc, "epst", [128, 1], F32)
    p.consts["eps"] = epst
    p.consts_res["eps"] = Res(p, "eps")
    p.dma("sp", epst[:, :], p.host_in["c_eps"], writes=[p.consts_res["eps"]])
    for nm in ("nhalf", "phalf"):
        tt = alloc(ctx, nc, nm, [128, 1], F32)
        p.consts[nm] = tt
        p.consts_res[nm] = Res(p, nm)
        p.dma("sp", tt[:, :], p.host_in["c_" + nm], writes=[p.consts_res[nm]])


def declare_host_inputs(nc, specs):
    aps = {}
    for name, shape in specs.items():
        aps[name] = nc.dram_tensor(name, list(shape), F32, kind="ExternalInput").ap()
    return aps


def build_ffn_l0_program(n_experts=NE, n_tg=None):
    nc = bass.Bass("TRN2", target_bir_lowering=False)
    hin = declare_host_inputs(nc, {
        "x": [T, D], "Wr": [D, 8], "br": [8],
        "W1": [NE, D, FF], "b1": [NE, FF], "g1": [NE, FF], "be1": [NE, FF],
        "W2": [NE, FF, D], "b2": [NE, D], "g2": [NE, D], "be2": [NE, D],
        "c_ident": [128, 128], "c_identf": [128, 128], "c_eps": [128, 1], "c_nhalf": [128, 1], "c_phalf": [128, 1],
    })
    out = nc.dram_tensor("out", [T, D], F32, kind="ExternalOutput").ap()
    w1m = nc.dram_tensor("w1m", [NE, D, FF], BF16, kind="Internal").ap()
    w2m = nc.dram_tensor("w2m", [NE, FF, D], BF16, kind="Internal").ap()
    with ExitStack() as ctx:
        p = Prog(nc)
        ctx.enter_context(p._stack)
        p.host_in = hin
        load_consts(p, ctx, nc)
        emit_wprep(p, [(hin["W1"], w1m, NE), (hin["W2"], w2m, NE)])
        aps = dict(hin)
        aps["w1m"], aps["w2m"] = w1m, w2m
        emit_ffn_l0(p, ctx, nc, aps, hin["x"], out, n_experts=n_experts, n_tg=n_tg)
        p.flush()
    return nc


def build_ffn_l1_program(n_experts=NE, n_tg=None):
    nc = bass.Bass("TRN2", target_bir_lowering=False)
    hin = declare_host_inputs(nc, {
        "x": [T, D], "wg_red": [16, D], "wg": [8, 16],
        "w1": [NE, D, FF], "b1": [NE, FF], "w2": [NE, FF, D], "b2": [NE, D],
        "c_ident": [128, 128], "c_identf": [128, 128], "c_eps": [128, 1], "c_nhalf": [128, 1], "c_phalf": [128, 1],
    })
    out = nc.dram_tensor("out", [T, D], F32, kind="ExternalOutput").ap()
    w1m = nc.dram_tensor("w1m", [NE, D, FF], BF16, kind="Internal").ap()
    w2m = nc.dram_tensor("w2m", [NE, FF, D], BF16, kind="Internal").ap()
    with ExitStack() as ctx:
        p = Prog(nc)
        ctx.enter_context(p._stack)
        p.host_in = hin
        load_consts(p, ctx, nc)
        emit_wprep(p, [(hin["w1"], w1m, NE), (hin["w2"], w2m, NE)])
        aps = dict(hin)
        aps["w1m"], aps["w2m"] = w1m, w2m
        emit_ffn_l1(p, ctx, nc, aps, hin["x"], out, n_experts=n_experts, n_tg=n_tg)
        p.flush()
    return nc


def build_fused_program():
    nc = bass.Bass("TRN2", target_bir_lowering=False)
    hin = declare_host_inputs(nc, {
        "x": [T, D],
        "a0_Wq_down": [D, D], "a0_Wk_down": [D, D], "a0_Wv_down": [D, D],
        "a0_Wq_up_c": [64, 32], "a0_Wq_up_e": [64, 32], "a0_Wk_up_c": [64, 32],
        "a0_Wk_up_e": [64, 32], "a0_Wv_up": [64, 64], "a0_Wc": [D, D],
        "f0_Wr": [D, 8], "f0_br": [8], "f0_W1": [NE, D, FF], "f0_b1": [NE, FF],
        "f0_g1": [NE, FF], "f0_be1": [NE, FF], "f0_W2": [NE, FF, D],
        "f0_b2": [NE, D], "f0_g2": [NE, D], "f0_be2": [NE, D],
        "a1_Wq": [D, D], "a1_Wqr": [D, 512], "a1_Wkr": [D, 32],
        "a1_Wkvd": [D, 256], "a1_Wku": [256, D], "a1_Wvu": [256, D],
        "a1_Wo": [D, D], "a1_lobo": [16],
        "f1_wg_red": [16, D], "f1_wg": [8, 16], "f1_w1": [NE, D, FF],
        "f1_b1": [NE, FF], "f1_w2": [NE, FF, D], "f1_b2": [NE, D],
        "c_ident": [128, 128], "c_identf": [128, 128], "c_eps": [128, 1], "c_nhalf": [128, 1], "c_phalf": [128, 1],
        "c_cos0": [32, T], "c_sin0": [32, T], "c_kmask": [128, 4, 512],
        "c_qmask": [128, 4, 512], "c_cos1": [64, T], "c_sin1": [64, T],
        "c_ones": [1, T],
    })
    out = nc.dram_tensor("out", [T, D], F32, kind="ExternalOutput").ap()
    o_scr = nc.dram_tensor("o_scr", [T, D], BF16, kind="Internal").ap()
    x1 = nc.dram_tensor("x1s", [T, D], F32, kind="Internal").ap()
    x2 = nc.dram_tensor("x2s", [T, D], F32, kind="Internal").ap()
    x3 = nc.dram_tensor("x3s", [T, D], F32, kind="Internal").ap()
    f0w1m = nc.dram_tensor("f0w1m", [NE, D, FF], BF16, kind="Internal").ap()
    f0w2m = nc.dram_tensor("f0w2m", [NE, FF, D], BF16, kind="Internal").ap()
    f1w1m = nc.dram_tensor("f1w1m", [NE, D, FF], BF16, kind="Internal").ap()
    f1w2m = nc.dram_tensor("f1w2m", [NE, FF, D], BF16, kind="Internal").ap()

    def sub(d, pre):
        return {k[len(pre):]: v for k, v in d.items() if k.startswith(pre)}

    with ExitStack() as ctx:
        p = Prog(nc)
        ctx.enter_context(p._stack)
        p.host_in = hin
        load_consts(p, ctx, nc)

        def _wprep():
            emit_wprep(p, [(hin["f0_W1"], f0w1m, NE), (hin["f0_W2"], f0w2m, NE),
                           (hin["f1_w1"], f1w1m, NE), (hin["f1_w2"], f1w2m, NE)])
        a1aps = sub(hin, "a1_")
        a1aps["lobo"] = hin["a1_lobo"]
        f0aps = sub(hin, "f0_")
        f0aps["w1m"], f0aps["w2m"] = f0w1m, f0w2m
        f1aps = sub(hin, "f1_")
        f1aps["w1m"], f1aps["w2m"] = f1w1m, f1w2m
        with ExitStack() as c:
            emit_attn_l0(p, c, nc, sub(hin, "a0_"), hin["x"], x1, o_scr,
                         wprep_cb=_wprep)
        with ExitStack() as c:
            emit_ffn_l0(p, c, nc, f0aps, x1, x2)
        with ExitStack() as c:
            emit_attn_l1(p, c, nc, a1aps, x2, x3, o_scr)
        with ExitStack() as c:
            emit_ffn_l1(p, c, nc, f1aps, x3, out)
        p.flush()
    return nc


_PROGS = {}


def _get_programs():
    if "p" not in _PROGS:
        _PROGS["p"] = (build_attn_l0_program(), build_ffn_l0_program(),
                       build_attn_l1_program(), build_ffn_l1_program())
    return _PROGS["p"]


def _f32(a):
    return np.ascontiguousarray(a, dtype=np.float32)


def _get_fused():
    if "f" not in _PROGS:
        _PROGS["f"] = build_fused_program()
    return _PROGS["f"]


def kernel(**inputs):
    import os
    n = 8
    consts = host_consts()
    if not os.environ.get("KERNEL_UNFUSED"):
        nc = _get_fused()
        wmap = {}
        for pre, src_pre, keys in (
            ("a0_", "L0_attn_", ("Wq_down", "Wk_down", "Wv_down", "Wq_up_c",
                                 "Wq_up_e", "Wk_up_c", "Wk_up_e", "Wv_up", "Wc")),
            ("f0_", "L0_ffn_", ("Wr", "br", "W1", "b1", "g1", "be1", "W2",
                                "b2", "g2", "be2")),
            ("f1_", "L1_ffn_", ("wg_red", "wg", "w1", "b1", "w2", "b2")),
        ):
            for k in keys:
                wmap[pre + k] = _f32(inputs[src_pre + k])
        for k, k2 in (("Wq", "Wq"), ("Wqr", "Wqr"), ("Wkr", "Wkr"),
                      ("Wkvd", "Wkvd"), ("Wku", "Wku"), ("Wvu", "Wvu"),
                      ("Wo", "Wo"), ("lobo", "lobo_log")):
            wmap["a1_" + k] = _f32(inputs["L1_attn_" + k2])
        wmap.update(consts)
        in_maps = []
        for c in range(n):
            m = {"x": _f32(inputs["L0_attn_x"][c])}
            m.update(wmap)
            in_maps.append(m)
        res = run_bass_kernel_spmd(nc, in_maps, core_ids=list(range(n)))
        return np.stack([res.results[c]["out"] for c in range(n)], axis=0)

    nc_a0, nc_f0, nc_a1, nc_f1 = _get_programs()

    def launch(nc, per_core_x, wmap):
        in_maps = []
        for c in range(n):
            m = {"x": per_core_x[c]}
            m.update(wmap)
            m.update(consts)
            in_maps.append(m)
        res = run_bass_kernel_spmd(nc, in_maps, core_ids=list(range(n)))
        return [res.results[c]["out"] for c in range(n)]

    x = [_f32(inputs["L0_attn_x"][c]) for c in range(n)]
    w_a0 = {k: _f32(inputs["L0_attn_" + k]) for k in
            ("Wq_down", "Wk_down", "Wv_down", "Wq_up_c", "Wq_up_e",
             "Wk_up_c", "Wk_up_e", "Wv_up", "Wc")}
    x = launch(nc_a0, x, w_a0)
    w_f0 = {k: _f32(inputs["L0_ffn_" + k]) for k in
            ("Wr", "br", "W1", "b1", "g1", "be1", "W2", "b2", "g2", "be2")}
    x = launch(nc_f0, x, w_f0)
    w_a1 = {k: _f32(inputs["L1_attn_" + k2]) for k, k2 in
            (("Wq", "Wq"), ("Wqr", "Wqr"), ("Wkr", "Wkr"), ("Wkvd", "Wkvd"),
             ("Wku", "Wku"), ("Wvu", "Wvu"), ("Wo", "Wo"), ("lobo", "lobo_log"))}
    x = launch(nc_a1, x, w_a1)
    w_f1 = {k: _f32(inputs["L1_ffn_" + k]) for k in
            ("wg_red", "wg", "w1", "b1", "w2", "b2")}
    x = launch(nc_f1, x, w_f1)
    return np.stack(x, axis=0)
